# version 10
# baseline (speedup 1.0000x reference)
import math
from contextlib import ExitStack

import numpy as np

import concourse.bass as bass
import concourse.mybir as mybir
from concourse.bass_utils import run_bass_kernel_spmd

F32 = mybir.dt.float32
BF16 = mybir.dt.bfloat16
I32 = mybir.dt.int32
AF = mybir.ActivationFunctionType
ALU = mybir.AluOpType
AX = mybir.AxisListType.X

BATCH, SEQ, D = 2, 4096, 2048
NCORES = 8
TOK = 1024
NT = TOK // 128
DC = D // 128
HA, HDA = 32, 64
KCH = SEQ // 128
HB, DKB, DVB, MB = 16, 128, 128, 64
E = 8192
CLAMP = 32.0
EPS_LN = 1e-5
EPS_RMS = 1e-5
EPS_FFN = 1e-8
SHIFT_A = 32.0
SHIFT_B = 32.0
ROPE_BASE = 10000.0
MEXT_LEN = 8704
MS_LEN = 5376
AG2C = 3 * D + HB * MB

DEBUG_TAPS = []
P2_HG = 16
P2_KC = KCH
DEBUG_DUMP = []
STOP_AFTER = None

DMA_SEM = {'sync': 'sd', 'gpsimd': 'gd', 'vector': 'vd', 'scalar': 'ad'}
CMP_SEM = {'tensor': 't', 'vector': 'v', 'scalar': 'a', 'gpsimd': 'g',
           'sync': 'sy'}


class Ctx:
    def __init__(self):
        self.nc = bass.Bass("TRN2", target_bir_lowering=False)
        self.es = ExitStack()
        self.sem = {}
        self.cnt = {}
        self.lw = {}
        self.block = None

    def bump(self, key, amt=1):
        self.cnt[key] += amt
        return self.cnt[key]

    def wait(self, eng, ename, key, val):
        if val <= 0:
            return
        if self.lw.get((ename, key), -1) >= val:
            return
        eng.wait_ge(self.sem[key], val)
        self.lw[(ename, key)] = val

    def sbuf(self, name, shape, dt=F32, stack=None):
        return (stack or self.es).enter_context(
            self.nc.sbuf_tensor(name, shape, dt))


class Chain:

    def __init__(self, c, dsem=None):
        self.c = c
        self.last = None
        self.pending = []
        self.dsem = dsem or {}

    def grp(self, ename, fn, dma=False, collective=False, extra=()):
        c = self.c
        key = ('cc' if collective else
               (self.dsem.get(ename, DMA_SEM[ename]) if dma
                else CMP_SEM[ename]))
        waits = [w for w in extra if w is not None]
        if self.pending:
            waits.extend(w for w in self.pending if w is not None)
            self.pending = []
        if self.last is not None:
            waits.append(self.last)
        holder = {}

        def body(eng):
            for (k, v) in waits:
                c.wait(eng, ename, k, v)
            instrs = fn(eng)
            assert instrs, f"empty group on {ename}"
            if dma:
                for ins in instrs:
                    ins.then_inc(c.sem[key], 16)
                holder['n'] = 16 * len(instrs)
            else:
                instrs[-1].then_inc(c.sem[key], 1)
                holder['n'] = 1
        getattr(c.block, ename)(body)
        self.last = (key, c.bump(key, holder['n']))
        return self.last


def drive(gens):
    live = list(gens)
    while live:
        nxt = []
        for g in live:
            try:
                next(g)
                nxt.append(g)
            except StopIteration:
                pass
        live = nxt


def ts(i, n=128):
    return slice(i * n, (i + 1) * n)


def rawap(t, off, ap):
    return bass.AP(tensor=t.tensor, offset=off, ap=ap)


def build_program():
    c = Ctx()
    nc = c.nc

    def din(name, shape, dt=F32):
        return nc.dram_tensor(name, shape, dt, kind="ExternalInput").ap()

    def dint(name, shape, dt=F32):
        return nc.dram_tensor(name, shape, dt, kind="Internal").ap()

    IN = {'xin': din("xin", [TOK, D])}
    for nm in ("wqT_a", "wkT_a", "wvT_a", "woT_a",
               "WqT_b", "WkT_b", "WvT_b", "WgT_b", "WoT_b"):
        IN[nm] = din(nm, [D, D])
    IN['rel_ms'] = din("rel_ms", [MS_LEN, 64])
    IN['Ws1T'] = din("Ws1T", [D, 16])
    IN['Ws2T'] = din("Ws2T", [16, 1024])
    IN['bs2'] = din("bs2", [1, 1024])
    IN['gnw'] = din("gnw", [1, 128])
    IN['wupT_s'] = din("wupT_s", [D // 8, E])
    IN['wgateT_s'] = din("wgateT_s", [D // 8, E])
    IN['wdownT_s'] = din("wdownT_s", [E // 8, D])
    IN['meta_f'] = din("meta_f", [1, 8])
    IN['gate_own'] = din("gate_own", [1, 32])

    xout = nc.dram_tensor("xout", [TOK, D], F32, kind="ExternalOutput").ap()
    taps = {t: nc.dram_tensor("tap_" + t, [TOK, D], F32,
                              kind="ExternalOutput").ap() for t in DEBUG_TAPS}

    msl = dint("msl", [1, MS_LEN])
    ar_in = dint("ar_in", [1, 8]); ar_out = dint("ar_out", [1, 8])
    agt_in = dint("agt_in", [768, E], BF16)
    agt_out = dint("agt_out", [8 * 768, E], BF16)
    ag1k_in = dint("ag1k_in", [2 * D, TOK], BF16)
    ag1k_out = dint("ag1k_out", [4 * 2 * D, TOK], BF16)
    ag1v_in = dint("ag1v_in", [TOK, D])
    ag1v_out = dint("ag1v_out", [4 * TOK, D])
    ag2_in = dint("ag2_in", [TOK, AG2C])
    ag2_out = dint("ag2_out", [4 * TOK, AG2C])
    cxh_dram = dint("cxh_dram", [D, TOK], BF16)
    cxl_dram = dint("cxl_dram", [D, TOK], BF16)
    rro_dram = [dint(f"rro{i}", [1, 512]) for i in range(2)]
    g_dram = dint("g_dram", [TOK, D])
    o_dram = dint("o_dram", [TOK, D])
    q_dram = dint("q_dram", [D, TOK])
    qs_dram = dint("qs_dram", [2 * D, TOK], BF16)
    G4 = [[0, 1, 2, 3], [4, 5, 6, 7]]
    G8 = [list(range(8))]

    with c.es:
        xres = c.sbuf("xres", [128, NT * D])
        hTh = c.sbuf("hTh", [128, DC * 512], BF16)
        hTl = c.sbuf("hTl", [128, DC * 512], BF16)
        ident = c.sbuf("ident", [128, 128])
        maskA = c.sbuf("maskA", [128, 128])
        ones_t = c.sbuf("ones_t", [128, 128])
        consts = c.sbuf("consts", [128, 24])
        (C_NEG_SA, C_EPS_LN, C_INV_D, C_SU, C_SG, C_SD, C_HALF, C_LNIF,
         C_EPS_RMS, C_INV_DV, C_NEG_SB, C_INV2PI, C_EPS_FFN, C_2PI_HI,
         C_2PI_LO, C_SCALEA, C_ONE, C_ZERO) = range(18)
        gnw_bc = c.sbuf("gnw_bc", [128, 128])
        bs2_sb = c.sbuf("bs2_sb", [128, 8])
        gate_sb = c.sbuf("gate_sb", [128, 32])
        q0_bc = c.sbuf("q0_bc", [128, 1])
        meta_sb = c.sbuf("meta_sb", [1, 8])
        scrA = c.sbuf("scrA", [128, 2048])
        stat = c.sbuf("stat", [128, 8])
        WB = {}
        _wb_id = [0]

        from contextlib import contextmanager

        @contextmanager
        def wblock():
            i = _wb_id[0]; _wb_id[0] += 1
            with nc.sbuf_tensor(f"wbuf{i}", [128, 2 * 2048], F32) as a, \
                 nc.sbuf_tensor(f"wbh{i}", [128, 2 * 2048], BF16) as b, \
                 nc.sbuf_tensor(f"wbl{i}", [128, 2 * 2048], BF16) as d:
                WB['wbuf'], WB['wbh'], WB['wbl'] = a, b, d
                yield
        stg = c.sbuf("stg", [128, 1024])
        stg2 = c.sbuf("stg2", [128, 1024])

        pb = [c.es.enter_context(nc.psum_tensor(f"pb{i}", [128, 512], F32))
              for i in range(8)]

        for s in ('sd', 'gd', 'vd', 'ad', 't', 'v', 'a', 'g', 'sy', 'cc',
                  'xs0', 'xs1', 'xs2', 'xs3', 'xg0', 'xg1', 'xg2', 'xg3'):
            c.sem[s] = c.es.enter_context(nc.semaphore(s))
            c.cnt[s] = 0
        c.block = c.es.enter_context(nc.Block())
        ch = Chain(c)

        def g_loads(sy):
            r = [sy.dma_start(out=meta_sb[:], in_=IN['meta_f']),
                 sy.dma_start(out=bs2_sb[:],
                              in_=rawap(IN['bs2'], 0, [[1, 128], [128, 8]]),
                              allow_slow_non_contiguous=True),
                 sy.dma_start(out=gnw_bc[:],
                              in_=rawap(IN['gnw'], 0, [[0, 128], [1, 128]])),
                 sy.dma_start(out=gate_sb[:],
                              in_=rawap(IN['gate_own'], 0, [[0, 128], [1, 32]])),
                 sy.dma_start(out=q0_bc[:],
                              in_=rawap(IN['meta_f'], 0, [[0, 128], [1, 1]]))]
            for tt in range(NT):
                r.append(sy.dma_start(out=xres[:, tt * D:(tt + 1) * D],
                                      in_=IN['xin'][ts(tt), :]))
            return r
        ch.grp('sync', g_loads, dma=True)

        def g_const(v):
            vals = {C_NEG_SA: -SHIFT_A, C_EPS_LN: EPS_LN, C_INV_D: 1.0 / D,
                    C_HALF: math.pi / 2, C_LNIF: -math.log(ROPE_BASE) / 64.0,
                    C_EPS_RMS: EPS_RMS, C_INV_DV: 1.0 / DVB,
                    C_NEG_SB: -SHIFT_B, C_INV2PI: 0.5 / math.pi,
                    C_EPS_FFN: EPS_FFN,
                    C_2PI_HI: float(np.float32(2 * math.pi)),
                    C_2PI_LO: float(2 * math.pi - float(np.float32(2 * math.pi))),
                    C_SCALEA: 1.0 / math.sqrt(HDA), C_ONE: 1.0, C_ZERO: 0.0}
            r = [v.memset(consts[:, col:col + 1], val) for col, val in vals.items()]
            r.append(v.memset(ones_t[:], 1.0))
            return r
        ch.grp('vector', g_const)

        def g_ident(gp):
            return [gp.affine_select(ident[:], ones_t[:, 0:128],
                                     pattern=[[-1, 128]], compare_op=ALU.is_equal,
                                     fill=0.0, base=0, channel_multiplier=1),
                    gp.affine_select(maskA[:], ones_t[:, 0:128],
                                     pattern=[[1, 128]], compare_op=ALU.is_ge,
                                     fill=0.0, base=0, channel_multiplier=-1)]
        ch.grp('gpsimd', g_ident)

        def transpose_block(src_ap, dst_ap, bank=7):
            p, f = src_ap.shape

            def g_t(te):
                return [te.transpose(pb[bank][0:f, 0:p], src_ap, ident[:])]
            ch.grp('tensor', g_t)

            def g_v(v):
                return [v.tensor_copy(dst_ap, pb[bank][0:f, 0:p])]
            ch.grp('vector', g_v)

        def split_evac(src_psum, hi_ap, lo_ap):
            def g1(v):
                return [v.tensor_copy(hi_ap, src_psum)]
            ch.grp('vector', g1)

            def g2(v):
                return [v.tensor_tensor(lo_ap, src_psum, hi_ap, ALU.subtract)]
            ch.grp('vector', g2)

        def emit_ln(th):
            for q in range(4):
                tt = th * 4 + q
                xt = xres[:, tt * D:(tt + 1) * D]

                def g1(v, xt=xt):
                    return [v.reduce_sum(stat[:, 0:1], xt, AX)]
                ch.grp('vector', g1)

                def g2(s, xt=xt):
                    return [s.activation(scrA[:], xt, AF.Square,
                                         bias=consts[:, C_ZERO:C_ZERO + 1],
                                         scale=1.0, accum_out=stat[:, 1:2])]
                ch.grp('scalar', g2)

                def g3(v):
                    return [v.tensor_scalar_mul(stat[:, 2:3], stat[:, 0:1], 1.0 / D),
                            v.tensor_tensor(stat[:, 4:5], stat[:, 0:1],
                                            stat[:, 0:1], ALU.mult)]
                ch.grp('vector', g3)

                def g3b(v):
                    return [v.scalar_tensor_tensor(stat[:, 5:6], stat[:, 4:5],
                                                   -1.0 / D, stat[:, 1:2],
                                                   ALU.mult, ALU.add)]
                ch.grp('vector', g3b)

                def g4(s):
                    return [s.activation(stat[:, 7:8], stat[:, 5:6], AF.Sqrt,
                                         bias=consts[:, C_EPS_LN:C_EPS_LN + 1],
                                         scale=consts[:, C_INV_D:C_INV_D + 1])]
                ch.grp('scalar', g4)

                def g4b(v):
                    return [v.reciprocal(stat[:, 6:7], stat[:, 7:8])]
                ch.grp('vector', g4b)

                def g5(v, xt=xt):
                    return [v.tensor_scalar(scrA[:], xt, stat[:, 2:3],
                                            stat[:, 6:7], ALU.subtract, ALU.mult)]
                ch.grp('vector', g5)

                for dc in range(DC):
                    def g_t(te, dc=dc):
                        return [te.transpose(pb[6][:, 0:128],
                                             scrA[:, dc * 128:(dc + 1) * 128],
                                             ident[:])]
                    ch.grp('tensor', g_t)

                    split_evac(pb[6][:, 0:128],
                               hTh[:, dc * 512 + q * 128: dc * 512 + (q + 1) * 128],
                               hTl[:, dc * 512 + q * 128: dc * 512 + (q + 1) * 128])

        def load_w_block(cc, WT, oc, slot, extra=(), rows=D):
            ndc = rows // 128
            so = slot * 2048

            def g(sy):
                r = []
                for q in range(0, ndc, 4):
                    n = min(4, ndc - q)
                    r.append(sy.dma_start(
                        out=WB['wbuf'][:, so + q * 128: so + (q + n) * 128],
                        in_=rawap(WT, q * 128 * (WT.shape[1]) + oc * 128,
                                  [[WT.shape[1], 128],
                                   [128 * WT.shape[1], n], [1, 128]])))
                return r
            cc.grp('sync', g, dma=True, extra=extra)

            def g2a(v):
                return [v.tensor_copy(WB['wbh'][:, so:so + ndc * 128],
                                      WB['wbuf'][:, so:so + ndc * 128])]
            cc.grp('vector', g2a)

            def g2b(v):
                return [v.tensor_tensor(WB['wbl'][:, so:so + ndc * 128],
                                        WB['wbuf'][:, so:so + ndc * 128],
                                        WB['wbh'][:, so:so + ndc * 128],
                                        ALU.subtract)]
            return cc.grp('vector', g2b)

        def proj3(WT, out_cb, th, ndc=DC):
            chL = Chain(c, dsem={'sync': 'xs3'})
            chM = Chain(c)
            chL.last = ch.last
            chM.last = ch.last
            HH = {}
            HH['ld', 0] = load_w_block(chL, WT, 0, 0, rows=ndc * 128)
            for oc in range(DC):
                bank = oc % 2
                if oc + 1 < DC:
                    HH['ld', oc + 1] = load_w_block(
                        chL, WT, oc + 1, (oc + 1) % 2,
                        extra=(HH.get(('mm', oc - 1)),), rows=ndc * 128)

                def g_mm(te, oc=oc, bank=bank):
                    so = (oc % 2) * 2048
                    r = []
                    n3 = 3 * ndc
                    i = 0
                    for dc in range(ndc):
                        wh = WB['wbh'][:, so + dc * 128: so + (dc + 1) * 128]
                        wl = WB['wbl'][:, so + dc * 128: so + (dc + 1) * 128]
                        xh = hTh[:, dc * 512: dc * 512 + 512]
                        xl = hTl[:, dc * 512: dc * 512 + 512]
                        for lhs, rhs in ((wh, xh), (wl, xh), (wh, xl)):
                            r.append(te.matmul(pb[bank][:, 0:512], lhs, rhs,
                                               start=(i == 0),
                                               stop=(i == n3 - 1)))
                            i += 1
                    return r
                HH['mm', oc] = chM.grp('tensor', g_mm,
                                       extra=(HH['ld', oc],
                                              HH.get(('cb', oc - 2))))
                ch.pending.append(HH['mm', oc])
                out_cb(oc, th, bank)
                HH['cb', oc] = ch.last

        def add_psum_T_to_xres(bank, oc, tg):
            def g_e(v):
                return [v.tensor_copy(stg[:, 0:512], pb[bank][:, 0:512])]
            ch.grp('vector', g_e)
            for q in range(4):
                def g_t(te, q=q):
                    return [te.transpose(pb[6][:, 0:128],
                                         stg[:, q * 128:(q + 1) * 128], ident[:])]
                ch.grp('tensor', g_t)

                def g_a(v, q=q, oc=oc, tg=tg):
                    tt = tg * 4 + q
                    xt = xres[:, tt * D + oc * 128: tt * D + (oc + 1) * 128]
                    return [v.tensor_tensor(stg2[:, 0:128], xt, pb[6][:, 0:128],
                                            ALU.add)]
                ch.grp('vector', g_a)

                def g_a2(v, q=q, oc=oc, tg=tg):
                    tt = tg * 4 + q
                    xt = xres[:, tt * D + oc * 128: tt * D + (oc + 1) * 128]
                    return [v.tensor_copy(xt, stg2[:, 0:128])]
                ch.grp('vector', g_a2)

        def tap_xres(name):
            if name is None or name not in taps:
                return

            def g(gp):
                return [gp.dma_start(out=taps[name][ts(tt), :],
                                     in_=xres[:, tt * D:(tt + 1) * D])
                        for tt in range(NT)]
            ch.grp('gpsimd', g, dma=True)

        internals = {'msl': msl, 'ag1k_out': ag1k_out, 'ag1v_out': ag1v_out,
                     'qs_dram': qs_dram,
                     'agt_out': agt_out, 'ag2_out': ag2_out,
                     'g_dram': g_dram, 'o_dram': o_dram}
        dumps = {}
        for nm in DEBUG_DUMP:
            t = internals[nm]
            dumps[nm] = nc.dram_tensor("dump_" + nm, list(t.shape), t.dtype,
                                       kind="ExternalOutput").ap()

        def emit_dumps():
            for nm, dst in dumps.items():
                def g_d(gp, nm=nm, dst=dst):
                    return [gp.dma_start(out=dst, in_=internals[nm])]
                ch.grp('gpsimd', g_d, dma=True)

        def stop_here(name):
            if STOP_AFTER != name:
                return False
            emit_dumps()

            def g_out(gp):
                return [gp.dma_start(out=xout[ts(tt), :],
                                     in_=xres[:, tt * D:(tt + 1) * D])
                        for tt in range(NT)]
            ch.grp('gpsimd', g_out, dma=True)

            def g_fin(gp):
                return [gp.engine_nop()]
            ch.grp('gpsimd', g_fin)
            return True

        if stop_here("x0"):
            print("cnt:", dict(c.cnt))
            return nc

        with nc.sbuf_tensor("tw", [128, 4096], F32) as tw, \
             nc.sbuf_tensor("tth", [128, 4096], F32) as tth, \
             nc.sbuf_tensor("ttb", [128, 4096], BF16) as ttb, \
             nc.sbuf_tensor("tacc", [128, 16], F32) as tacc, \
             nc.sbuf_tensor("scl", [128, 12], F32) as scl:
            specs = [(IN['wupT_s'], 2, E), (IN['wgateT_s'], 2, E),
                     (IN['wdownT_s'], 8, D)]
            blocks = []
            for wi, (wt, nb, rl) in enumerate(specs):
                for rb in range(nb):
                    for ci in range(0, rl, 4096):
                        cw = min(4096, rl - ci)
                        blocks.append((wi, wt, rb, ci, cw))

            for bi, (wi, wt, rb, ci, cw) in enumerate(blocks):
                def g_l(sy, wt=wt, rb=rb, ci=ci, cw=cw):
                    return [sy.dma_start(out=tw[:, 0:cw],
                                         in_=wt[ts(rb), ci:ci + cw])]
                ch.grp('sync', g_l, dma=True)

                def g_r(v, bi=bi, cw=cw):
                    return [v.tensor_reduce(tacc[:, bi:bi + 1], tw[:, 0:cw], AX,
                                            ALU.add, apply_absolute_value=True)]
                ch.grp('vector', g_r)

            def g_ws(v):
                return [v.reduce_sum(scl[:, 9 + 0:10 + 0], tacc[:, 0:4], AX),
                        v.reduce_sum(scl[:, 9 + 1:10 + 1], tacc[:, 4:8], AX),
                        v.reduce_sum(scl[:, 9 + 2:10 + 2], tacc[:, 8:16], AX)]
            ch.grp('vector', g_ws)

            def g_cp(te):
                return [te.matmul(pb[0][0:1, wi:wi + 1], scl[:, 9 + wi:10 + wi],
                                  ones_t[:, 0:1], start=(wi == 0), stop=(wi == 2),
                                  skip_group_check=True) for wi in range(3)]
            ch.grp('tensor', g_cp)

            def g_ev0(v):
                return [v.memset(scl[0:1, 0:8], 0.0)]
            ch.grp('vector', g_ev0)

            def g_ev(v):
                return [v.tensor_copy(scl[0:1, 0:3], pb[0][0:1, 0:3])]
            ch.grp('vector', g_ev)

            def g_ar_in(gp):
                return [gp.dma_start(out=ar_in, in_=scl[0:1, 0:8])]
            ch.grp('gpsimd', g_ar_in, dma=True)

            def g_ar(gp):
                return [gp.collective_compute("AllReduce", ALU.add,
                                              replica_groups=G8,
                                              ins=[ar_in], outs=[ar_out])]
            ch.grp('gpsimd', g_ar, collective=True)

            def g_bc(gp):
                return [gp.dma_start(out=scl[:, wi:wi + 1],
                                     in_=rawap(ar_out, wi, [[0, 128], [1, 1]]))
                        for wi in range(3)]
            ch.grp('gpsimd', g_bc, dma=True)

            def g_scalesA(v):
                return [v.tensor_scalar(consts[:, (C_SU, C_SG, C_SD)[wi]:
                                                (C_SU, C_SG, C_SD)[wi] + 1],
                                        scl[:, wi:wi + 1], 1.0 / (D * E),
                                        EPS_FFN, ALU.mult, ALU.add)
                        for wi in range(3)]
            ch.grp('vector', g_scalesA)

            def g_scalesB(v):
                return [v.tensor_scalar_mul(scl[:, 4 + wi:5 + wi],
                                            consts[:, (C_SU, C_SG, C_SD)[wi]:
                                                   (C_SU, C_SG, C_SD)[wi] + 1],
                                            0.5) for wi in range(3)]
            ch.grp('vector', g_scalesB)

            def g_scalesC(v):
                return [v.tensor_scalar_mul(scl[:, 8 + wi:9 + wi],
                                            scl[:, 4 + wi:5 + wi], -1.0)
                        for wi in range(3)]
            ch.grp('vector', g_scalesC)

            for (wi, wt, rb, ci, cw) in blocks:
                def g_l(sy, wt=wt, rb=rb, ci=ci, cw=cw):
                    return [sy.dma_start(out=tw[:, 0:cw],
                                         in_=wt[ts(rb), ci:ci + cw])]
                ch.grp('sync', g_l, dma=True)

                def g_t1(v, wi=wi, cw=cw):
                    nthr = scl[:, 8 + wi:9 + wi]
                    return [v.tensor_single_scalar(tth[:, 0:cw], tw[:, 0:cw],
                                                   nthr, ALU.is_lt)]
                ch.grp('vector', g_t1)

                def g_t2(v, wi=wi, cw=cw):
                    thr = scl[:, 4 + wi:5 + wi]
                    return [v.scalar_tensor_tensor(ttb[:, 0:cw], tw[:, 0:cw],
                                                   thr, tth[:, 0:cw],
                                                   ALU.is_gt, ALU.subtract)]
                ch.grp('vector', g_t2)

                def g_s(gp, wi=wi, rb=rb, ci=ci, cw=cw):
                    if wi < 2:
                        row0 = wi * 256 + rb * 128
                        return [gp.dma_start(out=agt_in[row0:row0 + 128,
                                                        ci:ci + cw],
                                             in_=ttb[:, 0:cw])]
                    row0 = 512 + rb * 32
                    return [gp.dma_start(
                        out=rawap(agt_in, row0 * E, [[2048, 128], [1, 2048]]),
                        in_=ttb[:, 0:cw])]
                ch.grp('gpsimd', g_s, dma=True)

            def g_agt(gp):
                return [gp.collective_compute("AllGather", ALU.bypass,
                                              replica_groups=G8,
                                              ins=[agt_in], outs=[agt_out])]
            ch.grp('gpsimd', g_agt, collective=True)

        if stop_here("tern"):
            print("cnt:", dict(c.cnt))
            return nc

        with nc.sbuf_tensor("rp", [128, 64], F32) as rp, \
             nc.sbuf_tensor("mcol", [128, 2], F32) as mcol:
            for i in range(MS_LEN // 128):
                def g_l(sy, i=i):
                    return [sy.dma_start(out=rp[:],
                                         in_=IN['rel_ms'][ts(i), :])]
                ch.grp('sync', g_l, dma=True)

                def g_r(v):
                    return [v.reduce_sum(mcol[:, 0:1], rp[:], AX)]
                ch.grp('vector', g_r)

                def g_r2(v):
                    return [v.tensor_scalar_mul(mcol[:, 1:2], mcol[:, 0:1],
                                                1.0 / 64.0)]
                ch.grp('vector', g_r2)

                def g_st(gp, i=i):
                    return [gp.dma_start(out=msl[0:1, i * 128:(i + 1) * 128],
                                         in_=mcol[:, 1:2])]
                ch.grp('gpsimd', g_st, dma=True)

        if stop_here("mext"):
            print("cnt:", dict(c.cnt))
            return nc

        def q_cb(oc, th, bank):
            split_evac(pb[bank][:, 0:512], stgh[:, 0:512], stgl[:, 0:512])

            def g(gp, oc=oc, th=th):
                return [gp.dma_start(out=qs_dram[oc * 128:(oc + 1) * 128,
                                                 th * 512:(th + 1) * 512],
                                     in_=stgh[:, 0:512]),
                        gp.dma_start(out=qs_dram[D + oc * 128: D + (oc + 1) * 128,
                                                 th * 512:(th + 1) * 512],
                                     in_=stgl[:, 0:512])]
            ch.grp('gpsimd', g, dma=True)

        def k_cb(oc, th, bank):
            split_evac(pb[bank][:, 0:512], stgh[:, 0:512], stgl[:, 0:512])

            def g(gp, oc=oc, th=th):
                return [gp.dma_start(out=ag1k_in[oc * 128:(oc + 1) * 128,
                                                 th * 512:(th + 1) * 512],
                                     in_=stgh[:, 0:512]),
                        gp.dma_start(out=ag1k_in[D + oc * 128: D + (oc + 1) * 128,
                                                 th * 512:(th + 1) * 512],
                                     in_=stgl[:, 0:512])]
            ch.grp('gpsimd', g, dma=True)

        def v_cb(oc, th, bank):
            def g_e(v):
                return [v.tensor_copy(stg[:, 0:512], pb[bank][:, 0:512])]
            ch.grp('vector', g_e)
            for q in range(4):
                transpose_block(stg[:, q * 128:(q + 1) * 128],
                                stg2[:, q * 128:(q + 1) * 128], bank=6)

            def g(gp, oc=oc, th=th):
                return [gp.dma_start(
                    out=ag1v_in[th * 512 + q * 128: th * 512 + (q + 1) * 128,
                                oc * 128:(oc + 1) * 128],
                    in_=stg2[:, q * 128:(q + 1) * 128]) for q in range(4)]
            ch.grp('gpsimd', g, dma=True)

        with wblock(), \
             nc.sbuf_tensor("stgh", [128, 1024], BF16) as stgh, \
             nc.sbuf_tensor("stgl", [128, 1024], BF16) as stgl:
            for th in range(2):
                emit_ln(th)
                proj3(IN['wqT_a'], q_cb, th)
                proj3(IN['wkT_a'], k_cb, th)
                proj3(IN['wvT_a'], v_cb, th)

        def g_ag1a(gp):
            return [gp.collective_compute("AllGather", ALU.bypass,
                                          replica_groups=G4,
                                          ins=[ag1k_in], outs=[ag1k_out])]
        ch.grp('gpsimd', g_ag1a, collective=True)

        def g_ag1b(gp):
            return [gp.collective_compute("AllGather", ALU.bypass,
                                          replica_groups=G4,
                                          ins=[ag1v_in], outs=[ag1v_out])]
        ch.grp('gpsimd', g_ag1b, collective=True)

        if stop_here("p1"):
            print("cnt:", dict(c.cnt))
            return nc

        CHA = 2
        with ExitStack() as st2:
            def sb2(name, shape, dt=F32):
                return st2.enter_context(nc.sbuf_tensor(name, shape, dt))

            class _NS2:
                pass
            bufs2 = []
            for ci in range(CHA):
                B = _NS2()
                B.kt = sb2(f"kt{ci}", [64, 2 * SEQ], BF16)
                B.va = sb2(f"va{ci}", [128, KCH * 65])
                B.qp = sb2(f"qp{ci}", [64, 2048], BF16)
                B.btl = sb2(f"btl{ci}", [128, 1024])
                B.sT = sb2(f"sT{ci}", [128, 1024])
                B.pT = sb2(f"pT{ci}", [128, 1024])
                B.cxh = sb2(f"cxh{ci}", [64, 512], BF16)
                B.cxl = sb2(f"cxl{ci}", [64, 512], BF16)
                B.pbS = (pb[ci * 4], pb[ci * 4 + 1])
                B.pbC = (pb[ci * 4 + 2], pb[ci * 4 + 3])
                bufs2.append(B)

            chains2 = [Chain(c, dsem={'sync': f'xs{i}', 'gpsimd': f'xg{i}'})
                       for i in range(CHA)]
            for c2 in chains2:
                c2.last = ch.last

            def gen_head_a(ci, h):
                B = bufs2[ci]
                chi = chains2[ci]

                def g_kt(gp, h=h):
                    r = [gp.dma_start(
                            out=B.qp[:, th * 1024: th * 1024 + 512],
                            in_=qs_dram[h * 64: h * 64 + 64,
                                        th * 512:(th + 1) * 512])
                         for th in range(2)]
                    r += [gp.dma_start(
                            out=B.qp[:, th * 1024 + 512: th * 1024 + 1024],
                            in_=qs_dram[D + h * 64: D + h * 64 + 64,
                                        th * 512:(th + 1) * 512])
                          for th in range(2)]
                    for rr in range(4):
                        r.append(gp.dma_start(
                            out=B.kt[:, rr * 1024:(rr + 1) * 1024],
                            in_=ag1k_out[rr * 2 * D + h * 64:
                                         rr * 2 * D + h * 64 + 64, :]))
                        r.append(gp.dma_start(
                            out=B.kt[:, SEQ + rr * 1024: SEQ + (rr + 1) * 1024],
                            in_=ag1k_out[rr * 2 * D + D + h * 64:
                                         rr * 2 * D + D + h * 64 + 64, :]))
                    return r
                chi.grp('gpsimd', g_kt, dma=True); yield

                def g_vo(v):
                    return [v.memset(B.va[:], 1.0)]
                chi.grp('vector', g_vo); yield

                def g_vl(sy, h=h):
                    return [sy.dma_start(
                        out=B.va[:, kc * 65: kc * 65 + 64],
                        in_=ag1v_out[kc * 128:(kc + 1) * 128,
                                     h * 64: h * 64 + 64])
                        for kc in range(KCH)]
                chi.grp('sync', g_vl, dma=True); yield

                for kc in range(KCH):
                    def g_b(sy, kc=kc):
                        return [sy.dma_start(
                            out=B.btl[:],
                            in_=rawap(msl, kc * 128, [[1, 128], [1, 1024]]))]
                    chi.grp('sync', g_b, dma=True); yield

                    def g_sc(te, kc=kc):
                        r = []
                        for tg in range(2):
                            i = 0
                            for (koff, qoff) in ((0, 0), (0, 512), (SEQ, 0)):
                                lhs = B.kt[:, koff + kc * 128:
                                           koff + (kc + 1) * 128]
                                rhs_ap = B.qp[:, tg * 1024 + qoff:
                                              tg * 1024 + qoff + 512]
                                r.append(te.matmul(
                                    B.pbS[tg][:, 0:512], lhs, rhs_ap,
                                    start=(i == 0), stop=(i == 2),
                                    skip_group_check=True))
                                i += 1
                        return r
                    chi.grp('tensor', g_sc); yield

                    def g_ev(v, kc=kc):
                        r = []
                        for tg in range(2):
                            bap = B.btl.ap()
                            rev = bass.AP(tensor=bap.tensor,
                                          offset=bap.offset + (1023 - tg * 512),
                                          ap=[[bap.ap[0][0], 128], [-1, 512]])
                            r.append(v.scalar_tensor_tensor(
                                B.sT[:, tg * 512:(tg + 1) * 512],
                                B.pbS[tg][:, 0:512], 1.0 / math.sqrt(HDA),
                                rev, ALU.mult, ALU.add))
                        return r
                    chi.grp('vector', g_ev); yield

                    def g_ex(s):
                        return [s.activation(B.pT[:], B.sT[:], AF.Exp,
                                             bias=consts[:, C_NEG_SA:C_NEG_SA + 1],
                                             scale=1.0)]
                    chi.grp('scalar', g_ex); yield

                    def g_cx(te, kc=kc):
                        return [te.matmul(
                            B.pbC[tg][0:65, 0:512],
                            B.va[:, kc * 65:(kc + 1) * 65],
                            B.pT[:, tg * 512:(tg + 1) * 512],
                            start=(kc == 0), stop=(kc == KCH - 1),
                            skip_group_check=True) for tg in range(2)]
                    chi.grp('tensor', g_cx); yield

                for tg in range(2):
                    def g_n1(v, tg=tg):
                        return [v.reciprocal(B.sT[0:1, 512:1024],
                                             B.pbC[tg][64:65, 0:512])]
                    chi.grp('vector', g_n1); yield

                    def g_n2a(gp, ci=ci):
                        return [gp.dma_start(out=rro_dram[ci],
                                             in_=B.sT[0:1, 512:1024])]
                    chi.grp('gpsimd', g_n2a, dma=True); yield

                    def g_n2(gp, ci=ci):
                        return [gp.dma_start(
                            out=B.pT[0:64, 512:1024],
                            in_=rawap(rro_dram[ci], 0, [[0, 64], [1, 512]]))]
                    chi.grp('gpsimd', g_n2, dma=True); yield

                    def g_n3(v, tg=tg):
                        return [v.tensor_tensor(B.pT[0:64, 0:512],
                                                B.pbC[tg][0:64, 0:512],
                                                B.pT[0:64, 512:1024],
                                                ALU.mult)]
                    chi.grp('vector', g_n3); yield

                    def g_n4(v):
                        return [v.tensor_copy(B.cxh[:], B.pT[0:64, 0:512])]
                    chi.grp('vector', g_n4); yield

                    def g_n5(v):
                        return [v.tensor_tensor(B.cxl[:], B.pT[0:64, 0:512],
                                                B.cxh[:], ALU.subtract)]
                    chi.grp('vector', g_n5); yield

                    def g_n6(gp, tg=tg, h=h):
                        return [gp.dma_start(
                            out=cxh_dram[h * 64: h * 64 + 64,
                                         tg * 512:(tg + 1) * 512],
                            in_=B.cxh[:]),
                            gp.dma_start(
                            out=cxl_dram[h * 64: h * 64 + 64,
                                         tg * 512:(tg + 1) * 512],
                            in_=B.cxl[:])]
                    chi.grp('gpsimd', g_n6, dma=True); yield

            gens = []
            for ci in range(CHA):
                def mk(ci=ci):
                    for h in range(ci, HA, CHA):
                        yield from gen_head_a(ci, h)
                gens.append(mk())
            drive(gens)

            def g_bar2(v):
                return [v.memset(stg2[:, 0:1], 0.0)]
            ch.grp('vector', g_bar2, extra=[c2.last for c2 in chains2])

        def wo_cb(oc, th, bank):
            add_psum_T_to_xres(bank, oc, th)

        with wblock():
            for th in range(2):
                def g_lh(sy, th=th):
                    return [sy.dma_start(
                        out=hTh[:, dc * 512:(dc + 1) * 512],
                        in_=cxh_dram[dc * 128:(dc + 1) * 128,
                                     th * 512:(th + 1) * 512])
                        for dc in range(DC)]
                ch.grp('sync', g_lh, dma=True)

                def g_ll(sy, th=th):
                    return [sy.dma_start(
                        out=hTl[:, dc * 512:(dc + 1) * 512],
                        in_=cxl_dram[dc * 128:(dc + 1) * 128,
                                     th * 512:(th + 1) * 512])
                        for dc in range(DC)]
                ch.grp('sync', g_ll, dma=True)
                proj3(IN['woT_a'], wo_cb, th)
        tap_xres("x1")


        if STOP_AFTER == "x1":
            emit_dumps()

            def g_out(gp):
                return [gp.dma_start(out=xout[ts(tt), :],
                                     in_=xres[:, tt * D:(tt + 1) * D])
                        for tt in range(NT)]
            ch.grp('gpsimd', g_out, dma=True)

            def g_fin(gp):
                return [gp.engine_nop()]
            ch.grp('gpsimd', g_fin)
            return nc

        _ffn_id = [0]

        def emit_ffn(tapname):
            sfx = str(_ffn_id[0]); _ffn_id[0] += 1
            with nc.sbuf_tensor("tABu" + sfx, [128, 2 * 2048], BF16) as tABu, \
                 nc.sbuf_tensor("tABg" + sfx, [128, 2 * 2048], BF16) as tABg, \
                 nc.sbuf_tensor("tdn" + sfx, [128, 4 * 2048], BF16) as tdn, \
                 nc.sbuf_tensor("hidh" + sfx, [128, 2 * 4 * 512], BF16) as hidh, \
                 nc.sbuf_tensor("hidl" + sfx, [128, 2 * 4 * 512], BF16) as hidl, \
                 nc.sbuf_tensor("dacc" + sfx, [128, 16 * 512], F32) as dacc:
                for th in range(2):
                    emit_ln(th)

                    def g_z(v):
                        return [v.memset(dacc[:], 0.0)]
                    ch.grp('vector', g_z)

                    H = {}
                    chW = Chain(c, dsem={'sync': 'xs0'})
                    chD = Chain(c, dsem={'gpsimd': 'xg0'})
                    chP = Chain(c); chE = Chain(c)
                    for cc_ in (chW, chD, chP, chE):
                        cc_.last = ch.last

                    def wait_h(keys):
                        while any(k in k2valid and k not in H for k in keys):
                            yield

                    k2valid = set()

                    def ex(*keys):
                        return [H[k] for k in keys if k in H]

                    def gen_W():
                        for ec in range(64):
                            sl = (ec % 2) * 2048
                            yield from wait_h([('sil', ec - 2)])

                            def g_lu(sy, ec=ec, sl=sl):
                                r = []
                                for dc in range(DC):
                                    row0 = (dc // 2) * 768 + (dc % 2) * 128
                                    r.append(sy.dma_start(
                                        out=tABu[:, sl + dc * 128:
                                                 sl + (dc + 1) * 128],
                                        in_=rawap(agt_out, row0 * E + ec * 128,
                                                  [[E, 128], [1, 128]])))
                                return r
                            H['wu', ec] = chW.grp('sync', g_lu, dma=True,
                                                  extra=ex(('sil', ec - 2)))
                            yield
                            yield from wait_h([('h32', ec - 2)])

                            def g_lg(sy, ec=ec, sl=sl):
                                r = []
                                for dc in range(DC):
                                    row0 = (dc // 2) * 768 + 256 + (dc % 2) * 128
                                    r.append(sy.dma_start(
                                        out=tABg[:, sl + dc * 128:
                                                 sl + (dc + 1) * 128],
                                        in_=rawap(agt_out, row0 * E + ec * 128,
                                                  [[E, 128], [1, 128]])))
                                return r
                            H['wg', ec] = chW.grp('sync', g_lg, dma=True,
                                                  extra=ex(('h32', ec - 2)))
                            yield

                    def gen_D():
                        for ecg in range(16):
                            for el in range(4):
                                ec = ecg * 4 + el
                                yield from wait_h([('pd', (ecg - 1) * 16 + 15)])

                                def g_ld(gp, ec=ec, el=el):
                                    row0 = (ec // 8) * 768 + 512 + (ec % 8) * 32
                                    return [gp.dma_start(
                                        out=tdn[:, el * 2048:(el + 1) * 2048],
                                        in_=rawap(agt_out, row0 * E,
                                                  [[2048, 128], [1, 2048]]))]
                                H['dn', ecg, el] = chD.grp(
                                    'gpsimd', g_ld, dma=True,
                                    extra=ex(('pd', (ecg - 1) * 16 + 15)))
                                yield

                    def gen_P():
                        for ecg in range(16):
                            hoff = (ecg % 2) * 2048
                            for el in range(4):
                                ec = ecg * 4 + el
                                bu = ec % 2
                                bg = 2 + ec % 2
                                yield from wait_h([('wu', ec), ('sil', ec - 2)])

                                def g_up(te, bu=bu, ec=ec):
                                    sl = (ec % 2) * 2048
                                    r = []
                                    i = 0
                                    for dc in range(DC):
                                        lhs = tABu[:, sl + dc * 128:
                                                   sl + (dc + 1) * 128]
                                        for rhs in (hTh[:, dc * 512:(dc + 1) * 512],
                                                    hTl[:, dc * 512:(dc + 1) * 512]):
                                            r.append(te.matmul(pb[bu][:, 0:512],
                                                               lhs, rhs,
                                                               start=(i == 0),
                                                               stop=(i == 31)))
                                            i += 1
                                    return r
                                H['pu', ec] = chP.grp(
                                    'tensor', g_up,
                                    extra=ex(('wu', ec), ('sil', ec - 2)))
                                yield
                                yield from wait_h([('wg', ec), ('h32', ec - 2)])

                                def g_gt(te, bg=bg, ec=ec):
                                    sl = (ec % 2) * 2048
                                    r = []
                                    i = 0
                                    for dc in range(DC):
                                        lhs = tABg[:, sl + dc * 128:
                                                   sl + (dc + 1) * 128]
                                        for rhs in (hTh[:, dc * 512:(dc + 1) * 512],
                                                    hTl[:, dc * 512:(dc + 1) * 512]):
                                            r.append(te.matmul(pb[bg][:, 0:512],
                                                               lhs, rhs,
                                                               start=(i == 0),
                                                               stop=(i == 31)))
                                            i += 1
                                    return r
                                H['pg', ec] = chP.grp(
                                    'tensor', g_gt,
                                    extra=ex(('wg', ec), ('h32', ec - 2)))
                                yield
                            for dc in range(DC):
                                kk = ecg * 16 + dc
                                bd = 4 + dc % 2
                                deps = [('dn', ecg, 3), ('hl', ecg * 4 + 3),
                                        ('ac1', kk - 2)]
                                yield from wait_h(deps)

                                def g_dn(te, dc=dc, bd=bd, hoff=hoff):
                                    r = []
                                    i = 0
                                    for el in range(4):
                                        lhs = tdn[:, el * 2048 + dc * 128:
                                                  el * 2048 + (dc + 1) * 128]
                                        for rhs in (hidh[:, hoff + el * 512:
                                                         hoff + (el + 1) * 512],
                                                    hidl[:, hoff + el * 512:
                                                         hoff + (el + 1) * 512]):
                                            r.append(te.matmul(pb[bd][:, 0:512],
                                                               lhs, rhs,
                                                               start=(i == 0),
                                                               stop=(i == 7)))
                                            i += 1
                                    return r
                                H['pd', kk] = chP.grp('tensor', g_dn,
                                                      extra=ex(*deps))
                                yield

                    def gen_E():
                        for ecg in range(16):
                            hoff = (ecg % 2) * 2048
                            for el in range(4):
                                ec = ecg * 4 + el
                                bu = ec % 2
                                bg = 2 + ec % 2
                                yield from wait_h([('pu', ec)])

                                def g_sil(s, bu=bu):
                                    return [s.activation(
                                        stg[:, 0:512], pb[bu][:, 0:512], AF.Silu,
                                        bias=consts[:, C_ZERO:C_ZERO + 1],
                                        scale=consts[:, C_SU:C_SU + 1])]
                                H['sil', ec] = chE.grp('scalar', g_sil,
                                                       extra=ex(('pu', ec)))
                                yield
                                yield from wait_h([('pg', ec)])

                                def g_h32(v, bg=bg):
                                    return [v.scalar_tensor_tensor(
                                        stg2[:, 0:512], pb[bg][:, 0:512],
                                        consts[:, C_SG:C_SG + 1], stg[:, 0:512],
                                        ALU.mult, ALU.mult)]
                                H['h32', ec] = chE.grp('vector', g_h32,
                                                       extra=ex(('pg', ec)))
                                yield
                                yield from wait_h([('pd', (ecg - 2) * 16 + 15)])

                                def g_hh(v, el=el, hoff=hoff):
                                    return [v.tensor_copy(
                                        hidh[:, hoff + el * 512:
                                             hoff + (el + 1) * 512],
                                        stg2[:, 0:512])]
                                chE.grp('vector', g_hh,
                                        extra=ex(('pd', (ecg - 2) * 16 + 15)))
                                yield

                                def g_hl(v, el=el, hoff=hoff):
                                    return [v.tensor_tensor(
                                        hidl[:, hoff + el * 512:
                                             hoff + (el + 1) * 512],
                                        stg2[:, 0:512],
                                        hidh[:, hoff + el * 512:
                                             hoff + (el + 1) * 512],
                                        ALU.subtract)]
                                H['hl', ec] = chE.grp('vector', g_hl)
                                yield
                            for dc in range(DC):
                                kk = ecg * 16 + dc
                                bd = 4 + dc % 2
                                yield from wait_h([('pd', kk)])

                                def g_ac1(v, dc=dc, bd=bd):
                                    return [v.scalar_tensor_tensor(
                                        scrA[:, 0:512], pb[bd][:, 0:512],
                                        consts[:, C_SD:C_SD + 1],
                                        dacc[:, dc * 512:(dc + 1) * 512],
                                        ALU.mult, ALU.add)]
                                H['ac1', kk] = chE.grp('vector', g_ac1,
                                                       extra=ex(('pd', kk)))
                                yield

                                def g_ac2(v, dc=dc):
                                    return [v.tensor_copy(
                                        dacc[:, dc * 512:(dc + 1) * 512],
                                        scrA[:, 0:512])]
                                chE.grp('vector', g_ac2)
                                yield

                    for ec in range(64):
                        k2valid.add(('wu', ec)); k2valid.add(('wg', ec))
                        k2valid.add(('pu', ec)); k2valid.add(('pg', ec))
                        k2valid.add(('sil', ec)); k2valid.add(('h32', ec))
                        k2valid.add(('hl', ec))
                    for ecg in range(16):
                        for el in range(4):
                            k2valid.add(('dn', ecg, el))
                        for dc in range(DC):
                            k2valid.add(('pd', ecg * 16 + dc))
                            k2valid.add(('ac1', ecg * 16 + dc))

                    drive([gen_W(), gen_D(), gen_P(), gen_E()])

                    def g_barf(v):
                        return [v.memset(stg2[:, 0:1], 0.0)]
                    ch.grp('vector', g_barf,
                           extra=[chW.last, chD.last, chP.last, chE.last])

                    for dc in range(DC):
                        for q in range(4):
                            def g_t(te, dc=dc, q=q):
                                return [te.transpose(
                                    pb[6][:, 0:128],
                                    dacc[:, dc * 512 + q * 128: dc * 512 + (q + 1) * 128],
                                    ident[:])]
                            ch.grp('tensor', g_t)

                            def g_a(v, q=q, dc=dc, th=th):
                                tt = th * 4 + q
                                xt = xres[:, tt * D + dc * 128: tt * D + (dc + 1) * 128]
                                return [v.tensor_tensor(stg2[:, 0:128], xt,
                                                        pb[6][:, 0:128], ALU.add)]
                            ch.grp('vector', g_a)

                            def g_a2(v, q=q, dc=dc, th=th):
                                tt = th * 4 + q
                                xt = xres[:, tt * D + dc * 128: tt * D + (dc + 1) * 128]
                                return [v.tensor_copy(xt, stg2[:, 0:128])]
                            ch.grp('vector', g_a2)
            tap_xres(tapname)

        emit_ffn("x2")
        if stop_here("x2"):
            print("cnt:", dict(c.cnt))
            return nc

        with ExitStack() as st5:
            def sb5(name, shape, dt=F32):
                return st5.enter_context(nc.sbuf_tensor(name, shape, dt))
            ropeC = sb5("ropeC", [128, 1024])
            ropeSs = sb5("ropeSs", [128, 1024])
            ropeCq = sb5("ropeCq", [128, 1024])
            ropeSq = sb5("ropeSq", [128, 1024])
            rsc = sb5("rsc", [128, 1024])
            rsc2 = sb5("rsc2", [128, 1024])
            ri32 = sb5("ri32", [128, 1024], I32)
            rswp = sb5("rswp", [128, 512])
            w1b = sb5("w1b", [128, 256])
            w1h = sb5("w1h", [128, 256], BF16)
            w1l = sb5("w1l", [128, 256], BF16)
            w2b = sb5("w2b", [16, 1024])
            w2h = sb5("w2h", [16, 1024], BF16)
            w2l = sb5("w2l", [16, 1024], BF16)
            usb = sb5("usb", [16, 512])
            uh = sb5("uh", [16, 512], BF16)
            ul = sb5("ul", [16, 512], BF16)
            def g_i1(gp):
                return [gp.iota(rsc[:], pattern=[[1, 1024]], base=0,
                                channel_multiplier=0,
                                allow_small_or_imprecise_dtypes=True),
                        gp.iota(stat[0:64, 7:8], pattern=[[1, 1]], base=0,
                                channel_multiplier=1,
                                allow_small_or_imprecise_dtypes=True),
                        gp.iota(stat[64:128, 7:8], pattern=[[1, 1]], base=0,
                                channel_multiplier=1,
                                allow_small_or_imprecise_dtypes=True)]
            ch.grp('gpsimd', g_i1)

            def g_i2(v):
                return [v.tensor_scalar_add(rsc2[:], rsc[:], q0_bc[:, 0:1])]
            ch.grp('vector', g_i2)

            def g_i3(s):
                return [s.activation(stat[:, 6:7], stat[:, 7:8], AF.Exp,
                                     bias=consts[:, C_ZERO:C_ZERO + 1],
                                     scale=consts[:, C_LNIF:C_LNIF + 1])]
            ch.grp('scalar', g_i3)

            def g_i4(v):
                return [v.tensor_scalar(rsc[:], rsc2[:], stat[:, 6:7], None,
                                        ALU.mult)]
            ch.grp('vector', g_i4)

            def emit_sincos(src_ap, dst_ap, add_half):
                if add_half:
                    def g_a(v):
                        return [v.tensor_scalar_add(rsc2[:], src_ap, math.pi / 2)]
                    ch.grp('vector', g_a)
                    base = rsc2
                else:
                    def g_a(v):
                        return [v.tensor_copy(rsc2[:], src_ap)]
                    ch.grp('vector', g_a)
                    base = rsc2

                def g_k(v):
                    return [v.tensor_scalar_mul(stg[:, 0:1024], base[:],
                                                0.5 / math.pi)]
                ch.grp('vector', g_k)

                def g_ki(v):
                    return [v.tensor_copy(ri32[:], stg[:, 0:1024])]
                ch.grp('vector', g_ki)

                def g_kf(v):
                    return [v.tensor_copy(stg[:, 0:1024], ri32[:])]
                ch.grp('vector', g_kf)

                hi = float(np.float32(2 * math.pi))
                lo = float(2 * math.pi - float(np.float32(2 * math.pi)))

                def g_y1(v):
                    return [v.scalar_tensor_tensor(stg2[:, 0:1024], stg[:, 0:1024],
                                                   -hi, base[:], ALU.mult, ALU.add)]
                ch.grp('vector', g_y1)

                def g_y2(v):
                    return [v.scalar_tensor_tensor(rsc2[:], stg[:, 0:1024],
                                                   -lo, stg2[:, 0:1024],
                                                   ALU.mult, ALU.add)]
                ch.grp('vector', g_y2)

                def g_cl(v):
                    return [v.tensor_scalar(stg[:, 0:1024], rsc2[:],
                                            math.pi, -math.pi, ALU.min, ALU.max)]
                ch.grp('vector', g_cl)

                def g_s(s):
                    return [s.activation(dst_ap, stg[:, 0:1024], AF.Sin,
                                         bias=consts[:, C_ZERO:C_ZERO + 1],
                                         scale=1.0)]
                ch.grp('scalar', g_s)

            emit_sincos(rsc[:], ropeSs[:], False)
            emit_sincos(rsc[:], ropeC[:], True)

            def g_sf(v):
                return [v.tensor_scalar_mul(ropeSq[0:64, :], ropeSs[0:64, :], -1.0)]
            ch.grp('vector', g_sf)

            def g_sf2(v):
                return [v.tensor_copy(ropeSq[64:128, :], ropeSs[64:128, :])]
            ch.grp('vector', g_sf2)

            def g_sf3(v):
                return [v.tensor_copy(ropeSs[:], ropeSq[:])]
            ch.grp('vector', g_sf3)

            def g_sc1(v):
                return [v.tensor_scalar_mul(ropeCq[:], ropeC[:],
                                            1.0 / math.sqrt(DKB))]
            ch.grp('vector', g_sc1)

            def g_sc2(v):
                return [v.tensor_scalar_mul(ropeSq[:], ropeSs[:],
                                            1.0 / math.sqrt(DKB))]
            ch.grp('vector', g_sc2)

            def g_w1(sy):
                return [sy.dma_start(out=w1b[:, dc * 16:(dc + 1) * 16],
                                     in_=IN['Ws1T'][ts(dc), :])
                        for dc in range(DC)]
            ch.grp('sync', g_w1, dma=True)

            def g_w1h(v):
                return [v.tensor_copy(w1h[:], w1b[:])]
            ch.grp('vector', g_w1h)

            def g_w1l(v):
                return [v.tensor_tensor(w1l[:], w1b[:], w1h[:], ALU.subtract)]
            ch.grp('vector', g_w1l)

            def g_w2(sy):
                return [sy.dma_start(out=w2b[:], in_=IN['Ws2T'])]
            ch.grp('sync', g_w2, dma=True)

            def g_w2h(v):
                return [v.tensor_copy(w2h[:], w2b[:])]
            ch.grp('vector', g_w2h)

            def g_w2l(v):
                return [v.tensor_tensor(w2l[:], w2b[:], w2h[:], ALU.subtract)]
            ch.grp('vector', g_w2l)

            def rope_apply(bank, th, ctab, stab, dst):
                def g_e(v):
                    return [v.tensor_copy(stg[:, 0:512], pb[bank][:, 0:512])]
                ch.grp('vector', g_e)

                def g_sw(gp):
                    return [gp.dma_start(out=rswp[0:64, :], in_=stg[64:128, 0:512]),
                            gp.dma_start(out=rswp[64:128, :], in_=stg[0:64, 0:512])]
                ch.grp('gpsimd', g_sw, dma=True)

                def g_m1(v, th=th):
                    return [v.tensor_tensor(scrA[:, 0:512], stg[:, 0:512],
                                            ctab[:, th * 512:(th + 1) * 512],
                                            ALU.mult)]
                ch.grp('vector', g_m1)

                def g_m2(v, th=th):
                    return [v.tensor_tensor(scrA[:, 512:1024], rswp[:],
                                            stab[:, th * 512:(th + 1) * 512],
                                            ALU.mult)]
                ch.grp('vector', g_m2)

                def g_m3(v):
                    return [v.tensor_tensor(dst, scrA[:, 0:512],
                                            scrA[:, 512:1024], ALU.add)]
                ch.grp('vector', g_m3)

            def store_tok(src, th, colbase, oc):
                for q in range(4):
                    def g_t(te, q=q):
                        return [te.transpose(pb[6][:, 0:128],
                                             src[:, q * 128:(q + 1) * 128],
                                             ident[:])]
                    ch.grp('tensor', g_t)

                    def g_v(v):
                        return [v.tensor_copy(stg[:, 0:128], pb[6][:, 0:128])]
                    ch.grp('vector', g_v)

                    def g_d(gp, q=q, th=th, oc=oc, colbase=colbase):
                        return [gp.dma_start(
                            out=ag2_in[th * 512 + q * 128: th * 512 + (q + 1) * 128,
                                       colbase + oc * 128: colbase + (oc + 1) * 128],
                            in_=stg[:, 0:128])]
                    ch.grp('gpsimd', g_d, dma=True)

            def k_cb5(oc, th, bank):
                rope_apply(bank, th, ropeC, ropeSs, stg2[:, 0:512])
                store_tok(stg2[:, 0:512], th, 0, oc)

            def v_cb5(oc, th, bank):
                def g_e(v):
                    return [v.tensor_copy(stg2[:, 0:512], pb[bank][:, 0:512])]
                ch.grp('vector', g_e)
                store_tok(stg2[:, 0:512], th, D, oc)

            def q_cb5(oc, th, bank):
                rope_apply(bank, th, ropeCq, ropeSq, stg2[:, 0:512])
                store_tok(stg2[:, 0:512], th, 2 * D, oc)

            def g_cb5(oc, th, bank):
                def g_e(v):
                    return [v.tensor_copy(stg2[:, 0:512], pb[bank][:, 0:512])]
                ch.grp('vector', g_e)
                for q in range(4):
                    def g_t(te, q=q):
                        return [te.transpose(pb[6][:, 0:128],
                                             stg2[:, q * 128:(q + 1) * 128],
                                             ident[:])]
                    ch.grp('tensor', g_t)

                    def g_v(v):
                        return [v.tensor_copy(stg[:, 0:128], pb[6][:, 0:128])]
                    ch.grp('vector', g_v)

                    def g_d(gp, q=q, th=th, oc=oc):
                        return [gp.dma_start(
                            out=g_dram[th * 512 + q * 128: th * 512 + (q + 1) * 128,
                                       oc * 128:(oc + 1) * 128],
                            in_=stg[:, 0:128])]
                    ch.grp('gpsimd', g_d, dma=True)

            _wb5 = wblock(); _wb5.__enter__()
            for th in range(2):
                emit_ln(th)
                proj3(IN['WkT_b'], k_cb5, th)
                proj3(IN['WvT_b'], v_cb5, th)
                proj3(IN['WqT_b'], q_cb5, th)
                proj3(IN['WgT_b'], g_cb5, th)
                def g_u(te):
                    r = []
                    i = 0
                    for dc in range(DC):
                        for lhs, rhs in (
                                (w1h[:, dc * 16:(dc + 1) * 16],
                                 hTh[:, dc * 512:(dc + 1) * 512]),
                                (w1l[:, dc * 16:(dc + 1) * 16],
                                 hTh[:, dc * 512:(dc + 1) * 512]),
                                (w1h[:, dc * 16:(dc + 1) * 16],
                                 hTl[:, dc * 512:(dc + 1) * 512])):
                            r.append(te.matmul(pb[3][0:16, 0:512], lhs, rhs,
                                               start=(i == 0), stop=(i == 47)))
                            i += 1
                    return r
                ch.grp('tensor', g_u)

                def g_ue(v):
                    return [v.tensor_copy(usb[:], pb[3][0:16, 0:512])]
                ch.grp('vector', g_ue)

                def g_uh(v):
                    return [v.tensor_copy(uh[:], usb[:])]
                ch.grp('vector', g_uh)

                def g_ul(v):
                    return [v.tensor_tensor(ul[:], usb[:], uh[:], ALU.subtract)]
                ch.grp('vector', g_ul)

                for fc in range(8):
                    def g_s2(te, fc=fc):
                        r = []
                        for i, (lhs, rhs) in enumerate((
                                (w2h[:, fc * 128:(fc + 1) * 128], uh[:]),
                                (w2l[:, fc * 128:(fc + 1) * 128], uh[:]),
                                (w2h[:, fc * 128:(fc + 1) * 128], ul[:]))):
                            r.append(te.matmul(pb[4][:, 0:512], lhs, rhs,
                                               start=(i == 0), stop=(i == 2)))
                        return r
                    ch.grp('tensor', g_s2)

                    def g_sb(s, fc=fc):
                        return [s.activation(stg[:, 0:512], pb[4][:, 0:512],
                                             AF.Identity,
                                             bias=bs2_sb[:, fc:fc + 1], scale=1.0)]
                    ch.grp('scalar', g_sb)

                    def g_clip(v):
                        return [v.tensor_scalar(stg2[:, 0:512], stg[:, 0:512],
                                                CLAMP, -CLAMP, ALU.min, ALU.max)]
                    ch.grp('vector', g_clip)

                    def g_es(s):
                        return [s.activation(scrA[:, 0:512], stg2[:, 0:512],
                                             AF.Exp,
                                             bias=consts[:, C_NEG_SB:C_NEG_SB + 1],
                                             scale=1.0)]
                    ch.grp('scalar', g_es)

                    for q in range(4):
                        def g_t(te, q=q):
                            return [te.transpose(pb[6][:, 0:128],
                                                 scrA[:, q * 128:(q + 1) * 128],
                                                 ident[:])]
                        ch.grp('tensor', g_t)

                        def g_v(v):
                            return [v.tensor_copy(stg[:, 0:128], pb[6][:, 0:128])]
                        ch.grp('vector', g_v)

                        def g_d(gp, q=q, th=th, fc=fc):
                            return [gp.dma_start(
                                out=ag2_in[th * 512 + q * 128:
                                           th * 512 + (q + 1) * 128,
                                           3 * D + fc * 128: 3 * D + (fc + 1) * 128],
                                in_=stg[:, 0:128])]
                        ch.grp('gpsimd', g_d, dma=True)

            _wb5.__exit__(None, None, None)

            def g_ag2(gp):
                return [gp.collective_compute("AllGather", ALU.bypass,
                                              replica_groups=G4,
                                              ins=[ag2_in], outs=[ag2_out])]
            ch.grp('gpsimd', g_ag2, collective=True)

        if stop_here("p5"):
            print("cnt:", dict(c.cnt))
            return nc

        CH6 = 4

        with ExitStack() as st6:
            def sb6(name, shape, dt=F32):
                return st6.enter_context(nc.sbuf_tensor(name, shape, dt))

            class _NS:
                pass
            bufs6 = []
            for ci in range(CH6):
                B = _NS()
                B.kch = sb6(f"kch{ci}", [128, 128])
                B.vch = sb6(f"vch{ci}", [128, 128])
                B.qch = sb6(f"qch{ci}", [128, 128])
                B.ech = sb6(f"ech{ci}", [128, 64])
                B.ktT = sb6(f"ktT{ci}", [128, 128])
                B.qtT = sb6(f"qtT{ci}", [128, 128])
                B.esT = sb6(f"esT{ci}", [64, 128])
                B.cumsT = sb6(f"cumsT{ci}", [64, 128])
                B.enzT = sb6(f"enzT{ci}", [64, 128])
                B.enz_t = sb6(f"enz_t{ci}", [128, 64])
                B.car = sb6(f"car{ci}", [64, 2])
                B.aT = sb6(f"aT{ci}", [128, 128])
                B.ok_t = sb6(f"ok_t{ci}", [128, 64])
                B.et_t = sb6(f"et_t{ci}", [128, 64])
                B.w_t = sb6(f"w_t{ci}", [128, 64])
                B.wtT = sb6(f"wtT{ci}", [64, 128])
                B.w2T = sb6(f"w2T{ci}", [128, 128])
                B.sS = sb6(f"sS{ci}", [128, 2 * 64])
                B.sC = sb6(f"sC{ci}", [64, 2 * 128])
                B.oac = sb6(f"oac{ci}", [128, 2 * 1024])
                B.sft = sb6(f"sft{ci}", [128, 8])
                B.pst = sb6(f"pst{ci}", [128, 640])
                B.pbA = pb[2 * ci]
                B.pbB = pb[2 * ci + 1]
                bufs6.append(B)

            chains6 = [Chain(c, dsem={'sync': f'xs{i}', 'gpsimd': f'xg{i}'})
                       for i in range(CH6)]
            for c6 in chains6:
                c6.last = ch.last

            def gen_head(ci, h):
                B = bufs6[ci]
                chi = chains6[ci]
                trn = B.pbA[:, 0:128]
                pA = B.pbA[:, 128:256]
                pW2 = B.pbA[:, 256:384]
                pO = B.pbA[:, 384:512]
                pY = B.pbB[:, 0:64]
                pSu = B.pbB[:, 64:128]
                pCu = B.pbB[0:64, 128:256]

                def g_z(v):
                    return [v.memset(B.sS[:], 0.0), v.memset(B.sC[:], 0.0),
                            v.memset(B.oac[:], 0.0), v.memset(B.car[:], 0.0)]
                chi.grp('vector', g_z); yield

                for j in range(KCH):
                    par = j % 2
                    npar = 1 - par

                    def g_l(sy, j=j, h=h):
                        return [
                            sy.dma_start(out=B.kch[:],
                                         in_=ag2_out[j * 128:(j + 1) * 128,
                                                     h * 128:(h + 1) * 128]),
                            sy.dma_start(out=B.vch[:],
                                         in_=ag2_out[j * 128:(j + 1) * 128,
                                                     D + h * 128: D + (h + 1) * 128]),
                            sy.dma_start(out=B.qch[:],
                                         in_=ag2_out[j * 128:(j + 1) * 128,
                                                     2 * D + h * 128:
                                                     2 * D + (h + 1) * 128]),
                            sy.dma_start(out=B.ech[:],
                                         in_=ag2_out[j * 128:(j + 1) * 128,
                                                     3 * D + h * 64:
                                                     3 * D + (h + 1) * 64])]
                    chi.grp('sync', g_l, dma=True); yield

                    def g_t1(te):
                        return [te.transpose(trn, B.kch[:], ident[:])]
                    chi.grp('tensor', g_t1); yield

                    def g_e1(v):
                        return [v.tensor_copy(B.ktT[:], trn)]
                    chi.grp('vector', g_e1); yield

                    def g_t2(te):
                        return [te.transpose(trn, B.qch[:], ident[:])]
                    chi.grp('tensor', g_t2); yield

                    def g_e2(v):
                        return [v.tensor_copy(B.qtT[:], trn)]
                    chi.grp('vector', g_e2); yield

                    def g_t3(te):
                        return [te.transpose(B.pbA[0:64, 0:128], B.ech[:],
                                             ident[:])]
                    chi.grp('tensor', g_t3); yield

                    def g_e3(v):
                        return [v.tensor_copy(B.esT[:], B.pbA[0:64, 0:128])]
                    chi.grp('vector', g_e3); yield

                    def g_cs(v, par=par):
                        return [v.tensor_tensor_scan(B.cumsT[:], ones_t[0:64, :],
                                                     B.esT[:], B.car[:, par:par + 1],
                                                     ALU.mult, ALU.add)]
                    chi.grp('vector', g_cs); yield

                    def g_cc(v, npar=npar):
                        return [v.tensor_copy(B.car[:, npar:npar + 1],
                                              B.cumsT[:, 127:128])]
                    chi.grp('vector', g_cc); yield

                    def g_rc(v):
                        return [v.reciprocal(B.enzT[:], B.cumsT[:])]
                    chi.grp('vector', g_rc); yield

                    def g_t4(te):
                        return [te.transpose(B.pbA[:, 0:64], B.enzT[:],
                                             ident[0:64, 0:64])]
                    chi.grp('tensor', g_t4); yield

                    def g_e4(v):
                        return [v.tensor_copy(B.enz_t[:], B.pbA[:, 0:64])]
                    chi.grp('vector', g_e4); yield

                    def g_A(te):
                        return [te.matmul(pA, B.ktT[:], B.qtT[:],
                                          start=True, stop=True)]
                    chi.grp('tensor', g_A); yield

                    def g_Am(v):
                        return [v.tensor_tensor(B.aT[:], pA, maskA[:],
                                                ALU.mult)]
                    chi.grp('vector', g_Am); yield

                    def g_y(te, par=par):
                        return [te.matmul(pY, B.aT[:], B.ech[:],
                                          start=True, stop=False),
                                te.matmul(pY, B.qtT[:],
                                          B.sS[:, par * 64:(par + 1) * 64],
                                          start=False, stop=True)]
                    chi.grp('tensor', g_y); yield

                    def g_ok(v):
                        return [v.tensor_tensor(B.ok_t[:], pY,
                                                B.enz_t[:], ALU.mult)]
                    chi.grp('vector', g_ok); yield

                    def g_mx(v):
                        return [v.reduce_max(B.sft[:, 0:1], B.ok_t[:], AX)]
                    chi.grp('vector', g_mx); yield

                    def g_nm(v):
                        return [v.tensor_scalar_mul(B.sft[:, 1:2], B.sft[:, 0:1],
                                                    -1.0)]
                    chi.grp('vector', g_nm); yield

                    def g_ex(s):
                        return [s.activation(B.et_t[:], B.ok_t[:], AF.Exp,
                                             bias=B.sft[:, 1:2], scale=1.0,
                                             accum_out=B.sft[:, 2:3])]
                    chi.grp('scalar', g_ex); yield

                    def g_rs(v):
                        return [v.reciprocal(B.sft[:, 3:4], B.sft[:, 2:3])]
                    chi.grp('vector', g_rs); yield

                    def g_w(v):
                        return [v.scalar_tensor_tensor(B.w_t[:], B.et_t[:],
                                                       B.sft[:, 3:4], B.enz_t[:],
                                                       ALU.mult, ALU.mult)]
                    chi.grp('vector', g_w); yield

                    def g_t5(te):
                        return [te.transpose(B.pbA[0:64, 0:128], B.w_t[:],
                                             ident[:])]
                    chi.grp('tensor', g_t5); yield

                    def g_e5(v):
                        return [v.tensor_copy(B.wtT[:], B.pbA[0:64, 0:128])]
                    chi.grp('vector', g_e5); yield

                    def g_W2(te):
                        return [te.matmul(pW2, B.esT[:], B.wtT[:],
                                          start=True, stop=True)]
                    chi.grp('tensor', g_W2); yield

                    def g_W2m(v):
                        return [v.tensor_tensor(B.w2T[:], pW2, maskA[:],
                                                ALU.mult)]
                    chi.grp('vector', g_W2m); yield

                    def g_o(te, par=par):
                        return [te.matmul(pO, B.w2T[:], B.vch[:],
                                          start=True, stop=False),
                                te.matmul(pO, B.wtT[:],
                                          B.sC[:, par * 128:(par + 1) * 128],
                                          start=False, stop=True)]
                    chi.grp('tensor', g_o); yield

                    def g_oa(v, j=j):
                        sl = (j % 8) * 128
                        op = (j // 8) % 2
                        return [v.scalar_tensor_tensor(
                            B.oac[:, op * 1024 + sl: op * 1024 + sl + 128],
                            pO, gate_sb[:, j:j + 1],
                            B.oac[:, (1 - op) * 1024 + sl:
                                  (1 - op) * 1024 + sl + 128],
                            ALU.mult, ALU.add)]
                    chi.grp('vector', g_oa); yield

                    def g_su(te):
                        return [te.matmul(pSu, B.kch[:], B.ech[:],
                                          start=True, stop=True)]
                    chi.grp('tensor', g_su); yield

                    def g_sa(v, par=par, npar=npar):
                        return [v.scalar_tensor_tensor(
                            B.sS[:, npar * 64:(npar + 1) * 64], pSu,
                            1.0, B.sS[:, par * 64:(par + 1) * 64],
                            ALU.mult, ALU.add)]
                    chi.grp('vector', g_sa); yield

                    def g_cu(te):
                        return [te.matmul(pCu, B.ech[:], B.vch[:],
                                          start=True, stop=True)]
                    chi.grp('tensor', g_cu); yield

                    def g_ca(v, par=par, npar=npar):
                        return [v.scalar_tensor_tensor(
                            B.sC[:, npar * 128:(npar + 1) * 128],
                            pCu,
                            1.0, B.sC[:, par * 128:(par + 1) * 128],
                            ALU.mult, ALU.add)]
                    chi.grp('vector', g_ca); yield

                for sslot in range(8):
                    opar = 1
                    otile = B.oac[:, opar * 1024 + sslot * 128:
                                  opar * 1024 + sslot * 128 + 128]

                    def g_sq(s, otile=otile):
                        return [s.activation(B.pst[:, 512:640], otile, AF.Square,
                                             bias=consts[:, C_ZERO:C_ZERO + 1],
                                             scale=1.0, accum_out=B.sft[:, 4:5])]
                    chi.grp('scalar', g_sq); yield

                    def g_sqr(s):
                        return [s.activation(B.sft[:, 5:6], B.sft[:, 4:5], AF.Sqrt,
                                             bias=consts[:, C_EPS_RMS:C_EPS_RMS + 1],
                                             scale=consts[:, C_INV_DV:C_INV_DV + 1])]
                    chi.grp('scalar', g_sqr); yield

                    def g_rq(v):
                        return [v.reciprocal(B.sft[:, 6:7], B.sft[:, 5:6])]
                    chi.grp('vector', g_rq); yield

                    def g_on(v, otile=otile):
                        return [v.tensor_scalar(B.pst[:, 0:128], otile,
                                                B.sft[:, 6:7], None, ALU.mult)]
                    chi.grp('vector', g_on); yield

                    def g_gn(v):
                        return [v.tensor_tensor(B.pst[:, 384:512], B.pst[:, 0:128],
                                                gnw_bc[:], ALU.mult)]
                    chi.grp('vector', g_gn); yield

                    def g_gl(sy, sslot=sslot, h=h):
                        return [sy.dma_start(
                            out=B.pst[:, 128:256],
                            in_=g_dram[sslot * 128:(sslot + 1) * 128,
                                       h * 128:(h + 1) * 128])]
                    chi.grp('sync', g_gl, dma=True); yield

                    def g_sil(s):
                        return [s.activation(B.pst[:, 512:640], B.pst[:, 128:256],
                                             AF.Silu,
                                             bias=consts[:, C_ZERO:C_ZERO + 1],
                                             scale=1.0)]
                    chi.grp('scalar', g_sil); yield

                    def g_og(v):
                        return [v.tensor_tensor(B.pst[:, 256:384], B.pst[:, 384:512],
                                                B.pst[:, 512:640], ALU.mult)]
                    chi.grp('vector', g_og); yield

                    def g_d(gp, sslot=sslot, h=h):
                        return [gp.dma_start(
                            out=o_dram[sslot * 128:(sslot + 1) * 128,
                                       h * 128:(h + 1) * 128],
                            in_=B.pst[:, 256:384])]
                    chi.grp('gpsimd', g_d, dma=True); yield

            gens = []
            for ci in range(CH6):
                def mk(ci=ci):
                    for h in range(ci, HB, CH6):
                        yield from gen_head(ci, h)
                gens.append(mk())
            drive(gens)

            def g_bar6(v):
                return [v.memset(stg2[:, 0:1], 0.0)]
            ch.grp('vector', g_bar6, extra=[c6.last for c6 in chains6])

        _wb7 = wblock(); _wb7.__enter__()
        for th in range(2):
            for fc in range(DC):
                for q in range(4):
                    tt = th * 4 + q

                    def g_l(sy, fc=fc, tt=tt):
                        return [sy.dma_start(out=stg[:, 0:128],
                                             in_=o_dram[ts(tt),
                                                        fc * 128:(fc + 1) * 128])]
                    ch.grp('sync', g_l, dma=True)

                    def g_t(te):
                        return [te.transpose(pb[6][:, 0:128], stg[:, 0:128],
                                             ident[:])]
                    ch.grp('tensor', g_t)
                    split_evac(pb[6][:, 0:128],
                               hTh[:, fc * 512 + q * 128: fc * 512 + (q + 1) * 128],
                               hTl[:, fc * 512 + q * 128: fc * 512 + (q + 1) * 128])
            proj3(IN['WoT_b'], wo_cb, th)
        _wb7.__exit__(None, None, None)
        tap_xres("x3")
        if stop_here("x3"):
            print("cnt:", dict(c.cnt))
            return nc

        emit_ffn(None)

        def g_out(gp):
            return [gp.dma_start(out=xout[ts(tt), :],
                                 in_=xres[:, tt * D:(tt + 1) * D])
                    for tt in range(NT)]
        ch.grp('gpsimd', g_out, dma=True)
        emit_dumps()

        def g_fin(gp):
            return [gp.engine_nop()]
        ch.grp('gpsimd', g_fin)

    return nc


def _prep_inputs(inputs):
    f32 = np.float32
    x = np.ascontiguousarray(inputs['L0_attn_a_hidden_states'], dtype=f32)
    w = {k: np.ascontiguousarray(np.asarray(v).T, dtype=f32)
         for k, v in (("wqT_a", inputs['L0_attn_a_wq']),
                      ("wkT_a", inputs['L0_attn_a_wk']),
                      ("wvT_a", inputs['L0_attn_a_wv']),
                      ("woT_a", inputs['L0_attn_a_wo']),
                      ("WqT_b", inputs['L0_attn_b_Wq']),
                      ("WkT_b", inputs['L0_attn_b_Wk']),
                      ("WvT_b", inputs['L0_attn_b_Wv']),
                      ("WgT_b", inputs['L0_attn_b_Wg']),
                      ("WoT_b", inputs['L0_attn_b_Wo']),
                      ("Ws1T", inputs['L0_attn_b_Ws1']),
                      ("Ws2T", inputs['L0_attn_b_Ws2']))}
    wupT = np.ascontiguousarray(np.asarray(inputs['L0_ffn_W_up']).T, dtype=f32)
    wgateT = np.ascontiguousarray(np.asarray(inputs['L0_ffn_W_gate']).T, dtype=f32)
    wdownT = np.ascontiguousarray(np.asarray(inputs['L0_ffn_W_down']).T, dtype=f32)
    rel = np.ascontiguousarray(inputs['L0_attn_a_rel_pos'], dtype=f32)
    bs2 = np.ascontiguousarray(np.asarray(inputs['L0_attn_b_bs2']).reshape(1, 1024), dtype=f32)
    gnw = np.ascontiguousarray(np.asarray(inputs['L0_attn_b_g_norm_weight']).reshape(1, 128), dtype=f32)

    in_maps = []
    for core in range(NCORES):
        b, qi = core // 4, core % 4
        q0 = qi * TOK
        own0 = qi * NT
        gate = np.array([1.0 if own0 <= j < own0 + NT else 0.0
                         for j in range(32)], dtype=f32)
        widx = np.clip(np.arange(MS_LEN) - 512 - q0, 0, 1022)
        rel_ms = np.ascontiguousarray(rel[widx])
        m = {
            'xin': np.ascontiguousarray(x[b, q0:q0 + TOK, :]),
            'rel_ms': rel_ms,
            'Ws2T': w['Ws2T'], 'Ws1T': w['Ws1T'],
            'bs2': bs2, 'gnw': gnw,
            'wupT_s': np.ascontiguousarray(wupT[core * 256:(core + 1) * 256, :]),
            'wgateT_s': np.ascontiguousarray(wgateT[core * 256:(core + 1) * 256, :]),
            'wdownT_s': np.ascontiguousarray(wdownT[core * 1024:(core + 1) * 1024, :]),
            'meta_f': np.array([[q0, own0, 0, 0, 0, 0, 0, 0]], dtype=f32),
            'gate_own': gate.reshape(1, 32),
        }
        for k in ("wqT_a", "wkT_a", "wvT_a", "woT_a", "WqT_b", "WkT_b",
                  "WvT_b", "WgT_b", "WoT_b"):
            m[k] = w[k]
        in_maps.append(m)
    return in_maps


_NC_CACHE = {}


def kernel(**inputs):
    key = (STOP_AFTER, tuple(DEBUG_TAPS))
    if key not in _NC_CACHE:
        _NC_CACHE[key] = build_program()
    nc = _NC_CACHE[key]
    in_maps = _prep_inputs(inputs)
    res = run_bass_kernel_spmd(nc, in_maps, core_ids=list(range(NCORES)))
    out = np.empty((BATCH, SEQ, D), dtype=np.float32)
    for core in range(NCORES):
        b, qi = core // 4, core % 4
        out[b, qi * TOK:(qi + 1) * TOK, :] = res.results[core]["xout"]
    kernel.last_results = res.results
    return out


if __name__ == "__main__":
    build_program()
    print("build ok")


# version 11
# speedup vs baseline: 1.0059x; 1.0059x over previous
import math
from contextlib import ExitStack

import numpy as np

import concourse.bass as bass
import concourse.mybir as mybir
from concourse.bass_utils import run_bass_kernel_spmd

F32 = mybir.dt.float32
BF16 = mybir.dt.bfloat16
I32 = mybir.dt.int32
AF = mybir.ActivationFunctionType
ALU = mybir.AluOpType
AX = mybir.AxisListType.X

BATCH, SEQ, D = 2, 4096, 2048
NCORES = 8
TOK = 1024
NT = TOK // 128
DC = D // 128
HA, HDA = 32, 64
KCH = SEQ // 128
HB, DKB, DVB, MB = 16, 128, 128, 64
E = 8192
CLAMP = 32.0
EPS_LN = 1e-5
EPS_RMS = 1e-5
EPS_FFN = 1e-8
SHIFT_A = 32.0
SHIFT_B = 32.0
ROPE_BASE = 10000.0
MEXT_LEN = 8704
MS_LEN = 5376
AG2C = 3 * D + HB * MB

DEBUG_TAPS = []
P2_HG = 16
P2_KC = KCH
DEBUG_DUMP = []
STOP_AFTER = None

DMA_SEM = {'sync': 'sd', 'gpsimd': 'gd', 'vector': 'vd', 'scalar': 'ad'}
CMP_SEM = {'tensor': 't', 'vector': 'v', 'scalar': 'a', 'gpsimd': 'g',
           'sync': 'sy'}


class Ctx:
    def __init__(self):
        self.nc = bass.Bass("TRN2", target_bir_lowering=False)
        self.es = ExitStack()
        self.sem = {}
        self.cnt = {}
        self.lw = {}
        self.block = None

    def bump(self, key, amt=1):
        self.cnt[key] += amt
        return self.cnt[key]

    def wait(self, eng, ename, key, val):
        if val <= 0:
            return
        if self.lw.get((ename, key), -1) >= val:
            return
        eng.wait_ge(self.sem[key], val)
        self.lw[(ename, key)] = val

    def sbuf(self, name, shape, dt=F32, stack=None):
        return (stack or self.es).enter_context(
            self.nc.sbuf_tensor(name, shape, dt))


class Chain:

    def __init__(self, c, dsem=None):
        self.c = c
        self.last = None
        self.pending = []
        self.dsem = dsem or {}

    def grp(self, ename, fn, dma=False, collective=False, extra=()):
        c = self.c
        key = ('cc' if collective else
               (self.dsem.get(ename, DMA_SEM[ename]) if dma
                else CMP_SEM[ename]))
        waits = [w for w in extra if w is not None]
        if self.pending:
            waits.extend(w for w in self.pending if w is not None)
            self.pending = []
        if self.last is not None:
            waits.append(self.last)
        holder = {}

        def body(eng):
            for (k, v) in waits:
                c.wait(eng, ename, k, v)
            instrs = fn(eng)
            assert instrs, f"empty group on {ename}"
            if dma:
                for ins in instrs:
                    ins.then_inc(c.sem[key], 16)
                holder['n'] = 16 * len(instrs)
            else:
                instrs[-1].then_inc(c.sem[key], 1)
                holder['n'] = 1
        getattr(c.block, ename)(body)
        self.last = (key, c.bump(key, holder['n']))
        return self.last


def drive(gens):
    live = list(gens)
    while live:
        nxt = []
        for g in live:
            try:
                next(g)
                nxt.append(g)
            except StopIteration:
                pass
        live = nxt


def ts(i, n=128):
    return slice(i * n, (i + 1) * n)


def rawap(t, off, ap):
    return bass.AP(tensor=t.tensor, offset=off, ap=ap)


def build_program():
    c = Ctx()
    nc = c.nc

    def din(name, shape, dt=F32):
        return nc.dram_tensor(name, shape, dt, kind="ExternalInput").ap()

    def dint(name, shape, dt=F32):
        return nc.dram_tensor(name, shape, dt, kind="Internal").ap()

    IN = {'xin': din("xin", [TOK, D])}
    for nm in ("wqT_a", "wkT_a", "wvT_a", "woT_a",
               "WqT_b", "WkT_b", "WvT_b", "WgT_b", "WoT_b"):
        IN[nm] = din(nm, [D, D])
    IN['rel_ms'] = din("rel_ms", [MS_LEN, 64])
    IN['Ws1T'] = din("Ws1T", [D, 16])
    IN['Ws2T'] = din("Ws2T", [16, 1024])
    IN['bs2'] = din("bs2", [1, 1024])
    IN['gnw'] = din("gnw", [1, 128])
    IN['wupT_s'] = din("wupT_s", [D // 8, E])
    IN['wgateT_s'] = din("wgateT_s", [D // 8, E])
    IN['wdownT_s'] = din("wdownT_s", [E // 8, D])
    IN['meta_f'] = din("meta_f", [1, 8])
    IN['gate_own'] = din("gate_own", [1, 32])

    xout = nc.dram_tensor("xout", [TOK, D], F32, kind="ExternalOutput").ap()
    taps = {t: nc.dram_tensor("tap_" + t, [TOK, D], F32,
                              kind="ExternalOutput").ap() for t in DEBUG_TAPS}

    msl = dint("msl", [1, MS_LEN])
    ar_in = dint("ar_in", [1, 8]); ar_out = dint("ar_out", [1, 8])
    agt_in = dint("agt_in", [768, E], BF16)
    agt_out = dint("agt_out", [8 * 768, E], BF16)
    ag1k_in = dint("ag1k_in", [2 * D, TOK], BF16)
    ag1k_out = dint("ag1k_out", [4 * 2 * D, TOK], BF16)
    ag1v_in = dint("ag1v_in", [TOK, D])
    ag1v_out = dint("ag1v_out", [4 * TOK, D])
    ag2_in = dint("ag2_in", [TOK, AG2C])
    ag2_out = dint("ag2_out", [4 * TOK, AG2C])
    cxh_dram = dint("cxh_dram", [D, TOK], BF16)
    cxl_dram = dint("cxl_dram", [D, TOK], BF16)
    rro_dram = [dint(f"rro{i}", [1, 512]) for i in range(2)]
    g_dram = dint("g_dram", [TOK, D])
    o_dram = dint("o_dram", [TOK, D])
    q_dram = dint("q_dram", [D, TOK])
    qs_dram = dint("qs_dram", [2 * D, TOK], BF16)
    G4 = [[0, 1, 2, 3], [4, 5, 6, 7]]
    G8 = [list(range(8))]

    with c.es:
        xres = c.sbuf("xres", [128, NT * D])
        hTh = c.sbuf("hTh", [128, DC * 512], BF16)
        hTl = c.sbuf("hTl", [128, DC * 512], BF16)
        ident = c.sbuf("ident", [128, 128])
        maskA = c.sbuf("maskA", [128, 128])
        ones_t = c.sbuf("ones_t", [128, 128])
        consts = c.sbuf("consts", [128, 24])
        (C_NEG_SA, C_EPS_LN, C_INV_D, C_SU, C_SG, C_SD, C_HALF, C_LNIF,
         C_EPS_RMS, C_INV_DV, C_NEG_SB, C_INV2PI, C_EPS_FFN, C_2PI_HI,
         C_2PI_LO, C_SCALEA, C_ONE, C_ZERO) = range(18)
        gnw_bc = c.sbuf("gnw_bc", [128, 128])
        bs2_sb = c.sbuf("bs2_sb", [128, 8])
        gate_sb = c.sbuf("gate_sb", [128, 32])
        q0_bc = c.sbuf("q0_bc", [128, 1])
        meta_sb = c.sbuf("meta_sb", [1, 8])
        scrA = c.sbuf("scrA", [128, 2048])
        stat = c.sbuf("stat", [128, 8])
        WB = {}
        _wb_id = [0]

        from contextlib import contextmanager

        @contextmanager
        def wblock():
            i = _wb_id[0]; _wb_id[0] += 1
            with nc.sbuf_tensor(f"wbuf{i}", [128, 2 * 2048], F32) as a, \
                 nc.sbuf_tensor(f"wbh{i}", [128, 2 * 2048], BF16) as b, \
                 nc.sbuf_tensor(f"wbl{i}", [128, 2 * 2048], BF16) as d:
                WB['wbuf'], WB['wbh'], WB['wbl'] = a, b, d
                yield
        stg = c.sbuf("stg", [128, 1024])
        stg2 = c.sbuf("stg2", [128, 1024])

        pb = [c.es.enter_context(nc.psum_tensor(f"pb{i}", [128, 512], F32))
              for i in range(8)]

        for s in ('sd', 'gd', 'vd', 'ad', 't', 'v', 'a', 'g', 'sy', 'cc',
                  'xs0', 'xs1', 'xs2', 'xs3', 'xg0', 'xg1', 'xg2', 'xg3'):
            c.sem[s] = c.es.enter_context(nc.semaphore(s))
            c.cnt[s] = 0
        c.block = c.es.enter_context(nc.Block())
        ch = Chain(c)

        def g_loads(sy):
            r = [sy.dma_start(out=meta_sb[:], in_=IN['meta_f']),
                 sy.dma_start(out=bs2_sb[:],
                              in_=rawap(IN['bs2'], 0, [[1, 128], [128, 8]]),
                              allow_slow_non_contiguous=True),
                 sy.dma_start(out=gnw_bc[:],
                              in_=rawap(IN['gnw'], 0, [[0, 128], [1, 128]])),
                 sy.dma_start(out=gate_sb[:],
                              in_=rawap(IN['gate_own'], 0, [[0, 128], [1, 32]])),
                 sy.dma_start(out=q0_bc[:],
                              in_=rawap(IN['meta_f'], 0, [[0, 128], [1, 1]]))]
            for tt in range(NT):
                r.append(sy.dma_start(out=xres[:, tt * D:(tt + 1) * D],
                                      in_=IN['xin'][ts(tt), :]))
            return r
        ch.grp('sync', g_loads, dma=True)

        def g_const(v):
            vals = {C_NEG_SA: -SHIFT_A, C_EPS_LN: EPS_LN, C_INV_D: 1.0 / D,
                    C_HALF: math.pi / 2, C_LNIF: -math.log(ROPE_BASE) / 64.0,
                    C_EPS_RMS: EPS_RMS, C_INV_DV: 1.0 / DVB,
                    C_NEG_SB: -SHIFT_B, C_INV2PI: 0.5 / math.pi,
                    C_EPS_FFN: EPS_FFN,
                    C_2PI_HI: float(np.float32(2 * math.pi)),
                    C_2PI_LO: float(2 * math.pi - float(np.float32(2 * math.pi))),
                    C_SCALEA: 1.0 / math.sqrt(HDA), C_ONE: 1.0, C_ZERO: 0.0}
            r = [v.memset(consts[:, col:col + 1], val) for col, val in vals.items()]
            r.append(v.memset(ones_t[:], 1.0))
            return r
        ch.grp('vector', g_const)

        def g_ident(gp):
            return [gp.affine_select(ident[:], ones_t[:, 0:128],
                                     pattern=[[-1, 128]], compare_op=ALU.is_equal,
                                     fill=0.0, base=0, channel_multiplier=1),
                    gp.affine_select(maskA[:], ones_t[:, 0:128],
                                     pattern=[[1, 128]], compare_op=ALU.is_ge,
                                     fill=0.0, base=0, channel_multiplier=-1)]
        ch.grp('gpsimd', g_ident)

        def transpose_block(src_ap, dst_ap, bank=7):
            p, f = src_ap.shape

            def g_t(te):
                return [te.transpose(pb[bank][0:f, 0:p], src_ap, ident[:])]
            ch.grp('tensor', g_t)

            def g_v(v):
                return [v.tensor_copy(dst_ap, pb[bank][0:f, 0:p])]
            ch.grp('vector', g_v)

        def split_evac(src_psum, hi_ap, lo_ap):
            def g1(v):
                return [v.tensor_copy(hi_ap, src_psum)]
            ch.grp('vector', g1)

            def g2(v):
                return [v.tensor_tensor(lo_ap, src_psum, hi_ap, ALU.subtract)]
            ch.grp('vector', g2)

        def emit_ln(th):
            for q in range(4):
                tt = th * 4 + q
                xt = xres[:, tt * D:(tt + 1) * D]

                def g1(v, xt=xt):
                    return [v.reduce_sum(stat[:, 0:1], xt, AX)]
                ch.grp('vector', g1)

                def g2(s, xt=xt):
                    return [s.activation(scrA[:], xt, AF.Square,
                                         bias=consts[:, C_ZERO:C_ZERO + 1],
                                         scale=1.0, accum_out=stat[:, 1:2])]
                ch.grp('scalar', g2)

                def g3(v):
                    return [v.tensor_scalar_mul(stat[:, 2:3], stat[:, 0:1], 1.0 / D),
                            v.tensor_tensor(stat[:, 4:5], stat[:, 0:1],
                                            stat[:, 0:1], ALU.mult)]
                ch.grp('vector', g3)

                def g3b(v):
                    return [v.scalar_tensor_tensor(stat[:, 5:6], stat[:, 4:5],
                                                   -1.0 / D, stat[:, 1:2],
                                                   ALU.mult, ALU.add)]
                ch.grp('vector', g3b)

                def g4(s):
                    return [s.activation(stat[:, 7:8], stat[:, 5:6], AF.Sqrt,
                                         bias=consts[:, C_EPS_LN:C_EPS_LN + 1],
                                         scale=consts[:, C_INV_D:C_INV_D + 1])]
                ch.grp('scalar', g4)

                def g4b(v):
                    return [v.reciprocal(stat[:, 6:7], stat[:, 7:8])]
                ch.grp('vector', g4b)

                def g5(v, xt=xt):
                    return [v.tensor_scalar(scrA[:], xt, stat[:, 2:3],
                                            stat[:, 6:7], ALU.subtract, ALU.mult)]
                ch.grp('vector', g5)

                for dc in range(DC):
                    def g_t(te, dc=dc):
                        return [te.transpose(pb[6][:, 0:128],
                                             scrA[:, dc * 128:(dc + 1) * 128],
                                             ident[:])]
                    ch.grp('tensor', g_t)

                    split_evac(pb[6][:, 0:128],
                               hTh[:, dc * 512 + q * 128: dc * 512 + (q + 1) * 128],
                               hTl[:, dc * 512 + q * 128: dc * 512 + (q + 1) * 128])

        def load_w_block(cc, WT, oc, slot, extra=(), rows=D):
            ndc = rows // 128
            so = slot * 2048

            def g(sy):
                r = []
                for q in range(0, ndc, 4):
                    n = min(4, ndc - q)
                    r.append(sy.dma_start(
                        out=WB['wbuf'][:, so + q * 128: so + (q + n) * 128],
                        in_=rawap(WT, q * 128 * (WT.shape[1]) + oc * 128,
                                  [[WT.shape[1], 128],
                                   [128 * WT.shape[1], n], [1, 128]])))
                return r
            cc.grp('sync', g, dma=True, extra=extra)

            def g2a(v):
                return [v.tensor_copy(WB['wbh'][:, so:so + ndc * 128],
                                      WB['wbuf'][:, so:so + ndc * 128])]
            cc.grp('vector', g2a)

            def g2b(v):
                return [v.tensor_tensor(WB['wbl'][:, so:so + ndc * 128],
                                        WB['wbuf'][:, so:so + ndc * 128],
                                        WB['wbh'][:, so:so + ndc * 128],
                                        ALU.subtract)]
            return cc.grp('vector', g2b)

        def proj3(WT, out_cb, th, ndc=DC):
            chL = Chain(c, dsem={'sync': 'xs3'})
            chM = Chain(c)
            chL.last = ch.last
            chM.last = ch.last
            HH = {}
            HH['ld', 0] = load_w_block(chL, WT, 0, 0, rows=ndc * 128)
            for oc in range(DC):
                bank = oc % 2
                if oc + 1 < DC:
                    HH['ld', oc + 1] = load_w_block(
                        chL, WT, oc + 1, (oc + 1) % 2,
                        extra=(HH.get(('mm', oc - 1)),), rows=ndc * 128)

                def g_mm(te, oc=oc, bank=bank):
                    so = (oc % 2) * 2048
                    r = []
                    n3 = 3 * ndc
                    i = 0
                    for dc in range(ndc):
                        wh = WB['wbh'][:, so + dc * 128: so + (dc + 1) * 128]
                        wl = WB['wbl'][:, so + dc * 128: so + (dc + 1) * 128]
                        xh = hTh[:, dc * 512: dc * 512 + 512]
                        xl = hTl[:, dc * 512: dc * 512 + 512]
                        for lhs, rhs in ((wh, xh), (wl, xh), (wh, xl)):
                            r.append(te.matmul(pb[bank][:, 0:512], lhs, rhs,
                                               start=(i == 0),
                                               stop=(i == n3 - 1)))
                            i += 1
                    return r
                HH['mm', oc] = chM.grp('tensor', g_mm,
                                       extra=(HH['ld', oc],
                                              HH.get(('cb', oc - 2))))
                ch.pending.append(HH['mm', oc])
                out_cb(oc, th, bank)
                HH['cb', oc] = ch.last

        def add_psum_T_to_xres(bank, oc, tg):
            def g_e(v):
                return [v.tensor_copy(stg[:, 0:512], pb[bank][:, 0:512])]
            ch.grp('vector', g_e)
            for q in range(4):
                def g_t(te, q=q):
                    return [te.transpose(pb[6][:, 0:128],
                                         stg[:, q * 128:(q + 1) * 128], ident[:])]
                ch.grp('tensor', g_t)

                def g_a(v, q=q, oc=oc, tg=tg):
                    tt = tg * 4 + q
                    xt = xres[:, tt * D + oc * 128: tt * D + (oc + 1) * 128]
                    return [v.tensor_tensor(stg2[:, 0:128], xt, pb[6][:, 0:128],
                                            ALU.add)]
                ch.grp('vector', g_a)

                def g_a2(v, q=q, oc=oc, tg=tg):
                    tt = tg * 4 + q
                    xt = xres[:, tt * D + oc * 128: tt * D + (oc + 1) * 128]
                    return [v.tensor_copy(xt, stg2[:, 0:128])]
                ch.grp('vector', g_a2)

        def tap_xres(name):
            if name is None or name not in taps:
                return

            def g(gp):
                return [gp.dma_start(out=taps[name][ts(tt), :],
                                     in_=xres[:, tt * D:(tt + 1) * D])
                        for tt in range(NT)]
            ch.grp('gpsimd', g, dma=True)

        internals = {'msl': msl, 'ag1k_out': ag1k_out, 'ag1v_out': ag1v_out,
                     'qs_dram': qs_dram,
                     'agt_out': agt_out, 'ag2_out': ag2_out,
                     'g_dram': g_dram, 'o_dram': o_dram}
        dumps = {}
        for nm in DEBUG_DUMP:
            t = internals[nm]
            dumps[nm] = nc.dram_tensor("dump_" + nm, list(t.shape), t.dtype,
                                       kind="ExternalOutput").ap()

        def emit_dumps():
            for nm, dst in dumps.items():
                def g_d(gp, nm=nm, dst=dst):
                    return [gp.dma_start(out=dst, in_=internals[nm])]
                ch.grp('gpsimd', g_d, dma=True)

        def stop_here(name):
            if STOP_AFTER != name:
                return False
            emit_dumps()

            def g_out(gp):
                return [gp.dma_start(out=xout[ts(tt), :],
                                     in_=xres[:, tt * D:(tt + 1) * D])
                        for tt in range(NT)]
            ch.grp('gpsimd', g_out, dma=True)

            def g_fin(gp):
                return [gp.engine_nop()]
            ch.grp('gpsimd', g_fin)
            return True

        if stop_here("x0"):
            print("cnt:", dict(c.cnt))
            return nc

        with nc.sbuf_tensor("tw", [128, 4096], F32) as tw, \
             nc.sbuf_tensor("tth", [128, 4096], F32) as tth, \
             nc.sbuf_tensor("ttb", [128, 4096], BF16) as ttb, \
             nc.sbuf_tensor("tacc", [128, 16], F32) as tacc, \
             nc.sbuf_tensor("scl", [128, 12], F32) as scl:
            specs = [(IN['wupT_s'], 2, E), (IN['wgateT_s'], 2, E),
                     (IN['wdownT_s'], 8, D)]
            blocks = []
            for wi, (wt, nb, rl) in enumerate(specs):
                for rb in range(nb):
                    for ci in range(0, rl, 4096):
                        cw = min(4096, rl - ci)
                        blocks.append((wi, wt, rb, ci, cw))

            for bi, (wi, wt, rb, ci, cw) in enumerate(blocks):
                def g_l(sy, wt=wt, rb=rb, ci=ci, cw=cw):
                    return [sy.dma_start(out=tw[:, 0:cw],
                                         in_=wt[ts(rb), ci:ci + cw])]
                ch.grp('sync', g_l, dma=True)

                def g_r(v, bi=bi, cw=cw):
                    return [v.tensor_reduce(tacc[:, bi:bi + 1], tw[:, 0:cw], AX,
                                            ALU.add, apply_absolute_value=True)]
                ch.grp('vector', g_r)

            def g_ws(v):
                return [v.reduce_sum(scl[:, 9 + 0:10 + 0], tacc[:, 0:4], AX),
                        v.reduce_sum(scl[:, 9 + 1:10 + 1], tacc[:, 4:8], AX),
                        v.reduce_sum(scl[:, 9 + 2:10 + 2], tacc[:, 8:16], AX)]
            ch.grp('vector', g_ws)

            def g_cp(te):
                return [te.matmul(pb[0][0:1, wi:wi + 1], scl[:, 9 + wi:10 + wi],
                                  ones_t[:, 0:1], start=(wi == 0), stop=(wi == 2),
                                  skip_group_check=True) for wi in range(3)]
            ch.grp('tensor', g_cp)

            def g_ev0(v):
                return [v.memset(scl[0:1, 0:8], 0.0)]
            ch.grp('vector', g_ev0)

            def g_ev(v):
                return [v.tensor_copy(scl[0:1, 0:3], pb[0][0:1, 0:3])]
            ch.grp('vector', g_ev)

            def g_ar_in(gp):
                return [gp.dma_start(out=ar_in, in_=scl[0:1, 0:8])]
            ch.grp('gpsimd', g_ar_in, dma=True)

            def g_ar(gp):
                return [gp.collective_compute("AllReduce", ALU.add,
                                              replica_groups=G8,
                                              ins=[ar_in], outs=[ar_out])]
            ch.grp('gpsimd', g_ar, collective=True)

            def g_bc(gp):
                return [gp.dma_start(out=scl[:, wi:wi + 1],
                                     in_=rawap(ar_out, wi, [[0, 128], [1, 1]]))
                        for wi in range(3)]
            ch.grp('gpsimd', g_bc, dma=True)

            def g_scalesA(v):
                return [v.tensor_scalar(consts[:, (C_SU, C_SG, C_SD)[wi]:
                                                (C_SU, C_SG, C_SD)[wi] + 1],
                                        scl[:, wi:wi + 1], 1.0 / (D * E),
                                        EPS_FFN, ALU.mult, ALU.add)
                        for wi in range(3)]
            ch.grp('vector', g_scalesA)

            def g_scalesB(v):
                return [v.tensor_scalar_mul(scl[:, 4 + wi:5 + wi],
                                            consts[:, (C_SU, C_SG, C_SD)[wi]:
                                                   (C_SU, C_SG, C_SD)[wi] + 1],
                                            0.5) for wi in range(3)]
            ch.grp('vector', g_scalesB)

            def g_scalesC(v):
                return [v.tensor_scalar_mul(scl[:, 8 + wi:9 + wi],
                                            scl[:, 4 + wi:5 + wi], -1.0)
                        for wi in range(3)]
            ch.grp('vector', g_scalesC)

            for (wi, wt, rb, ci, cw) in blocks:
                def g_l(sy, wt=wt, rb=rb, ci=ci, cw=cw):
                    return [sy.dma_start(out=tw[:, 0:cw],
                                         in_=wt[ts(rb), ci:ci + cw])]
                ch.grp('sync', g_l, dma=True)

                def g_t1(v, wi=wi, cw=cw):
                    nthr = scl[:, 8 + wi:9 + wi]
                    return [v.tensor_single_scalar(tth[:, 0:cw], tw[:, 0:cw],
                                                   nthr, ALU.is_lt)]
                ch.grp('vector', g_t1)

                def g_t2(v, wi=wi, cw=cw):
                    thr = scl[:, 4 + wi:5 + wi]
                    return [v.scalar_tensor_tensor(ttb[:, 0:cw], tw[:, 0:cw],
                                                   thr, tth[:, 0:cw],
                                                   ALU.is_gt, ALU.subtract)]
                ch.grp('vector', g_t2)

                def g_s(gp, wi=wi, rb=rb, ci=ci, cw=cw):
                    if wi < 2:
                        row0 = wi * 256 + rb * 128
                        return [gp.dma_start(out=agt_in[row0:row0 + 128,
                                                        ci:ci + cw],
                                             in_=ttb[:, 0:cw])]
                    row0 = 512 + rb * 32
                    return [gp.dma_start(
                        out=rawap(agt_in, row0 * E, [[2048, 128], [1, 2048]]),
                        in_=ttb[:, 0:cw])]
                ch.grp('gpsimd', g_s, dma=True)

            def g_agt(gp):
                return [gp.collective_compute("AllGather", ALU.bypass,
                                              replica_groups=G8,
                                              ins=[agt_in], outs=[agt_out])]
            ch.grp('gpsimd', g_agt, collective=True)

        if stop_here("tern"):
            print("cnt:", dict(c.cnt))
            return nc

        with nc.sbuf_tensor("rp", [128, 64], F32) as rp, \
             nc.sbuf_tensor("mcol", [128, 2], F32) as mcol:
            for i in range(MS_LEN // 128):
                def g_l(sy, i=i):
                    return [sy.dma_start(out=rp[:],
                                         in_=IN['rel_ms'][ts(i), :])]
                ch.grp('sync', g_l, dma=True)

                def g_r(v):
                    return [v.reduce_sum(mcol[:, 0:1], rp[:], AX)]
                ch.grp('vector', g_r)

                def g_r2(v):
                    return [v.tensor_scalar_mul(mcol[:, 1:2], mcol[:, 0:1],
                                                1.0 / 64.0)]
                ch.grp('vector', g_r2)

                def g_st(gp, i=i):
                    return [gp.dma_start(out=msl[0:1, i * 128:(i + 1) * 128],
                                         in_=mcol[:, 1:2])]
                ch.grp('gpsimd', g_st, dma=True)

        if stop_here("mext"):
            print("cnt:", dict(c.cnt))
            return nc

        def q_cb(oc, th, bank):
            split_evac(pb[bank][:, 0:512], stgh[:, 0:512], stgl[:, 0:512])

            def g(gp, oc=oc, th=th):
                return [gp.dma_start(out=qs_dram[oc * 128:(oc + 1) * 128,
                                                 th * 512:(th + 1) * 512],
                                     in_=stgh[:, 0:512]),
                        gp.dma_start(out=qs_dram[D + oc * 128: D + (oc + 1) * 128,
                                                 th * 512:(th + 1) * 512],
                                     in_=stgl[:, 0:512])]
            ch.grp('gpsimd', g, dma=True)

        def k_cb(oc, th, bank):
            split_evac(pb[bank][:, 0:512], stgh[:, 0:512], stgl[:, 0:512])

            def g(gp, oc=oc, th=th):
                return [gp.dma_start(out=ag1k_in[oc * 128:(oc + 1) * 128,
                                                 th * 512:(th + 1) * 512],
                                     in_=stgh[:, 0:512]),
                        gp.dma_start(out=ag1k_in[D + oc * 128: D + (oc + 1) * 128,
                                                 th * 512:(th + 1) * 512],
                                     in_=stgl[:, 0:512])]
            ch.grp('gpsimd', g, dma=True)

        def v_cb(oc, th, bank):
            def g_e(v):
                return [v.tensor_copy(stg[:, 0:512], pb[bank][:, 0:512])]
            ch.grp('vector', g_e)
            for q in range(4):
                transpose_block(stg[:, q * 128:(q + 1) * 128],
                                stg2[:, q * 128:(q + 1) * 128], bank=6)

            def g(gp, oc=oc, th=th):
                return [gp.dma_start(
                    out=ag1v_in[th * 512 + q * 128: th * 512 + (q + 1) * 128,
                                oc * 128:(oc + 1) * 128],
                    in_=stg2[:, q * 128:(q + 1) * 128]) for q in range(4)]
            ch.grp('gpsimd', g, dma=True)

        with wblock(), \
             nc.sbuf_tensor("stgh", [128, 1024], BF16) as stgh, \
             nc.sbuf_tensor("stgl", [128, 1024], BF16) as stgl:
            for th in range(2):
                emit_ln(th)
                proj3(IN['wqT_a'], q_cb, th)
                proj3(IN['wkT_a'], k_cb, th)
                proj3(IN['wvT_a'], v_cb, th)

        def g_ag1a(gp):
            return [gp.collective_compute("AllGather", ALU.bypass,
                                          replica_groups=G4,
                                          ins=[ag1k_in], outs=[ag1k_out])]
        ch.grp('gpsimd', g_ag1a, collective=True)

        def g_ag1b(gp):
            return [gp.collective_compute("AllGather", ALU.bypass,
                                          replica_groups=G4,
                                          ins=[ag1v_in], outs=[ag1v_out])]
        ch.grp('gpsimd', g_ag1b, collective=True)

        if stop_here("p1"):
            print("cnt:", dict(c.cnt))
            return nc

        CHA = 2
        with ExitStack() as st2:
            def sb2(name, shape, dt=F32):
                return st2.enter_context(nc.sbuf_tensor(name, shape, dt))

            class _NS2:
                pass
            bufs2 = []
            for ci in range(CHA):
                B = _NS2()
                B.kt = sb2(f"kt{ci}", [64, 2 * SEQ], BF16)
                B.va = sb2(f"va{ci}", [128, KCH * 65])
                B.qp = sb2(f"qp{ci}", [64, 2048], BF16)
                B.btl = sb2(f"btl{ci}", [128, 1024])
                B.sT = sb2(f"sT{ci}", [128, 1024])
                B.pT = sb2(f"pT{ci}", [128, 1024])
                B.cxh = sb2(f"cxh{ci}", [64, 512], BF16)
                B.cxl = sb2(f"cxl{ci}", [64, 512], BF16)
                B.pbS = (pb[ci * 4], pb[ci * 4 + 1])
                B.pbC = (pb[ci * 4 + 2], pb[ci * 4 + 3])
                bufs2.append(B)

            chains2 = [Chain(c, dsem={'sync': f'xs{i}', 'gpsimd': f'xg{i}'})
                       for i in range(CHA)]
            for c2 in chains2:
                c2.last = ch.last

            def gen_head_a(ci, h):
                B = bufs2[ci]
                chi = chains2[ci]

                def g_kt(gp, h=h):
                    r = [gp.dma_start(
                            out=B.qp[:, th * 1024: th * 1024 + 512],
                            in_=qs_dram[h * 64: h * 64 + 64,
                                        th * 512:(th + 1) * 512])
                         for th in range(2)]
                    r += [gp.dma_start(
                            out=B.qp[:, th * 1024 + 512: th * 1024 + 1024],
                            in_=qs_dram[D + h * 64: D + h * 64 + 64,
                                        th * 512:(th + 1) * 512])
                          for th in range(2)]
                    for rr in range(4):
                        r.append(gp.dma_start(
                            out=B.kt[:, rr * 1024:(rr + 1) * 1024],
                            in_=ag1k_out[rr * 2 * D + h * 64:
                                         rr * 2 * D + h * 64 + 64, :]))
                        r.append(gp.dma_start(
                            out=B.kt[:, SEQ + rr * 1024: SEQ + (rr + 1) * 1024],
                            in_=ag1k_out[rr * 2 * D + D + h * 64:
                                         rr * 2 * D + D + h * 64 + 64, :]))
                    return r
                chi.grp('gpsimd', g_kt, dma=True); yield

                def g_vo(v):
                    return [v.memset(B.va[:], 1.0)]
                chi.grp('vector', g_vo); yield

                def g_vl(sy, h=h):
                    return [sy.dma_start(
                        out=B.va[:, kc * 65: kc * 65 + 64],
                        in_=ag1v_out[kc * 128:(kc + 1) * 128,
                                     h * 64: h * 64 + 64])
                        for kc in range(KCH)]
                chi.grp('sync', g_vl, dma=True); yield

                for kc in range(KCH):
                    def g_b(sy, kc=kc):
                        return [sy.dma_start(
                            out=B.btl[:],
                            in_=rawap(msl, kc * 128, [[1, 128], [1, 1024]]))]
                    chi.grp('sync', g_b, dma=True); yield

                    def g_sc(te, kc=kc):
                        r = []
                        for tg in range(2):
                            i = 0
                            for (koff, qoff) in ((0, 0), (0, 512), (SEQ, 0)):
                                lhs = B.kt[:, koff + kc * 128:
                                           koff + (kc + 1) * 128]
                                rhs_ap = B.qp[:, tg * 1024 + qoff:
                                              tg * 1024 + qoff + 512]
                                r.append(te.matmul(
                                    B.pbS[tg][:, 0:512], lhs, rhs_ap,
                                    start=(i == 0), stop=(i == 2),
                                    skip_group_check=True))
                                i += 1
                        return r
                    chi.grp('tensor', g_sc); yield

                    def g_ev(v, kc=kc):
                        r = []
                        for tg in range(2):
                            bap = B.btl.ap()
                            rev = bass.AP(tensor=bap.tensor,
                                          offset=bap.offset + (1023 - tg * 512),
                                          ap=[[bap.ap[0][0], 128], [-1, 512]])
                            r.append(v.scalar_tensor_tensor(
                                B.sT[:, tg * 512:(tg + 1) * 512],
                                B.pbS[tg][:, 0:512], 1.0 / math.sqrt(HDA),
                                rev, ALU.mult, ALU.add))
                        return r
                    chi.grp('vector', g_ev); yield

                    def g_ex(s):
                        return [s.activation(B.pT[:], B.sT[:], AF.Exp,
                                             bias=consts[:, C_NEG_SA:C_NEG_SA + 1],
                                             scale=1.0)]
                    chi.grp('scalar', g_ex); yield

                    def g_cx(te, kc=kc):
                        return [te.matmul(
                            B.pbC[tg][0:65, 0:512],
                            B.va[:, kc * 65:(kc + 1) * 65],
                            B.pT[:, tg * 512:(tg + 1) * 512],
                            start=(kc == 0), stop=(kc == KCH - 1),
                            skip_group_check=True) for tg in range(2)]
                    chi.grp('tensor', g_cx); yield

                for tg in range(2):
                    def g_n1(v, tg=tg):
                        return [v.reciprocal(B.sT[0:1, 512:1024],
                                             B.pbC[tg][64:65, 0:512])]
                    chi.grp('vector', g_n1); yield

                    def g_n2a(gp, ci=ci):
                        return [gp.dma_start(out=rro_dram[ci],
                                             in_=B.sT[0:1, 512:1024])]
                    chi.grp('gpsimd', g_n2a, dma=True); yield

                    def g_n2(gp, ci=ci):
                        return [gp.dma_start(
                            out=B.pT[0:64, 512:1024],
                            in_=rawap(rro_dram[ci], 0, [[0, 64], [1, 512]]))]
                    chi.grp('gpsimd', g_n2, dma=True); yield

                    def g_n3(v, tg=tg):
                        return [v.tensor_tensor(B.pT[0:64, 0:512],
                                                B.pbC[tg][0:64, 0:512],
                                                B.pT[0:64, 512:1024],
                                                ALU.mult)]
                    chi.grp('vector', g_n3); yield

                    def g_n4(v):
                        return [v.tensor_copy(B.cxh[:], B.pT[0:64, 0:512])]
                    chi.grp('vector', g_n4); yield

                    def g_n5(v):
                        return [v.tensor_tensor(B.cxl[:], B.pT[0:64, 0:512],
                                                B.cxh[:], ALU.subtract)]
                    chi.grp('vector', g_n5); yield

                    def g_n6(gp, tg=tg, h=h):
                        return [gp.dma_start(
                            out=cxh_dram[h * 64: h * 64 + 64,
                                         tg * 512:(tg + 1) * 512],
                            in_=B.cxh[:]),
                            gp.dma_start(
                            out=cxl_dram[h * 64: h * 64 + 64,
                                         tg * 512:(tg + 1) * 512],
                            in_=B.cxl[:])]
                    chi.grp('gpsimd', g_n6, dma=True); yield

            gens = []
            for ci in range(CHA):
                def mk(ci=ci):
                    for h in range(ci, HA, CHA):
                        yield from gen_head_a(ci, h)
                gens.append(mk())
            drive(gens)

            def g_bar2(v):
                return [v.memset(stg2[:, 0:1], 0.0)]
            ch.grp('vector', g_bar2, extra=[c2.last for c2 in chains2])

        def wo_cb(oc, th, bank):
            add_psum_T_to_xres(bank, oc, th)

        with wblock():
            for th in range(2):
                def g_lh(sy, th=th):
                    return [sy.dma_start(
                        out=hTh[:, dc * 512:(dc + 1) * 512],
                        in_=cxh_dram[dc * 128:(dc + 1) * 128,
                                     th * 512:(th + 1) * 512])
                        for dc in range(DC)]
                ch.grp('sync', g_lh, dma=True)

                def g_ll(sy, th=th):
                    return [sy.dma_start(
                        out=hTl[:, dc * 512:(dc + 1) * 512],
                        in_=cxl_dram[dc * 128:(dc + 1) * 128,
                                     th * 512:(th + 1) * 512])
                        for dc in range(DC)]
                ch.grp('sync', g_ll, dma=True)
                proj3(IN['woT_a'], wo_cb, th)
        tap_xres("x1")


        if STOP_AFTER == "x1":
            emit_dumps()

            def g_out(gp):
                return [gp.dma_start(out=xout[ts(tt), :],
                                     in_=xres[:, tt * D:(tt + 1) * D])
                        for tt in range(NT)]
            ch.grp('gpsimd', g_out, dma=True)

            def g_fin(gp):
                return [gp.engine_nop()]
            ch.grp('gpsimd', g_fin)
            return nc

        _ffn_id = [0]

        def emit_ffn(tapname):
            sfx = str(_ffn_id[0]); _ffn_id[0] += 1
            with nc.sbuf_tensor("tABu" + sfx, [128, 2 * 2048], BF16) as tABu, \
                 nc.sbuf_tensor("tABg" + sfx, [128, 2 * 2048], BF16) as tABg, \
                 nc.sbuf_tensor("tdn" + sfx, [128, 4 * 2048], BF16) as tdn, \
                 nc.sbuf_tensor("hidh" + sfx, [128, 2 * 4 * 512], BF16) as hidh, \
                 nc.sbuf_tensor("hidl" + sfx, [128, 2 * 4 * 512], BF16) as hidl, \
                 nc.sbuf_tensor("dacc" + sfx, [128, 16 * 512], F32) as dacc:
                for th in range(2):
                    emit_ln(th)

                    def g_z(v):
                        return [v.memset(dacc[:], 0.0)]
                    ch.grp('vector', g_z)

                    H = {}
                    chW = Chain(c, dsem={'sync': 'xs0'})
                    chD = Chain(c, dsem={'gpsimd': 'xg0'})
                    chP = Chain(c); chE = Chain(c)
                    for cc_ in (chW, chD, chP, chE):
                        cc_.last = ch.last

                    def wait_h(keys):
                        while any(k in k2valid and k not in H for k in keys):
                            yield

                    k2valid = set()

                    def ex(*keys):
                        return [H[k] for k in keys if k in H]

                    def gen_W():
                        for ec in range(64):
                            sl = (ec % 2) * 2048
                            yield from wait_h([('sil', ec - 2)])

                            def g_lu(sy, ec=ec, sl=sl):
                                r = []
                                for dc in range(DC):
                                    row0 = (dc // 2) * 768 + (dc % 2) * 128
                                    r.append(sy.dma_start(
                                        out=tABu[:, sl + dc * 128:
                                                 sl + (dc + 1) * 128],
                                        in_=rawap(agt_out, row0 * E + ec * 128,
                                                  [[E, 128], [1, 128]])))
                                return r
                            H['wu', ec] = chW.grp('sync', g_lu, dma=True,
                                                  extra=ex(('sil', ec - 2)))
                            yield
                            yield from wait_h([('h32', ec - 2)])

                            def g_lg(sy, ec=ec, sl=sl):
                                r = []
                                for dc in range(DC):
                                    row0 = (dc // 2) * 768 + 256 + (dc % 2) * 128
                                    r.append(sy.dma_start(
                                        out=tABg[:, sl + dc * 128:
                                                 sl + (dc + 1) * 128],
                                        in_=rawap(agt_out, row0 * E + ec * 128,
                                                  [[E, 128], [1, 128]])))
                                return r
                            H['wg', ec] = chW.grp('sync', g_lg, dma=True,
                                                  extra=ex(('h32', ec - 2)))
                            yield

                    def gen_D():
                        for ecg in range(16):
                            for el in range(4):
                                ec = ecg * 4 + el
                                yield from wait_h([('pd', (ecg - 1) * 16 + 15)])

                                def g_ld(gp, ec=ec, el=el):
                                    row0 = (ec // 8) * 768 + 512 + (ec % 8) * 32
                                    return [gp.dma_start(
                                        out=tdn[:, el * 2048:(el + 1) * 2048],
                                        in_=rawap(agt_out, row0 * E,
                                                  [[2048, 128], [1, 2048]]))]
                                H['dn', ecg, el] = chD.grp(
                                    'gpsimd', g_ld, dma=True,
                                    extra=ex(('pd', (ecg - 1) * 16 + 15)))
                                yield

                    def gen_P():
                        for ecg in range(16):
                            hoff = (ecg % 2) * 2048
                            for el in range(4):
                                ec = ecg * 4 + el
                                bu = ec % 2
                                bg = 2 + ec % 2
                                yield from wait_h([('wu', ec), ('sil', ec - 2)])

                                def g_up(te, bu=bu, ec=ec):
                                    sl = (ec % 2) * 2048
                                    r = []
                                    i = 0
                                    for dc in range(DC):
                                        lhs = tABu[:, sl + dc * 128:
                                                   sl + (dc + 1) * 128]
                                        for rhs in (hTh[:, dc * 512:(dc + 1) * 512],
                                                    hTl[:, dc * 512:(dc + 1) * 512]):
                                            r.append(te.matmul(pb[bu][:, 0:512],
                                                               lhs, rhs,
                                                               start=(i == 0),
                                                               stop=(i == 31)))
                                            i += 1
                                    return r
                                H['pu', ec] = chP.grp(
                                    'tensor', g_up,
                                    extra=ex(('wu', ec), ('sil', ec - 2)))
                                yield
                                yield from wait_h([('wg', ec), ('h32', ec - 2)])

                                def g_gt(te, bg=bg, ec=ec):
                                    sl = (ec % 2) * 2048
                                    r = []
                                    i = 0
                                    for dc in range(DC):
                                        lhs = tABg[:, sl + dc * 128:
                                                   sl + (dc + 1) * 128]
                                        for rhs in (hTh[:, dc * 512:(dc + 1) * 512],
                                                    hTl[:, dc * 512:(dc + 1) * 512]):
                                            r.append(te.matmul(pb[bg][:, 0:512],
                                                               lhs, rhs,
                                                               start=(i == 0),
                                                               stop=(i == 31)))
                                            i += 1
                                    return r
                                H['pg', ec] = chP.grp(
                                    'tensor', g_gt,
                                    extra=ex(('wg', ec), ('h32', ec - 2)))
                                yield
                            for dc in range(DC):
                                kk = ecg * 16 + dc
                                bd = 4 + dc % 2
                                deps = [('dn', ecg, 3), ('hl', ecg * 4 + 3),
                                        ('ac1', kk - 2)]
                                yield from wait_h(deps)

                                def g_dn(te, dc=dc, bd=bd, hoff=hoff):
                                    r = []
                                    i = 0
                                    for el in range(4):
                                        lhs = tdn[:, el * 2048 + dc * 128:
                                                  el * 2048 + (dc + 1) * 128]
                                        for rhs in (hidh[:, hoff + el * 512:
                                                         hoff + (el + 1) * 512],
                                                    hidl[:, hoff + el * 512:
                                                         hoff + (el + 1) * 512]):
                                            r.append(te.matmul(pb[bd][:, 0:512],
                                                               lhs, rhs,
                                                               start=(i == 0),
                                                               stop=(i == 7)))
                                            i += 1
                                    return r
                                H['pd', kk] = chP.grp('tensor', g_dn,
                                                      extra=ex(*deps))
                                yield

                    def gen_E():
                        for ecg in range(16):
                            hoff = (ecg % 2) * 2048
                            for el in range(4):
                                ec = ecg * 4 + el
                                bu = ec % 2
                                bg = 2 + ec % 2
                                yield from wait_h([('pu', ec)])

                                def g_sil(s, bu=bu):
                                    return [s.activation(
                                        stg[:, 0:512], pb[bu][:, 0:512], AF.Silu,
                                        bias=consts[:, C_ZERO:C_ZERO + 1],
                                        scale=consts[:, C_SU:C_SU + 1])]
                                H['sil', ec] = chE.grp('scalar', g_sil,
                                                       extra=ex(('pu', ec)))
                                yield
                                yield from wait_h([('pg', ec)])

                                def g_h32(v, bg=bg):
                                    return [v.scalar_tensor_tensor(
                                        stg2[:, 0:512], pb[bg][:, 0:512],
                                        consts[:, C_SG:C_SG + 1], stg[:, 0:512],
                                        ALU.mult, ALU.mult)]
                                H['h32', ec] = chE.grp('vector', g_h32,
                                                       extra=ex(('pg', ec)))
                                yield
                                yield from wait_h([('pd', (ecg - 2) * 16 + 15)])

                                def g_hh(v, el=el, hoff=hoff):
                                    return [v.tensor_copy(
                                        hidh[:, hoff + el * 512:
                                             hoff + (el + 1) * 512],
                                        stg2[:, 0:512])]
                                chE.grp('vector', g_hh,
                                        extra=ex(('pd', (ecg - 2) * 16 + 15)))
                                yield

                                def g_hl(v, el=el, hoff=hoff):
                                    return [v.tensor_tensor(
                                        hidl[:, hoff + el * 512:
                                             hoff + (el + 1) * 512],
                                        stg2[:, 0:512],
                                        hidh[:, hoff + el * 512:
                                             hoff + (el + 1) * 512],
                                        ALU.subtract)]
                                H['hl', ec] = chE.grp('vector', g_hl)
                                yield
                            for dc in range(DC):
                                kk = ecg * 16 + dc
                                bd = 4 + dc % 2
                                yield from wait_h([('pd', kk)])

                                def g_ac1(v, dc=dc, bd=bd):
                                    return [v.scalar_tensor_tensor(
                                        scrA[:, 0:512], pb[bd][:, 0:512],
                                        consts[:, C_SD:C_SD + 1],
                                        dacc[:, dc * 512:(dc + 1) * 512],
                                        ALU.mult, ALU.add)]
                                H['ac1', kk] = chE.grp('vector', g_ac1,
                                                       extra=ex(('pd', kk)))
                                yield

                                def g_ac2(v, dc=dc):
                                    return [v.tensor_copy(
                                        dacc[:, dc * 512:(dc + 1) * 512],
                                        scrA[:, 0:512])]
                                chE.grp('vector', g_ac2)
                                yield

                    for ec in range(64):
                        k2valid.add(('wu', ec)); k2valid.add(('wg', ec))
                        k2valid.add(('pu', ec)); k2valid.add(('pg', ec))
                        k2valid.add(('sil', ec)); k2valid.add(('h32', ec))
                        k2valid.add(('hl', ec))
                    for ecg in range(16):
                        for el in range(4):
                            k2valid.add(('dn', ecg, el))
                        for dc in range(DC):
                            k2valid.add(('pd', ecg * 16 + dc))
                            k2valid.add(('ac1', ecg * 16 + dc))

                    drive([gen_W(), gen_D(), gen_P(), gen_E()])

                    def g_barf(v):
                        return [v.memset(stg2[:, 0:1], 0.0)]
                    ch.grp('vector', g_barf,
                           extra=[chW.last, chD.last, chP.last, chE.last])

                    for dc in range(DC):
                        for q in range(4):
                            def g_t(te, dc=dc, q=q):
                                return [te.transpose(
                                    pb[6][:, 0:128],
                                    dacc[:, dc * 512 + q * 128: dc * 512 + (q + 1) * 128],
                                    ident[:])]
                            ch.grp('tensor', g_t)

                            def g_a(v, q=q, dc=dc, th=th):
                                tt = th * 4 + q
                                xt = xres[:, tt * D + dc * 128: tt * D + (dc + 1) * 128]
                                return [v.tensor_tensor(stg2[:, 0:128], xt,
                                                        pb[6][:, 0:128], ALU.add)]
                            ch.grp('vector', g_a)

                            def g_a2(v, q=q, dc=dc, th=th):
                                tt = th * 4 + q
                                xt = xres[:, tt * D + dc * 128: tt * D + (dc + 1) * 128]
                                return [v.tensor_copy(xt, stg2[:, 0:128])]
                            ch.grp('vector', g_a2)
            tap_xres(tapname)

        emit_ffn("x2")
        if stop_here("x2"):
            print("cnt:", dict(c.cnt))
            return nc

        with ExitStack() as st5:
            def sb5(name, shape, dt=F32):
                return st5.enter_context(nc.sbuf_tensor(name, shape, dt))
            ropeC = sb5("ropeC", [128, 1024])
            ropeSs = sb5("ropeSs", [128, 1024])
            ropeCq = sb5("ropeCq", [128, 1024])
            ropeSq = sb5("ropeSq", [128, 1024])
            rsc = sb5("rsc", [128, 1024])
            rsc2 = sb5("rsc2", [128, 1024])
            ri32 = sb5("ri32", [128, 1024], I32)
            rswp = sb5("rswp", [128, 512])
            w1b = sb5("w1b", [128, 256])
            w1h = sb5("w1h", [128, 256], BF16)
            w1l = sb5("w1l", [128, 256], BF16)
            w2b = sb5("w2b", [16, 1024])
            w2h = sb5("w2h", [16, 1024], BF16)
            w2l = sb5("w2l", [16, 1024], BF16)
            usb = sb5("usb", [16, 512])
            uh = sb5("uh", [16, 512], BF16)
            ul = sb5("ul", [16, 512], BF16)
            def g_i1(gp):
                return [gp.iota(rsc[:], pattern=[[1, 1024]], base=0,
                                channel_multiplier=0,
                                allow_small_or_imprecise_dtypes=True),
                        gp.iota(stat[0:64, 7:8], pattern=[[1, 1]], base=0,
                                channel_multiplier=1,
                                allow_small_or_imprecise_dtypes=True),
                        gp.iota(stat[64:128, 7:8], pattern=[[1, 1]], base=0,
                                channel_multiplier=1,
                                allow_small_or_imprecise_dtypes=True)]
            ch.grp('gpsimd', g_i1)

            def g_i2(v):
                return [v.tensor_scalar_add(rsc2[:], rsc[:], q0_bc[:, 0:1])]
            ch.grp('vector', g_i2)

            def g_i3(s):
                return [s.activation(stat[:, 6:7], stat[:, 7:8], AF.Exp,
                                     bias=consts[:, C_ZERO:C_ZERO + 1],
                                     scale=consts[:, C_LNIF:C_LNIF + 1])]
            ch.grp('scalar', g_i3)

            def g_i4(v):
                return [v.tensor_scalar(rsc[:], rsc2[:], stat[:, 6:7], None,
                                        ALU.mult)]
            ch.grp('vector', g_i4)

            def emit_sincos(src_ap, dst_ap, add_half):
                if add_half:
                    def g_a(v):
                        return [v.tensor_scalar_add(rsc2[:], src_ap, math.pi / 2)]
                    ch.grp('vector', g_a)
                    base = rsc2
                else:
                    def g_a(v):
                        return [v.tensor_copy(rsc2[:], src_ap)]
                    ch.grp('vector', g_a)
                    base = rsc2

                def g_k(v):
                    return [v.tensor_scalar_mul(stg[:, 0:1024], base[:],
                                                0.5 / math.pi)]
                ch.grp('vector', g_k)

                def g_ki(v):
                    return [v.tensor_copy(ri32[:], stg[:, 0:1024])]
                ch.grp('vector', g_ki)

                def g_kf(v):
                    return [v.tensor_copy(stg[:, 0:1024], ri32[:])]
                ch.grp('vector', g_kf)

                hi = float(np.float32(2 * math.pi))
                lo = float(2 * math.pi - float(np.float32(2 * math.pi)))

                def g_y1(v):
                    return [v.scalar_tensor_tensor(stg2[:, 0:1024], stg[:, 0:1024],
                                                   -hi, base[:], ALU.mult, ALU.add)]
                ch.grp('vector', g_y1)

                def g_y2(v):
                    return [v.scalar_tensor_tensor(rsc2[:], stg[:, 0:1024],
                                                   -lo, stg2[:, 0:1024],
                                                   ALU.mult, ALU.add)]
                ch.grp('vector', g_y2)

                def g_cl(v):
                    return [v.tensor_scalar(stg[:, 0:1024], rsc2[:],
                                            math.pi, -math.pi, ALU.min, ALU.max)]
                ch.grp('vector', g_cl)

                def g_s(s):
                    return [s.activation(dst_ap, stg[:, 0:1024], AF.Sin,
                                         bias=consts[:, C_ZERO:C_ZERO + 1],
                                         scale=1.0)]
                ch.grp('scalar', g_s)

            emit_sincos(rsc[:], ropeSs[:], False)
            emit_sincos(rsc[:], ropeC[:], True)

            def g_sf(v):
                return [v.tensor_scalar_mul(ropeSq[0:64, :], ropeSs[0:64, :], -1.0)]
            ch.grp('vector', g_sf)

            def g_sf2(v):
                return [v.tensor_copy(ropeSq[64:128, :], ropeSs[64:128, :])]
            ch.grp('vector', g_sf2)

            def g_sf3(v):
                return [v.tensor_copy(ropeSs[:], ropeSq[:])]
            ch.grp('vector', g_sf3)

            def g_sc1(v):
                return [v.tensor_scalar_mul(ropeCq[:], ropeC[:],
                                            1.0 / math.sqrt(DKB))]
            ch.grp('vector', g_sc1)

            def g_sc2(v):
                return [v.tensor_scalar_mul(ropeSq[:], ropeSs[:],
                                            1.0 / math.sqrt(DKB))]
            ch.grp('vector', g_sc2)

            def g_w1(sy):
                return [sy.dma_start(out=w1b[:, dc * 16:(dc + 1) * 16],
                                     in_=IN['Ws1T'][ts(dc), :])
                        for dc in range(DC)]
            ch.grp('sync', g_w1, dma=True)

            def g_w1h(v):
                return [v.tensor_copy(w1h[:], w1b[:])]
            ch.grp('vector', g_w1h)

            def g_w1l(v):
                return [v.tensor_tensor(w1l[:], w1b[:], w1h[:], ALU.subtract)]
            ch.grp('vector', g_w1l)

            def g_w2(sy):
                return [sy.dma_start(out=w2b[:], in_=IN['Ws2T'])]
            ch.grp('sync', g_w2, dma=True)

            def g_w2h(v):
                return [v.tensor_copy(w2h[:], w2b[:])]
            ch.grp('vector', g_w2h)

            def g_w2l(v):
                return [v.tensor_tensor(w2l[:], w2b[:], w2h[:], ALU.subtract)]
            ch.grp('vector', g_w2l)

            def rope_apply(bank, th, ctab, stab, dst):
                def g_e(v):
                    return [v.tensor_copy(stg[:, 0:512], pb[bank][:, 0:512])]
                ch.grp('vector', g_e)

                def g_sw(gp):
                    return [gp.dma_start(out=rswp[0:64, :], in_=stg[64:128, 0:512]),
                            gp.dma_start(out=rswp[64:128, :], in_=stg[0:64, 0:512])]
                ch.grp('gpsimd', g_sw, dma=True)

                def g_m1(v, th=th):
                    return [v.tensor_tensor(scrA[:, 0:512], stg[:, 0:512],
                                            ctab[:, th * 512:(th + 1) * 512],
                                            ALU.mult)]
                ch.grp('vector', g_m1)

                def g_m2(v, th=th):
                    return [v.tensor_tensor(scrA[:, 512:1024], rswp[:],
                                            stab[:, th * 512:(th + 1) * 512],
                                            ALU.mult)]
                ch.grp('vector', g_m2)

                def g_m3(v):
                    return [v.tensor_tensor(dst, scrA[:, 0:512],
                                            scrA[:, 512:1024], ALU.add)]
                ch.grp('vector', g_m3)

            def store_tok(src, th, colbase, oc):
                for q in range(4):
                    def g_t(te, q=q):
                        return [te.transpose(pb[6][:, 0:128],
                                             src[:, q * 128:(q + 1) * 128],
                                             ident[:])]
                    ch.grp('tensor', g_t)

                    def g_v(v):
                        return [v.tensor_copy(stg[:, 0:128], pb[6][:, 0:128])]
                    ch.grp('vector', g_v)

                    def g_d(gp, q=q, th=th, oc=oc, colbase=colbase):
                        return [gp.dma_start(
                            out=ag2_in[th * 512 + q * 128: th * 512 + (q + 1) * 128,
                                       colbase + oc * 128: colbase + (oc + 1) * 128],
                            in_=stg[:, 0:128])]
                    ch.grp('gpsimd', g_d, dma=True)

            def k_cb5(oc, th, bank):
                rope_apply(bank, th, ropeC, ropeSs, stg2[:, 0:512])
                store_tok(stg2[:, 0:512], th, 0, oc)

            def v_cb5(oc, th, bank):
                def g_e(v):
                    return [v.tensor_copy(stg2[:, 0:512], pb[bank][:, 0:512])]
                ch.grp('vector', g_e)
                store_tok(stg2[:, 0:512], th, D, oc)

            def q_cb5(oc, th, bank):
                rope_apply(bank, th, ropeCq, ropeSq, stg2[:, 0:512])
                store_tok(stg2[:, 0:512], th, 2 * D, oc)

            def g_cb5(oc, th, bank):
                def g_e(v):
                    return [v.tensor_copy(stg2[:, 0:512], pb[bank][:, 0:512])]
                ch.grp('vector', g_e)
                for q in range(4):
                    def g_t(te, q=q):
                        return [te.transpose(pb[6][:, 0:128],
                                             stg2[:, q * 128:(q + 1) * 128],
                                             ident[:])]
                    ch.grp('tensor', g_t)

                    def g_v(v):
                        return [v.tensor_copy(stg[:, 0:128], pb[6][:, 0:128])]
                    ch.grp('vector', g_v)

                    def g_d(gp, q=q, th=th, oc=oc):
                        return [gp.dma_start(
                            out=g_dram[th * 512 + q * 128: th * 512 + (q + 1) * 128,
                                       oc * 128:(oc + 1) * 128],
                            in_=stg[:, 0:128])]
                    ch.grp('gpsimd', g_d, dma=True)

            _wb5 = wblock(); _wb5.__enter__()
            for th in range(2):
                emit_ln(th)
                proj3(IN['WkT_b'], k_cb5, th)
                proj3(IN['WvT_b'], v_cb5, th)
                proj3(IN['WqT_b'], q_cb5, th)
                proj3(IN['WgT_b'], g_cb5, th)
                def g_u(te):
                    r = []
                    i = 0
                    for dc in range(DC):
                        for lhs, rhs in (
                                (w1h[:, dc * 16:(dc + 1) * 16],
                                 hTh[:, dc * 512:(dc + 1) * 512]),
                                (w1l[:, dc * 16:(dc + 1) * 16],
                                 hTh[:, dc * 512:(dc + 1) * 512]),
                                (w1h[:, dc * 16:(dc + 1) * 16],
                                 hTl[:, dc * 512:(dc + 1) * 512])):
                            r.append(te.matmul(pb[3][0:16, 0:512], lhs, rhs,
                                               start=(i == 0), stop=(i == 47)))
                            i += 1
                    return r
                ch.grp('tensor', g_u)

                def g_ue(v):
                    return [v.tensor_copy(usb[:], pb[3][0:16, 0:512])]
                ch.grp('vector', g_ue)

                def g_uh(v):
                    return [v.tensor_copy(uh[:], usb[:])]
                ch.grp('vector', g_uh)

                def g_ul(v):
                    return [v.tensor_tensor(ul[:], usb[:], uh[:], ALU.subtract)]
                ch.grp('vector', g_ul)

                for fc in range(8):
                    def g_s2(te, fc=fc):
                        r = []
                        for i, (lhs, rhs) in enumerate((
                                (w2h[:, fc * 128:(fc + 1) * 128], uh[:]),
                                (w2l[:, fc * 128:(fc + 1) * 128], uh[:]),
                                (w2h[:, fc * 128:(fc + 1) * 128], ul[:]))):
                            r.append(te.matmul(pb[4][:, 0:512], lhs, rhs,
                                               start=(i == 0), stop=(i == 2)))
                        return r
                    ch.grp('tensor', g_s2)

                    def g_sb(s, fc=fc):
                        return [s.activation(stg[:, 0:512], pb[4][:, 0:512],
                                             AF.Identity,
                                             bias=bs2_sb[:, fc:fc + 1], scale=1.0)]
                    ch.grp('scalar', g_sb)

                    def g_clip(v):
                        return [v.tensor_scalar(stg2[:, 0:512], stg[:, 0:512],
                                                CLAMP, -CLAMP, ALU.min, ALU.max)]
                    ch.grp('vector', g_clip)

                    def g_es(s):
                        return [s.activation(scrA[:, 0:512], stg2[:, 0:512],
                                             AF.Exp,
                                             bias=consts[:, C_NEG_SB:C_NEG_SB + 1],
                                             scale=1.0)]
                    ch.grp('scalar', g_es)

                    for q in range(4):
                        def g_t(te, q=q):
                            return [te.transpose(pb[6][:, 0:128],
                                                 scrA[:, q * 128:(q + 1) * 128],
                                                 ident[:])]
                        ch.grp('tensor', g_t)

                        def g_v(v):
                            return [v.tensor_copy(stg[:, 0:128], pb[6][:, 0:128])]
                        ch.grp('vector', g_v)

                        def g_d(gp, q=q, th=th, fc=fc):
                            return [gp.dma_start(
                                out=ag2_in[th * 512 + q * 128:
                                           th * 512 + (q + 1) * 128,
                                           3 * D + fc * 128: 3 * D + (fc + 1) * 128],
                                in_=stg[:, 0:128])]
                        ch.grp('gpsimd', g_d, dma=True)

            _wb5.__exit__(None, None, None)

            def g_ag2(gp):
                return [gp.collective_compute("AllGather", ALU.bypass,
                                              replica_groups=G4,
                                              ins=[ag2_in], outs=[ag2_out])]
            ch.grp('gpsimd', g_ag2, collective=True)

        if stop_here("p5"):
            print("cnt:", dict(c.cnt))
            return nc

        CH6 = 4

        with ExitStack() as st6:
            def sb6(name, shape, dt=F32):
                return st6.enter_context(nc.sbuf_tensor(name, shape, dt))

            class _NS:
                pass
            bufs6 = []
            for ci in range(CH6):
                B = _NS()
                B.kch = sb6(f"kch{ci}", [128, 128])
                B.vch = sb6(f"vch{ci}", [128, 128])
                B.qch = sb6(f"qch{ci}", [128, 128])
                B.ech = sb6(f"ech{ci}", [128, 64])
                B.ktT = sb6(f"ktT{ci}", [128, 128])
                B.qtT = sb6(f"qtT{ci}", [128, 128])
                B.esT = sb6(f"esT{ci}", [64, 128])
                B.cumsT = sb6(f"cumsT{ci}", [64, 128])
                B.enzT = sb6(f"enzT{ci}", [64, 128])
                B.enz_t = sb6(f"enz_t{ci}", [128, 64])
                B.car = sb6(f"car{ci}", [64, 2])
                B.aT = sb6(f"aT{ci}", [128, 128])
                B.ok_t = sb6(f"ok_t{ci}", [128, 64])
                B.et_t = sb6(f"et_t{ci}", [128, 64])
                B.w_t = sb6(f"w_t{ci}", [128, 64])
                B.wtT = sb6(f"wtT{ci}", [64, 128])
                B.w2T = sb6(f"w2T{ci}", [128, 128])
                B.sS = sb6(f"sS{ci}", [128, 2 * 64])
                B.sC = sb6(f"sC{ci}", [64, 2 * 128])
                B.oac = sb6(f"oac{ci}", [128, 2 * 1024])
                B.sft = sb6(f"sft{ci}", [128, 24])
                B.pst = sb6(f"pst{ci}", [128, 1664])
                B.pbA = pb[2 * ci]
                B.pbB = pb[2 * ci + 1]
                bufs6.append(B)

            chains6 = [Chain(c, dsem={'sync': f'xs{i}', 'gpsimd': f'xg{i}'})
                       for i in range(CH6)]
            for c6 in chains6:
                c6.last = ch.last

            def gen_head(ci, h):
                B = bufs6[ci]
                chi = chains6[ci]
                trn = B.pbA[:, 0:128]
                pA = B.pbA[:, 128:256]
                pW2 = B.pbA[:, 256:384]
                pO = B.pbA[:, 384:512]
                pY = B.pbB[:, 0:64]
                pSu = B.pbB[:, 64:128]
                pCu = B.pbB[0:64, 128:256]

                def g_z(v):
                    return [v.memset(B.sS[:], 0.0), v.memset(B.sC[:], 0.0),
                            v.memset(B.oac[:], 0.0), v.memset(B.car[:], 0.0)]
                chi.grp('vector', g_z); yield

                for j in range(KCH):
                    par = j % 2
                    npar = 1 - par

                    def g_l(sy, j=j, h=h):
                        return [
                            sy.dma_start(out=B.kch[:],
                                         in_=ag2_out[j * 128:(j + 1) * 128,
                                                     h * 128:(h + 1) * 128]),
                            sy.dma_start(out=B.vch[:],
                                         in_=ag2_out[j * 128:(j + 1) * 128,
                                                     D + h * 128: D + (h + 1) * 128]),
                            sy.dma_start(out=B.qch[:],
                                         in_=ag2_out[j * 128:(j + 1) * 128,
                                                     2 * D + h * 128:
                                                     2 * D + (h + 1) * 128]),
                            sy.dma_start(out=B.ech[:],
                                         in_=ag2_out[j * 128:(j + 1) * 128,
                                                     3 * D + h * 64:
                                                     3 * D + (h + 1) * 64])]
                    chi.grp('sync', g_l, dma=True); yield

                    def g_t1(te):
                        return [te.transpose(trn, B.kch[:], ident[:])]
                    chi.grp('tensor', g_t1); yield

                    def g_e1(v):
                        return [v.tensor_copy(B.ktT[:], trn)]
                    chi.grp('vector', g_e1); yield

                    def g_t2(te):
                        return [te.transpose(trn, B.qch[:], ident[:])]
                    chi.grp('tensor', g_t2); yield

                    def g_e2(v):
                        return [v.tensor_copy(B.qtT[:], trn)]
                    chi.grp('vector', g_e2); yield

                    def g_t3(te):
                        return [te.transpose(B.pbA[0:64, 0:128], B.ech[:],
                                             ident[:])]
                    chi.grp('tensor', g_t3); yield

                    def g_e3(v):
                        return [v.tensor_copy(B.esT[:], B.pbA[0:64, 0:128])]
                    chi.grp('vector', g_e3); yield

                    def g_cs(v, par=par):
                        return [v.tensor_tensor_scan(B.cumsT[:], ones_t[0:64, :],
                                                     B.esT[:], B.car[:, par:par + 1],
                                                     ALU.mult, ALU.add)]
                    chi.grp('vector', g_cs); yield

                    def g_cc(v, npar=npar):
                        return [v.tensor_copy(B.car[:, npar:npar + 1],
                                              B.cumsT[:, 127:128])]
                    chi.grp('vector', g_cc); yield

                    def g_rc(v):
                        return [v.reciprocal(B.enzT[:], B.cumsT[:])]
                    chi.grp('vector', g_rc); yield

                    def g_t4(te):
                        return [te.transpose(B.pbA[:, 0:64], B.enzT[:],
                                             ident[0:64, 0:64])]
                    chi.grp('tensor', g_t4); yield

                    def g_e4(v):
                        return [v.tensor_copy(B.enz_t[:], B.pbA[:, 0:64])]
                    chi.grp('vector', g_e4); yield

                    def g_A(te):
                        return [te.matmul(pA, B.ktT[:], B.qtT[:],
                                          start=True, stop=True)]
                    chi.grp('tensor', g_A); yield

                    def g_Am(v):
                        return [v.tensor_tensor(B.aT[:], pA, maskA[:],
                                                ALU.mult)]
                    chi.grp('vector', g_Am); yield

                    def g_y(te, par=par):
                        return [te.matmul(pY, B.aT[:], B.ech[:],
                                          start=True, stop=False),
                                te.matmul(pY, B.qtT[:],
                                          B.sS[:, par * 64:(par + 1) * 64],
                                          start=False, stop=True)]
                    chi.grp('tensor', g_y); yield

                    def g_ok(v):
                        return [v.tensor_tensor(B.ok_t[:], pY,
                                                B.enz_t[:], ALU.mult)]
                    chi.grp('vector', g_ok); yield

                    def g_mx(v):
                        return [v.reduce_max(B.sft[:, 0:1], B.ok_t[:], AX)]
                    chi.grp('vector', g_mx); yield

                    def g_nm(v):
                        return [v.tensor_scalar_mul(B.sft[:, 1:2], B.sft[:, 0:1],
                                                    -1.0)]
                    chi.grp('vector', g_nm); yield

                    def g_ex(s):
                        return [s.activation(B.et_t[:], B.ok_t[:], AF.Exp,
                                             bias=B.sft[:, 1:2], scale=1.0,
                                             accum_out=B.sft[:, 2:3])]
                    chi.grp('scalar', g_ex); yield

                    def g_rs(v):
                        return [v.reciprocal(B.sft[:, 3:4], B.sft[:, 2:3])]
                    chi.grp('vector', g_rs); yield

                    def g_w(v):
                        return [v.scalar_tensor_tensor(B.w_t[:], B.et_t[:],
                                                       B.sft[:, 3:4], B.enz_t[:],
                                                       ALU.mult, ALU.mult)]
                    chi.grp('vector', g_w); yield

                    def g_t5(te):
                        return [te.transpose(B.pbA[0:64, 0:128], B.w_t[:],
                                             ident[:])]
                    chi.grp('tensor', g_t5); yield

                    def g_e5(v):
                        return [v.tensor_copy(B.wtT[:], B.pbA[0:64, 0:128])]
                    chi.grp('vector', g_e5); yield

                    def g_W2(te):
                        return [te.matmul(pW2, B.esT[:], B.wtT[:],
                                          start=True, stop=True)]
                    chi.grp('tensor', g_W2); yield

                    def g_W2m(v):
                        return [v.tensor_tensor(B.w2T[:], pW2, maskA[:],
                                                ALU.mult)]
                    chi.grp('vector', g_W2m); yield

                    def g_o(te, par=par):
                        return [te.matmul(pO, B.w2T[:], B.vch[:],
                                          start=True, stop=False),
                                te.matmul(pO, B.wtT[:],
                                          B.sC[:, par * 128:(par + 1) * 128],
                                          start=False, stop=True)]
                    chi.grp('tensor', g_o); yield

                    def g_oa(v, j=j):
                        sl = (j % 8) * 128
                        op = (j // 8) % 2
                        return [v.scalar_tensor_tensor(
                            B.oac[:, op * 1024 + sl: op * 1024 + sl + 128],
                            pO, gate_sb[:, j:j + 1],
                            B.oac[:, (1 - op) * 1024 + sl:
                                  (1 - op) * 1024 + sl + 128],
                            ALU.mult, ALU.add)]
                    chi.grp('vector', g_oa); yield

                    def g_su(te):
                        return [te.matmul(pSu, B.kch[:], B.ech[:],
                                          start=True, stop=True)]
                    chi.grp('tensor', g_su); yield

                    def g_sa(v, par=par, npar=npar):
                        return [v.scalar_tensor_tensor(
                            B.sS[:, npar * 64:(npar + 1) * 64], pSu,
                            1.0, B.sS[:, par * 64:(par + 1) * 64],
                            ALU.mult, ALU.add)]
                    chi.grp('vector', g_sa); yield

                    def g_cu(te):
                        return [te.matmul(pCu, B.ech[:], B.vch[:],
                                          start=True, stop=True)]
                    chi.grp('tensor', g_cu); yield

                    def g_ca(v, par=par, npar=npar):
                        return [v.scalar_tensor_tensor(
                            B.sC[:, npar * 128:(npar + 1) * 128],
                            pCu,
                            1.0, B.sC[:, par * 128:(par + 1) * 128],
                            ALU.mult, ALU.add)]
                    chi.grp('vector', g_ca); yield

                def g_sq8(s):
                    return [s.activation(
                        B.pst[:, 640 + sl * 128: 640 + (sl + 1) * 128],
                        B.oac[:, 1024 + sl * 128: 1024 + (sl + 1) * 128],
                        AF.Square, bias=consts[:, C_ZERO:C_ZERO + 1],
                        scale=1.0, accum_out=B.sft[:, sl:sl + 1])
                        for sl in range(8)]
                chi.grp('scalar', g_sq8); yield

                def g_sqr8(s):
                    return [s.activation(B.sft[:, 8:16], B.sft[:, 0:8],
                                         AF.Sqrt,
                                         bias=consts[:, C_EPS_RMS:C_EPS_RMS + 1],
                                         scale=consts[:, C_INV_DV:C_INV_DV + 1])]
                chi.grp('scalar', g_sqr8); yield

                def g_rq8(v):
                    return [v.reciprocal(B.sft[:, 16:24], B.sft[:, 8:16])]
                chi.grp('vector', g_rq8); yield

                def g_gl8(sy, h=h):
                    return [sy.dma_start(
                        out=B.oac[:, sl * 128:(sl + 1) * 128],
                        in_=g_dram[sl * 128:(sl + 1) * 128,
                                   h * 128:(h + 1) * 128])
                        for sl in range(8)]
                chi.grp('sync', g_gl8, dma=True); yield

                def g_sil8(s):
                    return [s.activation(B.pst[:, 640:1664], B.oac[:, 0:1024],
                                         AF.Silu,
                                         bias=consts[:, C_ZERO:C_ZERO + 1],
                                         scale=1.0)]
                chi.grp('scalar', g_sil8); yield

                for sl in range(8):
                    otile = B.oac[:, 1024 + sl * 128: 1024 + (sl + 1) * 128]

                    def g_on(v, sl=sl, otile=otile):
                        return [v.tensor_scalar(B.pst[:, 0:128], otile,
                                                B.sft[:, 16 + sl:17 + sl],
                                                None, ALU.mult)]
                    chi.grp('vector', g_on); yield

                    def g_gn(v):
                        return [v.tensor_tensor(B.pst[:, 384:512],
                                                B.pst[:, 0:128],
                                                gnw_bc[:], ALU.mult)]
                    chi.grp('vector', g_gn); yield

                    def g_og(v, sl=sl):
                        return [v.tensor_tensor(
                            B.oac[:, sl * 128:(sl + 1) * 128],
                            B.pst[:, 384:512],
                            B.pst[:, 640 + sl * 128: 640 + (sl + 1) * 128],
                            ALU.mult)]
                    chi.grp('vector', g_og); yield

                def g_d8(gp, h=h):
                    return [gp.dma_start(
                        out=o_dram[sl * 128:(sl + 1) * 128,
                                   h * 128:(h + 1) * 128],
                        in_=B.oac[:, sl * 128:(sl + 1) * 128])
                        for sl in range(8)]
                chi.grp('gpsimd', g_d8, dma=True); yield

            gens = []
            for ci in range(CH6):
                def mk(ci=ci):
                    for h in range(ci, HB, CH6):
                        yield from gen_head(ci, h)
                gens.append(mk())
            drive(gens)

            def g_bar6(v):
                return [v.memset(stg2[:, 0:1], 0.0)]
            ch.grp('vector', g_bar6, extra=[c6.last for c6 in chains6])

        _wb7 = wblock(); _wb7.__enter__()
        for th in range(2):
            for fc in range(DC):
                for q in range(4):
                    tt = th * 4 + q

                    def g_l(sy, fc=fc, tt=tt):
                        return [sy.dma_start(out=stg[:, 0:128],
                                             in_=o_dram[ts(tt),
                                                        fc * 128:(fc + 1) * 128])]
                    ch.grp('sync', g_l, dma=True)

                    def g_t(te):
                        return [te.transpose(pb[6][:, 0:128], stg[:, 0:128],
                                             ident[:])]
                    ch.grp('tensor', g_t)
                    split_evac(pb[6][:, 0:128],
                               hTh[:, fc * 512 + q * 128: fc * 512 + (q + 1) * 128],
                               hTl[:, fc * 512 + q * 128: fc * 512 + (q + 1) * 128])
            proj3(IN['WoT_b'], wo_cb, th)
        _wb7.__exit__(None, None, None)
        tap_xres("x3")
        if stop_here("x3"):
            print("cnt:", dict(c.cnt))
            return nc

        emit_ffn(None)

        def g_out(gp):
            return [gp.dma_start(out=xout[ts(tt), :],
                                 in_=xres[:, tt * D:(tt + 1) * D])
                    for tt in range(NT)]
        ch.grp('gpsimd', g_out, dma=True)
        emit_dumps()

        def g_fin(gp):
            return [gp.engine_nop()]
        ch.grp('gpsimd', g_fin)

    return nc


def _prep_inputs(inputs):
    f32 = np.float32
    x = np.ascontiguousarray(inputs['L0_attn_a_hidden_states'], dtype=f32)
    w = {k: np.ascontiguousarray(np.asarray(v).T, dtype=f32)
         for k, v in (("wqT_a", inputs['L0_attn_a_wq']),
                      ("wkT_a", inputs['L0_attn_a_wk']),
                      ("wvT_a", inputs['L0_attn_a_wv']),
                      ("woT_a", inputs['L0_attn_a_wo']),
                      ("WqT_b", inputs['L0_attn_b_Wq']),
                      ("WkT_b", inputs['L0_attn_b_Wk']),
                      ("WvT_b", inputs['L0_attn_b_Wv']),
                      ("WgT_b", inputs['L0_attn_b_Wg']),
                      ("WoT_b", inputs['L0_attn_b_Wo']),
                      ("Ws1T", inputs['L0_attn_b_Ws1']),
                      ("Ws2T", inputs['L0_attn_b_Ws2']))}
    wupT = np.ascontiguousarray(np.asarray(inputs['L0_ffn_W_up']).T, dtype=f32)
    wgateT = np.ascontiguousarray(np.asarray(inputs['L0_ffn_W_gate']).T, dtype=f32)
    wdownT = np.ascontiguousarray(np.asarray(inputs['L0_ffn_W_down']).T, dtype=f32)
    rel = np.ascontiguousarray(inputs['L0_attn_a_rel_pos'], dtype=f32)
    bs2 = np.ascontiguousarray(np.asarray(inputs['L0_attn_b_bs2']).reshape(1, 1024), dtype=f32)
    gnw = np.ascontiguousarray(np.asarray(inputs['L0_attn_b_g_norm_weight']).reshape(1, 128), dtype=f32)

    in_maps = []
    for core in range(NCORES):
        b, qi = core // 4, core % 4
        q0 = qi * TOK
        own0 = qi * NT
        gate = np.array([1.0 if own0 <= j < own0 + NT else 0.0
                         for j in range(32)], dtype=f32)
        widx = np.clip(np.arange(MS_LEN) - 512 - q0, 0, 1022)
        rel_ms = np.ascontiguousarray(rel[widx])
        m = {
            'xin': np.ascontiguousarray(x[b, q0:q0 + TOK, :]),
            'rel_ms': rel_ms,
            'Ws2T': w['Ws2T'], 'Ws1T': w['Ws1T'],
            'bs2': bs2, 'gnw': gnw,
            'wupT_s': np.ascontiguousarray(wupT[core * 256:(core + 1) * 256, :]),
            'wgateT_s': np.ascontiguousarray(wgateT[core * 256:(core + 1) * 256, :]),
            'wdownT_s': np.ascontiguousarray(wdownT[core * 1024:(core + 1) * 1024, :]),
            'meta_f': np.array([[q0, own0, 0, 0, 0, 0, 0, 0]], dtype=f32),
            'gate_own': gate.reshape(1, 32),
        }
        for k in ("wqT_a", "wkT_a", "wvT_a", "woT_a", "WqT_b", "WkT_b",
                  "WvT_b", "WgT_b", "WoT_b"):
            m[k] = w[k]
        in_maps.append(m)
    return in_maps


_NC_CACHE = {}


def kernel(**inputs):
    key = (STOP_AFTER, tuple(DEBUG_TAPS))
    if key not in _NC_CACHE:
        _NC_CACHE[key] = build_program()
    nc = _NC_CACHE[key]
    in_maps = _prep_inputs(inputs)
    res = run_bass_kernel_spmd(nc, in_maps, core_ids=list(range(NCORES)))
    out = np.empty((BATCH, SEQ, D), dtype=np.float32)
    for core in range(NCORES):
        b, qi = core // 4, core % 4
        out[b, qi * TOK:(qi + 1) * TOK, :] = res.results[core]["xout"]
    kernel.last_results = res.results
    return out


if __name__ == "__main__":
    build_program()
    print("build ok")


# version 12 (final)
# speedup vs baseline: 1.0154x; 1.0094x over previous
import math
from contextlib import ExitStack

import numpy as np

import concourse.bass as bass
import concourse.mybir as mybir
from concourse.bass_utils import run_bass_kernel_spmd

F32 = mybir.dt.float32
BF16 = mybir.dt.bfloat16
I32 = mybir.dt.int32
AF = mybir.ActivationFunctionType
ALU = mybir.AluOpType
AX = mybir.AxisListType.X

BATCH, SEQ, D = 2, 4096, 2048
NCORES = 8
TOK = 1024
NT = TOK // 128
DC = D // 128
HA, HDA = 32, 64
KCH = SEQ // 128
HB, DKB, DVB, MB = 16, 128, 128, 64
E = 8192
CLAMP = 32.0
EPS_LN = 1e-5
EPS_RMS = 1e-5
EPS_FFN = 1e-8
SHIFT_A = 32.0
SHIFT_B = 32.0
ROPE_BASE = 10000.0
MEXT_LEN = 8704
MS_LEN = 5376
AG2C = 3 * D + HB * MB

DEBUG_TAPS = []
P2_HG = 16
P2_KC = KCH
DEBUG_DUMP = []
STOP_AFTER = None

DMA_SEM = {'sync': 'sd', 'gpsimd': 'gd', 'vector': 'vd', 'scalar': 'ad'}
CMP_SEM = {'tensor': 't', 'vector': 'v', 'scalar': 'a', 'gpsimd': 'g',
           'sync': 'sy'}


class Ctx:
    def __init__(self):
        self.nc = bass.Bass("TRN2", target_bir_lowering=False)
        self.es = ExitStack()
        self.sem = {}
        self.cnt = {}
        self.lw = {}
        self.block = None

    def bump(self, key, amt=1):
        self.cnt[key] += amt
        return self.cnt[key]

    def wait(self, eng, ename, key, val):
        if val <= 0:
            return
        if self.lw.get((ename, key), -1) >= val:
            return
        eng.wait_ge(self.sem[key], val)
        self.lw[(ename, key)] = val

    def sbuf(self, name, shape, dt=F32, stack=None):
        return (stack or self.es).enter_context(
            self.nc.sbuf_tensor(name, shape, dt))


class Chain:

    def __init__(self, c, dsem=None):
        self.c = c
        self.last = None
        self.pending = []
        self.dsem = dsem or {}

    def grp(self, ename, fn, dma=False, collective=False, extra=()):
        c = self.c
        key = ('cc' if collective else
               (self.dsem.get(ename, DMA_SEM[ename]) if dma
                else CMP_SEM[ename]))
        waits = [w for w in extra if w is not None]
        if self.pending:
            waits.extend(w for w in self.pending if w is not None)
            self.pending = []
        if self.last is not None:
            waits.append(self.last)
        holder = {}

        def body(eng):
            for (k, v) in waits:
                c.wait(eng, ename, k, v)
            instrs = fn(eng)
            assert instrs, f"empty group on {ename}"
            if dma:
                for ins in instrs:
                    ins.then_inc(c.sem[key], 16)
                holder['n'] = 16 * len(instrs)
            else:
                instrs[-1].then_inc(c.sem[key], 1)
                holder['n'] = 1
        getattr(c.block, ename)(body)
        self.last = (key, c.bump(key, holder['n']))
        return self.last


def drive(gens):
    live = list(gens)
    while live:
        nxt = []
        for g in live:
            try:
                next(g)
                nxt.append(g)
            except StopIteration:
                pass
        live = nxt


def ts(i, n=128):
    return slice(i * n, (i + 1) * n)


def rawap(t, off, ap):
    return bass.AP(tensor=t.tensor, offset=off, ap=ap)


def build_program():
    c = Ctx()
    nc = c.nc

    def din(name, shape, dt=F32):
        return nc.dram_tensor(name, shape, dt, kind="ExternalInput").ap()

    def dint(name, shape, dt=F32):
        return nc.dram_tensor(name, shape, dt, kind="Internal").ap()

    IN = {'xin': din("xin", [TOK, D])}
    for nm in ("wqT_a", "wkT_a", "wvT_a", "woT_a",
               "WqT_b", "WkT_b", "WvT_b", "WgT_b", "WoT_b"):
        IN[nm] = din(nm, [D, D])
    IN['rel_ms'] = din("rel_ms", [MS_LEN, 64])
    IN['Ws1T'] = din("Ws1T", [D, 16])
    IN['Ws2T'] = din("Ws2T", [16, 1024])
    IN['bs2'] = din("bs2", [1, 1024])
    IN['gnw'] = din("gnw", [1, 128])
    IN['wupT_s'] = din("wupT_s", [D // 8, E])
    IN['wgateT_s'] = din("wgateT_s", [D // 8, E])
    IN['wdownT_s'] = din("wdownT_s", [E // 8, D])
    IN['meta_f'] = din("meta_f", [1, 8])
    IN['gate_own'] = din("gate_own", [1, 32])

    xout = nc.dram_tensor("xout", [TOK, D], F32, kind="ExternalOutput").ap()
    taps = {t: nc.dram_tensor("tap_" + t, [TOK, D], F32,
                              kind="ExternalOutput").ap() for t in DEBUG_TAPS}

    msl = dint("msl", [1, MS_LEN])
    ar_in = dint("ar_in", [1, 8]); ar_out = dint("ar_out", [1, 8])
    agt_in = dint("agt_in", [768, E], BF16)
    agt_out = dint("agt_out", [8 * 768, E], BF16)
    ag1k_in = dint("ag1k_in", [2 * D, TOK], BF16)
    ag1k_out = dint("ag1k_out", [4 * 2 * D, TOK], BF16)
    ag1v_in = dint("ag1v_in", [TOK, D])
    ag1v_out = dint("ag1v_out", [4 * TOK, D])
    ag2_in = dint("ag2_in", [TOK, AG2C])
    ag2_out = dint("ag2_out", [4 * TOK, AG2C])
    cxh_dram = dint("cxh_dram", [D, TOK], BF16)
    cxl_dram = dint("cxl_dram", [D, TOK], BF16)
    rro_dram = [dint(f"rro{i}", [1, 512]) for i in range(2)]
    g_dram = dint("g_dram", [TOK, D])
    o_dram = dint("o_dram", [TOK, D])
    q_dram = dint("q_dram", [D, TOK])
    qs_dram = dint("qs_dram", [2 * D, TOK], BF16)
    G4 = [[0, 1, 2, 3], [4, 5, 6, 7]]
    G8 = [list(range(8))]

    with c.es:
        xres = c.sbuf("xres", [128, NT * D])
        hTh = c.sbuf("hTh", [128, DC * 512], BF16)
        hTl = c.sbuf("hTl", [128, DC * 512], BF16)
        ident = c.sbuf("ident", [128, 128])
        maskA = c.sbuf("maskA", [128, 128])
        ones_t = c.sbuf("ones_t", [128, 128])
        consts = c.sbuf("consts", [128, 24])
        (C_NEG_SA, C_EPS_LN, C_INV_D, C_SU, C_SG, C_SD, C_HALF, C_LNIF,
         C_EPS_RMS, C_INV_DV, C_NEG_SB, C_INV2PI, C_EPS_FFN, C_2PI_HI,
         C_2PI_LO, C_SCALEA, C_ONE, C_ZERO) = range(18)
        gnw_bc = c.sbuf("gnw_bc", [128, 128])
        bs2_sb = c.sbuf("bs2_sb", [128, 8])
        gate_sb = c.sbuf("gate_sb", [128, 32])
        q0_bc = c.sbuf("q0_bc", [128, 1])
        meta_sb = c.sbuf("meta_sb", [1, 8])
        scrA = c.sbuf("scrA", [128, 2048])
        stat = c.sbuf("stat", [128, 8])
        WB = {}
        _wb_id = [0]

        from contextlib import contextmanager

        @contextmanager
        def wblock():
            i = _wb_id[0]; _wb_id[0] += 1
            with nc.sbuf_tensor(f"wbuf{i}", [128, 2 * 2048], F32) as a, \
                 nc.sbuf_tensor(f"wbh{i}", [128, 2 * 2048], BF16) as b, \
                 nc.sbuf_tensor(f"wbl{i}", [128, 2 * 2048], BF16) as d:
                WB['wbuf'], WB['wbh'], WB['wbl'] = a, b, d
                yield
        stg = c.sbuf("stg", [128, 1024])
        stg2 = c.sbuf("stg2", [128, 1024])

        pb = [c.es.enter_context(nc.psum_tensor(f"pb{i}", [128, 512], F32))
              for i in range(8)]

        for s in ('sd', 'gd', 'vd', 'ad', 't', 'v', 'a', 'g', 'sy', 'cc',
                  'xs0', 'xs1', 'xs2', 'xs3', 'xg0', 'xg1', 'xg2', 'xg3'):
            c.sem[s] = c.es.enter_context(nc.semaphore(s))
            c.cnt[s] = 0
        c.block = c.es.enter_context(nc.Block())
        ch = Chain(c)

        def g_loads(sy):
            r = [sy.dma_start(out=meta_sb[:], in_=IN['meta_f']),
                 sy.dma_start(out=bs2_sb[:],
                              in_=rawap(IN['bs2'], 0, [[1, 128], [128, 8]]),
                              allow_slow_non_contiguous=True),
                 sy.dma_start(out=gnw_bc[:],
                              in_=rawap(IN['gnw'], 0, [[0, 128], [1, 128]])),
                 sy.dma_start(out=gate_sb[:],
                              in_=rawap(IN['gate_own'], 0, [[0, 128], [1, 32]])),
                 sy.dma_start(out=q0_bc[:],
                              in_=rawap(IN['meta_f'], 0, [[0, 128], [1, 1]]))]
            for tt in range(NT):
                r.append(sy.dma_start(out=xres[:, tt * D:(tt + 1) * D],
                                      in_=IN['xin'][ts(tt), :]))
            return r
        ch.grp('sync', g_loads, dma=True)

        def g_const(v):
            vals = {C_NEG_SA: -SHIFT_A, C_EPS_LN: EPS_LN, C_INV_D: 1.0 / D,
                    C_HALF: math.pi / 2, C_LNIF: -math.log(ROPE_BASE) / 64.0,
                    C_EPS_RMS: EPS_RMS, C_INV_DV: 1.0 / DVB,
                    C_NEG_SB: -SHIFT_B, C_INV2PI: 0.5 / math.pi,
                    C_EPS_FFN: EPS_FFN,
                    C_2PI_HI: float(np.float32(2 * math.pi)),
                    C_2PI_LO: float(2 * math.pi - float(np.float32(2 * math.pi))),
                    C_SCALEA: 1.0 / math.sqrt(HDA), C_ONE: 1.0, C_ZERO: 0.0}
            r = [v.memset(consts[:, col:col + 1], val) for col, val in vals.items()]
            r.append(v.memset(ones_t[:], 1.0))
            return r
        ch.grp('vector', g_const)

        def g_ident(gp):
            return [gp.affine_select(ident[:], ones_t[:, 0:128],
                                     pattern=[[-1, 128]], compare_op=ALU.is_equal,
                                     fill=0.0, base=0, channel_multiplier=1),
                    gp.affine_select(maskA[:], ones_t[:, 0:128],
                                     pattern=[[1, 128]], compare_op=ALU.is_ge,
                                     fill=0.0, base=0, channel_multiplier=-1)]
        ch.grp('gpsimd', g_ident)

        def transpose_block(src_ap, dst_ap, bank=7):
            p, f = src_ap.shape

            def g_t(te):
                return [te.transpose(pb[bank][0:f, 0:p], src_ap, ident[:])]
            ch.grp('tensor', g_t)

            def g_v(v):
                return [v.tensor_copy(dst_ap, pb[bank][0:f, 0:p])]
            ch.grp('vector', g_v)

        def split_evac(src_psum, hi_ap, lo_ap):
            def g1(v):
                return [v.tensor_copy(hi_ap, src_psum)]
            ch.grp('vector', g1)

            def g2(v):
                return [v.tensor_tensor(lo_ap, src_psum, hi_ap, ALU.subtract)]
            ch.grp('vector', g2)

        def emit_ln(th):
            for q in range(4):
                tt = th * 4 + q
                xt = xres[:, tt * D:(tt + 1) * D]

                def g1(v, xt=xt):
                    return [v.reduce_sum(stat[:, 0:1], xt, AX)]
                ch.grp('vector', g1)

                def g2(s, xt=xt):
                    return [s.activation(scrA[:], xt, AF.Square,
                                         bias=consts[:, C_ZERO:C_ZERO + 1],
                                         scale=1.0, accum_out=stat[:, 1:2])]
                ch.grp('scalar', g2)

                def g3(v):
                    return [v.tensor_scalar_mul(stat[:, 2:3], stat[:, 0:1], 1.0 / D),
                            v.tensor_tensor(stat[:, 4:5], stat[:, 0:1],
                                            stat[:, 0:1], ALU.mult)]
                ch.grp('vector', g3)

                def g3b(v):
                    return [v.scalar_tensor_tensor(stat[:, 5:6], stat[:, 4:5],
                                                   -1.0 / D, stat[:, 1:2],
                                                   ALU.mult, ALU.add)]
                ch.grp('vector', g3b)

                def g4(s):
                    return [s.activation(stat[:, 7:8], stat[:, 5:6], AF.Sqrt,
                                         bias=consts[:, C_EPS_LN:C_EPS_LN + 1],
                                         scale=consts[:, C_INV_D:C_INV_D + 1])]
                ch.grp('scalar', g4)

                def g4b(v):
                    return [v.reciprocal(stat[:, 6:7], stat[:, 7:8])]
                ch.grp('vector', g4b)

                def g5(v, xt=xt):
                    return [v.tensor_scalar(scrA[:], xt, stat[:, 2:3],
                                            stat[:, 6:7], ALU.subtract, ALU.mult)]
                ch.grp('vector', g5)

                for dc in range(DC):
                    def g_t(te, dc=dc):
                        return [te.transpose(pb[6][:, 0:128],
                                             scrA[:, dc * 128:(dc + 1) * 128],
                                             ident[:])]
                    ch.grp('tensor', g_t)

                    split_evac(pb[6][:, 0:128],
                               hTh[:, dc * 512 + q * 128: dc * 512 + (q + 1) * 128],
                               hTl[:, dc * 512 + q * 128: dc * 512 + (q + 1) * 128])

        def load_w_block(cc, WT, oc, slot, extra=(), rows=D):
            ndc = rows // 128
            so = slot * 2048

            def g(sy):
                r = []
                for q in range(0, ndc, 4):
                    n = min(4, ndc - q)
                    r.append(sy.dma_start(
                        out=WB['wbuf'][:, so + q * 128: so + (q + n) * 128],
                        in_=rawap(WT, q * 128 * (WT.shape[1]) + oc * 128,
                                  [[WT.shape[1], 128],
                                   [128 * WT.shape[1], n], [1, 128]])))
                return r
            cc.grp('sync', g, dma=True, extra=extra)

            def g2a(v):
                return [v.tensor_copy(WB['wbh'][:, so:so + ndc * 128],
                                      WB['wbuf'][:, so:so + ndc * 128])]
            cc.grp('vector', g2a)

            def g2b(v):
                return [v.tensor_tensor(WB['wbl'][:, so:so + ndc * 128],
                                        WB['wbuf'][:, so:so + ndc * 128],
                                        WB['wbh'][:, so:so + ndc * 128],
                                        ALU.subtract)]
            return cc.grp('vector', g2b)

        def proj3(WT, out_cb, th, ndc=DC):
            chL = Chain(c, dsem={'sync': 'xs3'})
            chM = Chain(c)
            chL.last = ch.last
            chM.last = ch.last
            HH = {}
            HH['ld', 0] = load_w_block(chL, WT, 0, 0, rows=ndc * 128)
            for oc in range(DC):
                bank = oc % 3
                if oc + 1 < DC:
                    HH['ld', oc + 1] = load_w_block(
                        chL, WT, oc + 1, (oc + 1) % 2,
                        extra=(HH.get(('mm', oc - 1)),), rows=ndc * 128)

                def g_mm(te, oc=oc, bank=bank):
                    so = (oc % 2) * 2048
                    r = []
                    n3 = 3 * ndc
                    i = 0
                    for dc in range(ndc):
                        wh = WB['wbh'][:, so + dc * 128: so + (dc + 1) * 128]
                        wl = WB['wbl'][:, so + dc * 128: so + (dc + 1) * 128]
                        xh = hTh[:, dc * 512: dc * 512 + 512]
                        xl = hTl[:, dc * 512: dc * 512 + 512]
                        for lhs, rhs in ((wh, xh), (wl, xh), (wh, xl)):
                            r.append(te.matmul(pb[bank][:, 0:512], lhs, rhs,
                                               start=(i == 0),
                                               stop=(i == n3 - 1)))
                            i += 1
                    return r
                HH['mm', oc] = chM.grp('tensor', g_mm,
                                       extra=(HH['ld', oc],
                                              HH.get(('cb', oc - 3))))
                ch.pending.append(HH['mm', oc])
                out_cb(oc, th, bank)
                HH['cb', oc] = ch.last

        def add_psum_T_to_xres(bank, oc, tg):
            def g_e(v):
                return [v.tensor_copy(stg[:, 0:512], pb[bank][:, 0:512])]
            ch.grp('vector', g_e)
            for q in range(4):
                def g_t(te, q=q):
                    return [te.transpose(pb[6][:, 0:128],
                                         stg[:, q * 128:(q + 1) * 128], ident[:])]
                ch.grp('tensor', g_t)

                def g_a(v, q=q, oc=oc, tg=tg):
                    tt = tg * 4 + q
                    xt = xres[:, tt * D + oc * 128: tt * D + (oc + 1) * 128]
                    return [v.tensor_tensor(stg2[:, 0:128], xt, pb[6][:, 0:128],
                                            ALU.add)]
                ch.grp('vector', g_a)

                def g_a2(v, q=q, oc=oc, tg=tg):
                    tt = tg * 4 + q
                    xt = xres[:, tt * D + oc * 128: tt * D + (oc + 1) * 128]
                    return [v.tensor_copy(xt, stg2[:, 0:128])]
                ch.grp('vector', g_a2)

        def tap_xres(name):
            if name is None or name not in taps:
                return

            def g(gp):
                return [gp.dma_start(out=taps[name][ts(tt), :],
                                     in_=xres[:, tt * D:(tt + 1) * D])
                        for tt in range(NT)]
            ch.grp('gpsimd', g, dma=True)

        internals = {'msl': msl, 'ag1k_out': ag1k_out, 'ag1v_out': ag1v_out,
                     'qs_dram': qs_dram,
                     'agt_out': agt_out, 'ag2_out': ag2_out,
                     'g_dram': g_dram, 'o_dram': o_dram}
        dumps = {}
        for nm in DEBUG_DUMP:
            t = internals[nm]
            dumps[nm] = nc.dram_tensor("dump_" + nm, list(t.shape), t.dtype,
                                       kind="ExternalOutput").ap()

        def emit_dumps():
            for nm, dst in dumps.items():
                def g_d(gp, nm=nm, dst=dst):
                    return [gp.dma_start(out=dst, in_=internals[nm])]
                ch.grp('gpsimd', g_d, dma=True)

        def stop_here(name):
            if STOP_AFTER != name:
                return False
            emit_dumps()

            def g_out(gp):
                return [gp.dma_start(out=xout[ts(tt), :],
                                     in_=xres[:, tt * D:(tt + 1) * D])
                        for tt in range(NT)]
            ch.grp('gpsimd', g_out, dma=True)

            def g_fin(gp):
                return [gp.engine_nop()]
            ch.grp('gpsimd', g_fin)
            return True

        if stop_here("x0"):
            print("cnt:", dict(c.cnt))
            return nc

        with nc.sbuf_tensor("tw", [128, 4096], F32) as tw, \
             nc.sbuf_tensor("tth", [128, 4096], F32) as tth, \
             nc.sbuf_tensor("ttb", [128, 4096], BF16) as ttb, \
             nc.sbuf_tensor("tacc", [128, 16], F32) as tacc, \
             nc.sbuf_tensor("scl", [128, 12], F32) as scl:
            specs = [(IN['wupT_s'], 2, E), (IN['wgateT_s'], 2, E),
                     (IN['wdownT_s'], 8, D)]
            blocks = []
            for wi, (wt, nb, rl) in enumerate(specs):
                for rb in range(nb):
                    for ci in range(0, rl, 4096):
                        cw = min(4096, rl - ci)
                        blocks.append((wi, wt, rb, ci, cw))

            for bi, (wi, wt, rb, ci, cw) in enumerate(blocks):
                def g_l(sy, wt=wt, rb=rb, ci=ci, cw=cw):
                    return [sy.dma_start(out=tw[:, 0:cw],
                                         in_=wt[ts(rb), ci:ci + cw])]
                ch.grp('sync', g_l, dma=True)

                def g_r(v, bi=bi, cw=cw):
                    return [v.tensor_reduce(tacc[:, bi:bi + 1], tw[:, 0:cw], AX,
                                            ALU.add, apply_absolute_value=True)]
                ch.grp('vector', g_r)

            def g_ws(v):
                return [v.reduce_sum(scl[:, 9 + 0:10 + 0], tacc[:, 0:4], AX),
                        v.reduce_sum(scl[:, 9 + 1:10 + 1], tacc[:, 4:8], AX),
                        v.reduce_sum(scl[:, 9 + 2:10 + 2], tacc[:, 8:16], AX)]
            ch.grp('vector', g_ws)

            def g_cp(te):
                return [te.matmul(pb[0][0:1, wi:wi + 1], scl[:, 9 + wi:10 + wi],
                                  ones_t[:, 0:1], start=(wi == 0), stop=(wi == 2),
                                  skip_group_check=True) for wi in range(3)]
            ch.grp('tensor', g_cp)

            def g_ev0(v):
                return [v.memset(scl[0:1, 0:8], 0.0)]
            ch.grp('vector', g_ev0)

            def g_ev(v):
                return [v.tensor_copy(scl[0:1, 0:3], pb[0][0:1, 0:3])]
            ch.grp('vector', g_ev)

            def g_ar_in(gp):
                return [gp.dma_start(out=ar_in, in_=scl[0:1, 0:8])]
            ch.grp('gpsimd', g_ar_in, dma=True)

            def g_ar(gp):
                return [gp.collective_compute("AllReduce", ALU.add,
                                              replica_groups=G8,
                                              ins=[ar_in], outs=[ar_out])]
            ch.grp('gpsimd', g_ar, collective=True)

            def g_bc(gp):
                return [gp.dma_start(out=scl[:, wi:wi + 1],
                                     in_=rawap(ar_out, wi, [[0, 128], [1, 1]]))
                        for wi in range(3)]
            ch.grp('gpsimd', g_bc, dma=True)

            def g_scalesA(v):
                return [v.tensor_scalar(consts[:, (C_SU, C_SG, C_SD)[wi]:
                                                (C_SU, C_SG, C_SD)[wi] + 1],
                                        scl[:, wi:wi + 1], 1.0 / (D * E),
                                        EPS_FFN, ALU.mult, ALU.add)
                        for wi in range(3)]
            ch.grp('vector', g_scalesA)

            def g_scalesB(v):
                return [v.tensor_scalar_mul(scl[:, 4 + wi:5 + wi],
                                            consts[:, (C_SU, C_SG, C_SD)[wi]:
                                                   (C_SU, C_SG, C_SD)[wi] + 1],
                                            0.5) for wi in range(3)]
            ch.grp('vector', g_scalesB)

            def g_scalesC(v):
                return [v.tensor_scalar_mul(scl[:, 8 + wi:9 + wi],
                                            scl[:, 4 + wi:5 + wi], -1.0)
                        for wi in range(3)]
            ch.grp('vector', g_scalesC)

            for (wi, wt, rb, ci, cw) in blocks:
                def g_l(sy, wt=wt, rb=rb, ci=ci, cw=cw):
                    return [sy.dma_start(out=tw[:, 0:cw],
                                         in_=wt[ts(rb), ci:ci + cw])]
                ch.grp('sync', g_l, dma=True)

                def g_t1(v, wi=wi, cw=cw):
                    nthr = scl[:, 8 + wi:9 + wi]
                    return [v.tensor_single_scalar(tth[:, 0:cw], tw[:, 0:cw],
                                                   nthr, ALU.is_lt)]
                ch.grp('vector', g_t1)

                def g_t2(v, wi=wi, cw=cw):
                    thr = scl[:, 4 + wi:5 + wi]
                    return [v.scalar_tensor_tensor(ttb[:, 0:cw], tw[:, 0:cw],
                                                   thr, tth[:, 0:cw],
                                                   ALU.is_gt, ALU.subtract)]
                ch.grp('vector', g_t2)

                def g_s(gp, wi=wi, rb=rb, ci=ci, cw=cw):
                    if wi < 2:
                        row0 = wi * 256 + rb * 128
                        return [gp.dma_start(out=agt_in[row0:row0 + 128,
                                                        ci:ci + cw],
                                             in_=ttb[:, 0:cw])]
                    row0 = 512 + rb * 32
                    return [gp.dma_start(
                        out=rawap(agt_in, row0 * E, [[2048, 128], [1, 2048]]),
                        in_=ttb[:, 0:cw])]
                ch.grp('gpsimd', g_s, dma=True)

            def g_agt(gp):
                return [gp.collective_compute("AllGather", ALU.bypass,
                                              replica_groups=G8,
                                              ins=[agt_in], outs=[agt_out])]
            ch.grp('gpsimd', g_agt, collective=True)

        if stop_here("tern"):
            print("cnt:", dict(c.cnt))
            return nc

        with nc.sbuf_tensor("rp", [128, 64], F32) as rp, \
             nc.sbuf_tensor("mcol", [128, 2], F32) as mcol:
            for i in range(MS_LEN // 128):
                def g_l(sy, i=i):
                    return [sy.dma_start(out=rp[:],
                                         in_=IN['rel_ms'][ts(i), :])]
                ch.grp('sync', g_l, dma=True)

                def g_r(v):
                    return [v.reduce_sum(mcol[:, 0:1], rp[:], AX)]
                ch.grp('vector', g_r)

                def g_r2(v):
                    return [v.tensor_scalar_mul(mcol[:, 1:2], mcol[:, 0:1],
                                                1.0 / 64.0)]
                ch.grp('vector', g_r2)

                def g_st(gp, i=i):
                    return [gp.dma_start(out=msl[0:1, i * 128:(i + 1) * 128],
                                         in_=mcol[:, 1:2])]
                ch.grp('gpsimd', g_st, dma=True)

        if stop_here("mext"):
            print("cnt:", dict(c.cnt))
            return nc

        def q_cb(oc, th, bank):
            split_evac(pb[bank][:, 0:512], stgh[:, 0:512], stgl[:, 0:512])

            def g(gp, oc=oc, th=th):
                return [gp.dma_start(out=qs_dram[oc * 128:(oc + 1) * 128,
                                                 th * 512:(th + 1) * 512],
                                     in_=stgh[:, 0:512]),
                        gp.dma_start(out=qs_dram[D + oc * 128: D + (oc + 1) * 128,
                                                 th * 512:(th + 1) * 512],
                                     in_=stgl[:, 0:512])]
            ch.grp('gpsimd', g, dma=True)

        def k_cb(oc, th, bank):
            split_evac(pb[bank][:, 0:512], stgh[:, 0:512], stgl[:, 0:512])

            def g(gp, oc=oc, th=th):
                return [gp.dma_start(out=ag1k_in[oc * 128:(oc + 1) * 128,
                                                 th * 512:(th + 1) * 512],
                                     in_=stgh[:, 0:512]),
                        gp.dma_start(out=ag1k_in[D + oc * 128: D + (oc + 1) * 128,
                                                 th * 512:(th + 1) * 512],
                                     in_=stgl[:, 0:512])]
            ch.grp('gpsimd', g, dma=True)

        def v_cb(oc, th, bank):
            def g_e(v):
                return [v.tensor_copy(stg[:, 0:512], pb[bank][:, 0:512])]
            ch.grp('vector', g_e)
            for q in range(4):
                transpose_block(stg[:, q * 128:(q + 1) * 128],
                                stg2[:, q * 128:(q + 1) * 128], bank=6)

            def g(gp, oc=oc, th=th):
                return [gp.dma_start(
                    out=ag1v_in[th * 512 + q * 128: th * 512 + (q + 1) * 128,
                                oc * 128:(oc + 1) * 128],
                    in_=stg2[:, q * 128:(q + 1) * 128]) for q in range(4)]
            ch.grp('gpsimd', g, dma=True)

        with wblock(), \
             nc.sbuf_tensor("stgh", [128, 1024], BF16) as stgh, \
             nc.sbuf_tensor("stgl", [128, 1024], BF16) as stgl:
            for th in range(2):
                emit_ln(th)
                proj3(IN['wqT_a'], q_cb, th)
                proj3(IN['wkT_a'], k_cb, th)
                proj3(IN['wvT_a'], v_cb, th)

        def g_ag1a(gp):
            return [gp.collective_compute("AllGather", ALU.bypass,
                                          replica_groups=G4,
                                          ins=[ag1k_in], outs=[ag1k_out])]
        ch.grp('gpsimd', g_ag1a, collective=True)

        def g_ag1b(gp):
            return [gp.collective_compute("AllGather", ALU.bypass,
                                          replica_groups=G4,
                                          ins=[ag1v_in], outs=[ag1v_out])]
        ch.grp('gpsimd', g_ag1b, collective=True)

        if stop_here("p1"):
            print("cnt:", dict(c.cnt))
            return nc

        CHA = 2
        with ExitStack() as st2:
            def sb2(name, shape, dt=F32):
                return st2.enter_context(nc.sbuf_tensor(name, shape, dt))

            class _NS2:
                pass
            bufs2 = []
            for ci in range(CHA):
                B = _NS2()
                B.kt = sb2(f"kt{ci}", [64, 2 * SEQ], BF16)
                B.va = sb2(f"va{ci}", [128, KCH * 65])
                B.qp = sb2(f"qp{ci}", [64, 2048], BF16)
                B.btl = sb2(f"btl{ci}", [128, 1024])
                B.sT = sb2(f"sT{ci}", [128, 1024])
                B.pT = sb2(f"pT{ci}", [128, 1024])
                B.cxh = sb2(f"cxh{ci}", [64, 512], BF16)
                B.cxl = sb2(f"cxl{ci}", [64, 512], BF16)
                B.pbS = (pb[ci * 4], pb[ci * 4 + 1])
                B.pbC = (pb[ci * 4 + 2], pb[ci * 4 + 3])
                bufs2.append(B)

            chains2 = [Chain(c, dsem={'sync': f'xs{i}', 'gpsimd': f'xg{i}'})
                       for i in range(CHA)]
            for c2 in chains2:
                c2.last = ch.last

            def gen_head_a(ci, h):
                B = bufs2[ci]
                chi = chains2[ci]

                def g_kt(gp, h=h):
                    r = [gp.dma_start(
                            out=B.qp[:, th * 1024: th * 1024 + 512],
                            in_=qs_dram[h * 64: h * 64 + 64,
                                        th * 512:(th + 1) * 512])
                         for th in range(2)]
                    r += [gp.dma_start(
                            out=B.qp[:, th * 1024 + 512: th * 1024 + 1024],
                            in_=qs_dram[D + h * 64: D + h * 64 + 64,
                                        th * 512:(th + 1) * 512])
                          for th in range(2)]
                    for rr in range(4):
                        r.append(gp.dma_start(
                            out=B.kt[:, rr * 1024:(rr + 1) * 1024],
                            in_=ag1k_out[rr * 2 * D + h * 64:
                                         rr * 2 * D + h * 64 + 64, :]))
                        r.append(gp.dma_start(
                            out=B.kt[:, SEQ + rr * 1024: SEQ + (rr + 1) * 1024],
                            in_=ag1k_out[rr * 2 * D + D + h * 64:
                                         rr * 2 * D + D + h * 64 + 64, :]))
                    return r
                chi.grp('gpsimd', g_kt, dma=True); yield

                def g_vo(v):
                    return [v.memset(B.va[:], 1.0)]
                chi.grp('vector', g_vo); yield

                def g_vl(sy, h=h):
                    return [sy.dma_start(
                        out=B.va[:, kc * 65: kc * 65 + 64],
                        in_=ag1v_out[kc * 128:(kc + 1) * 128,
                                     h * 64: h * 64 + 64])
                        for kc in range(KCH)]
                chi.grp('sync', g_vl, dma=True); yield

                for kc in range(KCH):
                    def g_b(sy, kc=kc):
                        return [sy.dma_start(
                            out=B.btl[:],
                            in_=rawap(msl, kc * 128, [[1, 128], [1, 1024]]))]
                    chi.grp('sync', g_b, dma=True); yield

                    def g_sc(te, kc=kc):
                        r = []
                        for tg in range(2):
                            i = 0
                            for (koff, qoff) in ((0, 0), (0, 512), (SEQ, 0)):
                                lhs = B.kt[:, koff + kc * 128:
                                           koff + (kc + 1) * 128]
                                rhs_ap = B.qp[:, tg * 1024 + qoff:
                                              tg * 1024 + qoff + 512]
                                r.append(te.matmul(
                                    B.pbS[tg][:, 0:512], lhs, rhs_ap,
                                    start=(i == 0), stop=(i == 2),
                                    skip_group_check=True))
                                i += 1
                        return r
                    chi.grp('tensor', g_sc); yield

                    def g_ev(v, kc=kc):
                        r = []
                        for tg in range(2):
                            bap = B.btl.ap()
                            rev = bass.AP(tensor=bap.tensor,
                                          offset=bap.offset + (1023 - tg * 512),
                                          ap=[[bap.ap[0][0], 128], [-1, 512]])
                            r.append(v.scalar_tensor_tensor(
                                B.sT[:, tg * 512:(tg + 1) * 512],
                                B.pbS[tg][:, 0:512], 1.0 / math.sqrt(HDA),
                                rev, ALU.mult, ALU.add))
                        return r
                    chi.grp('vector', g_ev); yield

                    def g_ex(s):
                        return [s.activation(B.pT[:], B.sT[:], AF.Exp,
                                             bias=consts[:, C_NEG_SA:C_NEG_SA + 1],
                                             scale=1.0)]
                    chi.grp('scalar', g_ex); yield

                    def g_cx(te, kc=kc):
                        return [te.matmul(
                            B.pbC[tg][0:65, 0:512],
                            B.va[:, kc * 65:(kc + 1) * 65],
                            B.pT[:, tg * 512:(tg + 1) * 512],
                            start=(kc == 0), stop=(kc == KCH - 1),
                            skip_group_check=True) for tg in range(2)]
                    chi.grp('tensor', g_cx); yield

                for tg in range(2):
                    def g_n1(v, tg=tg):
                        return [v.reciprocal(B.sT[0:1, 512:1024],
                                             B.pbC[tg][64:65, 0:512])]
                    chi.grp('vector', g_n1); yield

                    def g_n2a(gp, ci=ci):
                        return [gp.dma_start(out=rro_dram[ci],
                                             in_=B.sT[0:1, 512:1024])]
                    chi.grp('gpsimd', g_n2a, dma=True); yield

                    def g_n2(gp, ci=ci):
                        return [gp.dma_start(
                            out=B.pT[0:64, 512:1024],
                            in_=rawap(rro_dram[ci], 0, [[0, 64], [1, 512]]))]
                    chi.grp('gpsimd', g_n2, dma=True); yield

                    def g_n3(v, tg=tg):
                        return [v.tensor_tensor(B.pT[0:64, 0:512],
                                                B.pbC[tg][0:64, 0:512],
                                                B.pT[0:64, 512:1024],
                                                ALU.mult)]
                    chi.grp('vector', g_n3); yield

                    def g_n4(v):
                        return [v.tensor_copy(B.cxh[:], B.pT[0:64, 0:512])]
                    chi.grp('vector', g_n4); yield

                    def g_n5(v):
                        return [v.tensor_tensor(B.cxl[:], B.pT[0:64, 0:512],
                                                B.cxh[:], ALU.subtract)]
                    chi.grp('vector', g_n5); yield

                    def g_n6(gp, tg=tg, h=h):
                        return [gp.dma_start(
                            out=cxh_dram[h * 64: h * 64 + 64,
                                         tg * 512:(tg + 1) * 512],
                            in_=B.cxh[:]),
                            gp.dma_start(
                            out=cxl_dram[h * 64: h * 64 + 64,
                                         tg * 512:(tg + 1) * 512],
                            in_=B.cxl[:])]
                    chi.grp('gpsimd', g_n6, dma=True); yield

            gens = []
            for ci in range(CHA):
                def mk(ci=ci):
                    for h in range(ci, HA, CHA):
                        yield from gen_head_a(ci, h)
                gens.append(mk())
            drive(gens)

            def g_bar2(v):
                return [v.memset(stg2[:, 0:1], 0.0)]
            ch.grp('vector', g_bar2, extra=[c2.last for c2 in chains2])

        def wo_cb(oc, th, bank):
            add_psum_T_to_xres(bank, oc, th)

        with wblock():
            for th in range(2):
                def g_lh(sy, th=th):
                    return [sy.dma_start(
                        out=hTh[:, dc * 512:(dc + 1) * 512],
                        in_=cxh_dram[dc * 128:(dc + 1) * 128,
                                     th * 512:(th + 1) * 512])
                        for dc in range(DC)]
                ch.grp('sync', g_lh, dma=True)

                def g_ll(sy, th=th):
                    return [sy.dma_start(
                        out=hTl[:, dc * 512:(dc + 1) * 512],
                        in_=cxl_dram[dc * 128:(dc + 1) * 128,
                                     th * 512:(th + 1) * 512])
                        for dc in range(DC)]
                ch.grp('sync', g_ll, dma=True)
                proj3(IN['woT_a'], wo_cb, th)
        tap_xres("x1")


        if STOP_AFTER == "x1":
            emit_dumps()

            def g_out(gp):
                return [gp.dma_start(out=xout[ts(tt), :],
                                     in_=xres[:, tt * D:(tt + 1) * D])
                        for tt in range(NT)]
            ch.grp('gpsimd', g_out, dma=True)

            def g_fin(gp):
                return [gp.engine_nop()]
            ch.grp('gpsimd', g_fin)
            return nc

        _ffn_id = [0]

        def emit_ffn(tapname):
            sfx = str(_ffn_id[0]); _ffn_id[0] += 1
            with nc.sbuf_tensor("tABu" + sfx, [128, 2 * 2048], BF16) as tABu, \
                 nc.sbuf_tensor("tABg" + sfx, [128, 2 * 2048], BF16) as tABg, \
                 nc.sbuf_tensor("tdn" + sfx, [128, 4 * 2048], BF16) as tdn, \
                 nc.sbuf_tensor("hidh" + sfx, [128, 2 * 4 * 512], BF16) as hidh, \
                 nc.sbuf_tensor("hidl" + sfx, [128, 2 * 4 * 512], BF16) as hidl, \
                 nc.sbuf_tensor("dacc" + sfx, [128, 16 * 512], F32) as dacc:
                for th in range(2):
                    emit_ln(th)

                    def g_z(v):
                        return [v.memset(dacc[:], 0.0)]
                    ch.grp('vector', g_z)

                    H = {}
                    chW = Chain(c, dsem={'sync': 'xs0'})
                    chD = Chain(c, dsem={'gpsimd': 'xg0'})
                    chP = Chain(c); chE = Chain(c)
                    for cc_ in (chW, chD, chP, chE):
                        cc_.last = ch.last

                    def wait_h(keys):
                        while any(k in k2valid and k not in H for k in keys):
                            yield

                    k2valid = set()

                    def ex(*keys):
                        return [H[k] for k in keys if k in H]

                    def gen_W():
                        for ec in range(64):
                            sl = (ec % 2) * 2048
                            yield from wait_h([('sil', ec - 2)])

                            def g_lu(sy, ec=ec, sl=sl):
                                r = []
                                for dc in range(DC):
                                    row0 = (dc // 2) * 768 + (dc % 2) * 128
                                    r.append(sy.dma_start(
                                        out=tABu[:, sl + dc * 128:
                                                 sl + (dc + 1) * 128],
                                        in_=rawap(agt_out, row0 * E + ec * 128,
                                                  [[E, 128], [1, 128]])))
                                return r
                            H['wu', ec] = chW.grp('sync', g_lu, dma=True,
                                                  extra=ex(('sil', ec - 2)))
                            yield
                            yield from wait_h([('h32', ec - 2)])

                            def g_lg(sy, ec=ec, sl=sl):
                                r = []
                                for dc in range(DC):
                                    row0 = (dc // 2) * 768 + 256 + (dc % 2) * 128
                                    r.append(sy.dma_start(
                                        out=tABg[:, sl + dc * 128:
                                                 sl + (dc + 1) * 128],
                                        in_=rawap(agt_out, row0 * E + ec * 128,
                                                  [[E, 128], [1, 128]])))
                                return r
                            H['wg', ec] = chW.grp('sync', g_lg, dma=True,
                                                  extra=ex(('h32', ec - 2)))
                            yield

                    def gen_D():
                        for ecg in range(16):
                            for el in range(4):
                                ec = ecg * 4 + el
                                yield from wait_h([('pd', (ecg - 1) * 16 + 15)])

                                def g_ld(gp, ec=ec, el=el):
                                    row0 = (ec // 8) * 768 + 512 + (ec % 8) * 32
                                    return [gp.dma_start(
                                        out=tdn[:, el * 2048:(el + 1) * 2048],
                                        in_=rawap(agt_out, row0 * E,
                                                  [[2048, 128], [1, 2048]]))]
                                H['dn', ecg, el] = chD.grp(
                                    'gpsimd', g_ld, dma=True,
                                    extra=ex(('pd', (ecg - 1) * 16 + 15)))
                                yield

                    def gen_P():
                        for ecg in range(16):
                            hoff = (ecg % 2) * 2048
                            for el in range(4):
                                ec = ecg * 4 + el
                                bu = ec % 2
                                bg = 2 + ec % 2
                                yield from wait_h([('wu', ec), ('sil', ec - 2)])

                                def g_up(te, bu=bu, ec=ec):
                                    sl = (ec % 2) * 2048
                                    r = []
                                    i = 0
                                    for dc in range(DC):
                                        lhs = tABu[:, sl + dc * 128:
                                                   sl + (dc + 1) * 128]
                                        for rhs in (hTh[:, dc * 512:(dc + 1) * 512],
                                                    hTl[:, dc * 512:(dc + 1) * 512]):
                                            r.append(te.matmul(pb[bu][:, 0:512],
                                                               lhs, rhs,
                                                               start=(i == 0),
                                                               stop=(i == 31)))
                                            i += 1
                                    return r
                                H['pu', ec] = chP.grp(
                                    'tensor', g_up,
                                    extra=ex(('wu', ec), ('sil', ec - 2)))
                                yield
                                yield from wait_h([('wg', ec), ('h32', ec - 2)])

                                def g_gt(te, bg=bg, ec=ec):
                                    sl = (ec % 2) * 2048
                                    r = []
                                    i = 0
                                    for dc in range(DC):
                                        lhs = tABg[:, sl + dc * 128:
                                                   sl + (dc + 1) * 128]
                                        for rhs in (hTh[:, dc * 512:(dc + 1) * 512],
                                                    hTl[:, dc * 512:(dc + 1) * 512]):
                                            r.append(te.matmul(pb[bg][:, 0:512],
                                                               lhs, rhs,
                                                               start=(i == 0),
                                                               stop=(i == 31)))
                                            i += 1
                                    return r
                                H['pg', ec] = chP.grp(
                                    'tensor', g_gt,
                                    extra=ex(('wg', ec), ('h32', ec - 2)))
                                yield
                            for dc in range(DC):
                                kk = ecg * 16 + dc
                                bd = 4 + dc % 2
                                deps = [('dn', ecg, 3), ('hl', ecg * 4 + 3),
                                        ('ac1', kk - 2)]
                                yield from wait_h(deps)

                                def g_dn(te, dc=dc, bd=bd, hoff=hoff):
                                    r = []
                                    i = 0
                                    for el in range(4):
                                        lhs = tdn[:, el * 2048 + dc * 128:
                                                  el * 2048 + (dc + 1) * 128]
                                        for rhs in (hidh[:, hoff + el * 512:
                                                         hoff + (el + 1) * 512],
                                                    hidl[:, hoff + el * 512:
                                                         hoff + (el + 1) * 512]):
                                            r.append(te.matmul(pb[bd][:, 0:512],
                                                               lhs, rhs,
                                                               start=(i == 0),
                                                               stop=(i == 7)))
                                            i += 1
                                    return r
                                H['pd', kk] = chP.grp('tensor', g_dn,
                                                      extra=ex(*deps))
                                yield

                    def gen_E():
                        for ecg in range(16):
                            hoff = (ecg % 2) * 2048
                            for el in range(4):
                                ec = ecg * 4 + el
                                bu = ec % 2
                                bg = 2 + ec % 2
                                yield from wait_h([('pu', ec)])

                                def g_sil(s, bu=bu):
                                    return [s.activation(
                                        stg[:, 0:512], pb[bu][:, 0:512], AF.Silu,
                                        bias=consts[:, C_ZERO:C_ZERO + 1],
                                        scale=consts[:, C_SU:C_SU + 1])]
                                H['sil', ec] = chE.grp('scalar', g_sil,
                                                       extra=ex(('pu', ec)))
                                yield
                                yield from wait_h([('pg', ec)])

                                def g_h32(v, bg=bg):
                                    return [v.scalar_tensor_tensor(
                                        stg2[:, 0:512], pb[bg][:, 0:512],
                                        consts[:, C_SG:C_SG + 1], stg[:, 0:512],
                                        ALU.mult, ALU.mult)]
                                H['h32', ec] = chE.grp('vector', g_h32,
                                                       extra=ex(('pg', ec)))
                                yield
                                yield from wait_h([('pd', (ecg - 2) * 16 + 15)])

                                def g_hh(v, el=el, hoff=hoff):
                                    return [v.tensor_copy(
                                        hidh[:, hoff + el * 512:
                                             hoff + (el + 1) * 512],
                                        stg2[:, 0:512])]
                                chE.grp('vector', g_hh,
                                        extra=ex(('pd', (ecg - 2) * 16 + 15)))
                                yield

                                def g_hl(v, el=el, hoff=hoff):
                                    return [v.tensor_tensor(
                                        hidl[:, hoff + el * 512:
                                             hoff + (el + 1) * 512],
                                        stg2[:, 0:512],
                                        hidh[:, hoff + el * 512:
                                             hoff + (el + 1) * 512],
                                        ALU.subtract)]
                                H['hl', ec] = chE.grp('vector', g_hl)
                                yield
                            for dc in range(DC):
                                kk = ecg * 16 + dc
                                bd = 4 + dc % 2
                                yield from wait_h([('pd', kk)])

                                def g_ac1(v, dc=dc, bd=bd):
                                    return [v.scalar_tensor_tensor(
                                        scrA[:, 0:512], pb[bd][:, 0:512],
                                        consts[:, C_SD:C_SD + 1],
                                        dacc[:, dc * 512:(dc + 1) * 512],
                                        ALU.mult, ALU.add)]
                                H['ac1', kk] = chE.grp('vector', g_ac1,
                                                       extra=ex(('pd', kk)))
                                yield

                                def g_ac2(v, dc=dc):
                                    return [v.tensor_copy(
                                        dacc[:, dc * 512:(dc + 1) * 512],
                                        scrA[:, 0:512])]
                                chE.grp('vector', g_ac2)
                                yield

                    for ec in range(64):
                        k2valid.add(('wu', ec)); k2valid.add(('wg', ec))
                        k2valid.add(('pu', ec)); k2valid.add(('pg', ec))
                        k2valid.add(('sil', ec)); k2valid.add(('h32', ec))
                        k2valid.add(('hl', ec))
                    for ecg in range(16):
                        for el in range(4):
                            k2valid.add(('dn', ecg, el))
                        for dc in range(DC):
                            k2valid.add(('pd', ecg * 16 + dc))
                            k2valid.add(('ac1', ecg * 16 + dc))

                    drive([gen_W(), gen_D(), gen_P(), gen_E()])

                    def g_barf(v):
                        return [v.memset(stg2[:, 0:1], 0.0)]
                    ch.grp('vector', g_barf,
                           extra=[chW.last, chD.last, chP.last, chE.last])

                    for dc in range(DC):
                        for q in range(4):
                            def g_t(te, dc=dc, q=q):
                                return [te.transpose(
                                    pb[6][:, 0:128],
                                    dacc[:, dc * 512 + q * 128: dc * 512 + (q + 1) * 128],
                                    ident[:])]
                            ch.grp('tensor', g_t)

                            def g_a(v, q=q, dc=dc, th=th):
                                tt = th * 4 + q
                                xt = xres[:, tt * D + dc * 128: tt * D + (dc + 1) * 128]
                                return [v.tensor_tensor(stg2[:, 0:128], xt,
                                                        pb[6][:, 0:128], ALU.add)]
                            ch.grp('vector', g_a)

                            def g_a2(v, q=q, dc=dc, th=th):
                                tt = th * 4 + q
                                xt = xres[:, tt * D + dc * 128: tt * D + (dc + 1) * 128]
                                return [v.tensor_copy(xt, stg2[:, 0:128])]
                            ch.grp('vector', g_a2)
            tap_xres(tapname)

        emit_ffn("x2")
        if stop_here("x2"):
            print("cnt:", dict(c.cnt))
            return nc

        with ExitStack() as st5:
            def sb5(name, shape, dt=F32):
                return st5.enter_context(nc.sbuf_tensor(name, shape, dt))
            ropeC = sb5("ropeC", [128, 1024])
            ropeSs = sb5("ropeSs", [128, 1024])
            ropeCq = sb5("ropeCq", [128, 1024])
            ropeSq = sb5("ropeSq", [128, 1024])
            rsc = sb5("rsc", [128, 1024])
            rsc2 = sb5("rsc2", [128, 1024])
            ri32 = sb5("ri32", [128, 1024], I32)
            rswp = sb5("rswp", [128, 512])
            w1b = sb5("w1b", [128, 256])
            w1h = sb5("w1h", [128, 256], BF16)
            w1l = sb5("w1l", [128, 256], BF16)
            w2b = sb5("w2b", [16, 1024])
            w2h = sb5("w2h", [16, 1024], BF16)
            w2l = sb5("w2l", [16, 1024], BF16)
            usb = sb5("usb", [16, 512])
            uh = sb5("uh", [16, 512], BF16)
            ul = sb5("ul", [16, 512], BF16)
            def g_i1(gp):
                return [gp.iota(rsc[:], pattern=[[1, 1024]], base=0,
                                channel_multiplier=0,
                                allow_small_or_imprecise_dtypes=True),
                        gp.iota(stat[0:64, 7:8], pattern=[[1, 1]], base=0,
                                channel_multiplier=1,
                                allow_small_or_imprecise_dtypes=True),
                        gp.iota(stat[64:128, 7:8], pattern=[[1, 1]], base=0,
                                channel_multiplier=1,
                                allow_small_or_imprecise_dtypes=True)]
            ch.grp('gpsimd', g_i1)

            def g_i2(v):
                return [v.tensor_scalar_add(rsc2[:], rsc[:], q0_bc[:, 0:1])]
            ch.grp('vector', g_i2)

            def g_i3(s):
                return [s.activation(stat[:, 6:7], stat[:, 7:8], AF.Exp,
                                     bias=consts[:, C_ZERO:C_ZERO + 1],
                                     scale=consts[:, C_LNIF:C_LNIF + 1])]
            ch.grp('scalar', g_i3)

            def g_i4(v):
                return [v.tensor_scalar(rsc[:], rsc2[:], stat[:, 6:7], None,
                                        ALU.mult)]
            ch.grp('vector', g_i4)

            def emit_sincos(src_ap, dst_ap, add_half):
                if add_half:
                    def g_a(v):
                        return [v.tensor_scalar_add(rsc2[:], src_ap, math.pi / 2)]
                    ch.grp('vector', g_a)
                    base = rsc2
                else:
                    def g_a(v):
                        return [v.tensor_copy(rsc2[:], src_ap)]
                    ch.grp('vector', g_a)
                    base = rsc2

                def g_k(v):
                    return [v.tensor_scalar_mul(stg[:, 0:1024], base[:],
                                                0.5 / math.pi)]
                ch.grp('vector', g_k)

                def g_ki(v):
                    return [v.tensor_copy(ri32[:], stg[:, 0:1024])]
                ch.grp('vector', g_ki)

                def g_kf(v):
                    return [v.tensor_copy(stg[:, 0:1024], ri32[:])]
                ch.grp('vector', g_kf)

                hi = float(np.float32(2 * math.pi))
                lo = float(2 * math.pi - float(np.float32(2 * math.pi)))

                def g_y1(v):
                    return [v.scalar_tensor_tensor(stg2[:, 0:1024], stg[:, 0:1024],
                                                   -hi, base[:], ALU.mult, ALU.add)]
                ch.grp('vector', g_y1)

                def g_y2(v):
                    return [v.scalar_tensor_tensor(rsc2[:], stg[:, 0:1024],
                                                   -lo, stg2[:, 0:1024],
                                                   ALU.mult, ALU.add)]
                ch.grp('vector', g_y2)

                def g_cl(v):
                    return [v.tensor_scalar(stg[:, 0:1024], rsc2[:],
                                            math.pi, -math.pi, ALU.min, ALU.max)]
                ch.grp('vector', g_cl)

                def g_s(s):
                    return [s.activation(dst_ap, stg[:, 0:1024], AF.Sin,
                                         bias=consts[:, C_ZERO:C_ZERO + 1],
                                         scale=1.0)]
                ch.grp('scalar', g_s)

            emit_sincos(rsc[:], ropeSs[:], False)
            emit_sincos(rsc[:], ropeC[:], True)

            def g_sf(v):
                return [v.tensor_scalar_mul(ropeSq[0:64, :], ropeSs[0:64, :], -1.0)]
            ch.grp('vector', g_sf)

            def g_sf2(v):
                return [v.tensor_copy(ropeSq[64:128, :], ropeSs[64:128, :])]
            ch.grp('vector', g_sf2)

            def g_sf3(v):
                return [v.tensor_copy(ropeSs[:], ropeSq[:])]
            ch.grp('vector', g_sf3)

            def g_sc1(v):
                return [v.tensor_scalar_mul(ropeCq[:], ropeC[:],
                                            1.0 / math.sqrt(DKB))]
            ch.grp('vector', g_sc1)

            def g_sc2(v):
                return [v.tensor_scalar_mul(ropeSq[:], ropeSs[:],
                                            1.0 / math.sqrt(DKB))]
            ch.grp('vector', g_sc2)

            def g_w1(sy):
                return [sy.dma_start(out=w1b[:, dc * 16:(dc + 1) * 16],
                                     in_=IN['Ws1T'][ts(dc), :])
                        for dc in range(DC)]
            ch.grp('sync', g_w1, dma=True)

            def g_w1h(v):
                return [v.tensor_copy(w1h[:], w1b[:])]
            ch.grp('vector', g_w1h)

            def g_w1l(v):
                return [v.tensor_tensor(w1l[:], w1b[:], w1h[:], ALU.subtract)]
            ch.grp('vector', g_w1l)

            def g_w2(sy):
                return [sy.dma_start(out=w2b[:], in_=IN['Ws2T'])]
            ch.grp('sync', g_w2, dma=True)

            def g_w2h(v):
                return [v.tensor_copy(w2h[:], w2b[:])]
            ch.grp('vector', g_w2h)

            def g_w2l(v):
                return [v.tensor_tensor(w2l[:], w2b[:], w2h[:], ALU.subtract)]
            ch.grp('vector', g_w2l)

            def rope_apply(bank, th, ctab, stab, dst):
                def g_e(v):
                    return [v.tensor_copy(stg[:, 0:512], pb[bank][:, 0:512])]
                ch.grp('vector', g_e)

                def g_sw(gp):
                    return [gp.dma_start(out=rswp[0:64, :], in_=stg[64:128, 0:512]),
                            gp.dma_start(out=rswp[64:128, :], in_=stg[0:64, 0:512])]
                ch.grp('gpsimd', g_sw, dma=True)

                def g_m1(v, th=th):
                    return [v.tensor_tensor(scrA[:, 0:512], stg[:, 0:512],
                                            ctab[:, th * 512:(th + 1) * 512],
                                            ALU.mult)]
                ch.grp('vector', g_m1)

                def g_m2(v, th=th):
                    return [v.tensor_tensor(scrA[:, 512:1024], rswp[:],
                                            stab[:, th * 512:(th + 1) * 512],
                                            ALU.mult)]
                ch.grp('vector', g_m2)

                def g_m3(v):
                    return [v.tensor_tensor(dst, scrA[:, 0:512],
                                            scrA[:, 512:1024], ALU.add)]
                ch.grp('vector', g_m3)

            def store_tok(src, th, colbase, oc):
                for q in range(4):
                    def g_t(te, q=q):
                        return [te.transpose(pb[6][:, 0:128],
                                             src[:, q * 128:(q + 1) * 128],
                                             ident[:])]
                    ch.grp('tensor', g_t)

                    def g_v(v):
                        return [v.tensor_copy(stg[:, 0:128], pb[6][:, 0:128])]
                    ch.grp('vector', g_v)

                    def g_d(gp, q=q, th=th, oc=oc, colbase=colbase):
                        return [gp.dma_start(
                            out=ag2_in[th * 512 + q * 128: th * 512 + (q + 1) * 128,
                                       colbase + oc * 128: colbase + (oc + 1) * 128],
                            in_=stg[:, 0:128])]
                    ch.grp('gpsimd', g_d, dma=True)

            def k_cb5(oc, th, bank):
                rope_apply(bank, th, ropeC, ropeSs, stg2[:, 0:512])
                store_tok(stg2[:, 0:512], th, 0, oc)

            def v_cb5(oc, th, bank):
                def g_e(v):
                    return [v.tensor_copy(stg2[:, 0:512], pb[bank][:, 0:512])]
                ch.grp('vector', g_e)
                store_tok(stg2[:, 0:512], th, D, oc)

            def q_cb5(oc, th, bank):
                rope_apply(bank, th, ropeCq, ropeSq, stg2[:, 0:512])
                store_tok(stg2[:, 0:512], th, 2 * D, oc)

            def g_cb5(oc, th, bank):
                def g_e(v):
                    return [v.tensor_copy(stg2[:, 0:512], pb[bank][:, 0:512])]
                ch.grp('vector', g_e)
                for q in range(4):
                    def g_t(te, q=q):
                        return [te.transpose(pb[6][:, 0:128],
                                             stg2[:, q * 128:(q + 1) * 128],
                                             ident[:])]
                    ch.grp('tensor', g_t)

                    def g_v(v):
                        return [v.tensor_copy(stg[:, 0:128], pb[6][:, 0:128])]
                    ch.grp('vector', g_v)

                    def g_d(gp, q=q, th=th, oc=oc):
                        return [gp.dma_start(
                            out=g_dram[th * 512 + q * 128: th * 512 + (q + 1) * 128,
                                       oc * 128:(oc + 1) * 128],
                            in_=stg[:, 0:128])]
                    ch.grp('gpsimd', g_d, dma=True)

            _wb5 = wblock(); _wb5.__enter__()
            for th in range(2):
                emit_ln(th)
                proj3(IN['WkT_b'], k_cb5, th)
                proj3(IN['WvT_b'], v_cb5, th)
                proj3(IN['WqT_b'], q_cb5, th)
                proj3(IN['WgT_b'], g_cb5, th)
                def g_u(te):
                    r = []
                    i = 0
                    for dc in range(DC):
                        for lhs, rhs in (
                                (w1h[:, dc * 16:(dc + 1) * 16],
                                 hTh[:, dc * 512:(dc + 1) * 512]),
                                (w1l[:, dc * 16:(dc + 1) * 16],
                                 hTh[:, dc * 512:(dc + 1) * 512]),
                                (w1h[:, dc * 16:(dc + 1) * 16],
                                 hTl[:, dc * 512:(dc + 1) * 512])):
                            r.append(te.matmul(pb[3][0:16, 0:512], lhs, rhs,
                                               start=(i == 0), stop=(i == 47)))
                            i += 1
                    return r
                ch.grp('tensor', g_u)

                def g_ue(v):
                    return [v.tensor_copy(usb[:], pb[3][0:16, 0:512])]
                ch.grp('vector', g_ue)

                def g_uh(v):
                    return [v.tensor_copy(uh[:], usb[:])]
                ch.grp('vector', g_uh)

                def g_ul(v):
                    return [v.tensor_tensor(ul[:], usb[:], uh[:], ALU.subtract)]
                ch.grp('vector', g_ul)

                for fc in range(8):
                    def g_s2(te, fc=fc):
                        r = []
                        for i, (lhs, rhs) in enumerate((
                                (w2h[:, fc * 128:(fc + 1) * 128], uh[:]),
                                (w2l[:, fc * 128:(fc + 1) * 128], uh[:]),
                                (w2h[:, fc * 128:(fc + 1) * 128], ul[:]))):
                            r.append(te.matmul(pb[4][:, 0:512], lhs, rhs,
                                               start=(i == 0), stop=(i == 2)))
                        return r
                    ch.grp('tensor', g_s2)

                    def g_sb(s, fc=fc):
                        return [s.activation(stg[:, 0:512], pb[4][:, 0:512],
                                             AF.Identity,
                                             bias=bs2_sb[:, fc:fc + 1], scale=1.0)]
                    ch.grp('scalar', g_sb)

                    def g_clip(v):
                        return [v.tensor_scalar(stg2[:, 0:512], stg[:, 0:512],
                                                CLAMP, -CLAMP, ALU.min, ALU.max)]
                    ch.grp('vector', g_clip)

                    def g_es(s):
                        return [s.activation(scrA[:, 0:512], stg2[:, 0:512],
                                             AF.Exp,
                                             bias=consts[:, C_NEG_SB:C_NEG_SB + 1],
                                             scale=1.0)]
                    ch.grp('scalar', g_es)

                    for q in range(4):
                        def g_t(te, q=q):
                            return [te.transpose(pb[6][:, 0:128],
                                                 scrA[:, q * 128:(q + 1) * 128],
                                                 ident[:])]
                        ch.grp('tensor', g_t)

                        def g_v(v):
                            return [v.tensor_copy(stg[:, 0:128], pb[6][:, 0:128])]
                        ch.grp('vector', g_v)

                        def g_d(gp, q=q, th=th, fc=fc):
                            return [gp.dma_start(
                                out=ag2_in[th * 512 + q * 128:
                                           th * 512 + (q + 1) * 128,
                                           3 * D + fc * 128: 3 * D + (fc + 1) * 128],
                                in_=stg[:, 0:128])]
                        ch.grp('gpsimd', g_d, dma=True)

            _wb5.__exit__(None, None, None)

            def g_ag2(gp):
                return [gp.collective_compute("AllGather", ALU.bypass,
                                              replica_groups=G4,
                                              ins=[ag2_in], outs=[ag2_out])]
            ch.grp('gpsimd', g_ag2, collective=True)

        if stop_here("p5"):
            print("cnt:", dict(c.cnt))
            return nc

        CH6 = 4

        with ExitStack() as st6:
            def sb6(name, shape, dt=F32):
                return st6.enter_context(nc.sbuf_tensor(name, shape, dt))

            class _NS:
                pass
            bufs6 = []
            for ci in range(CH6):
                B = _NS()
                B.kch = sb6(f"kch{ci}", [128, 128])
                B.vch = sb6(f"vch{ci}", [128, 128])
                B.qch = sb6(f"qch{ci}", [128, 128])
                B.ech = sb6(f"ech{ci}", [128, 64])
                B.ktT = sb6(f"ktT{ci}", [128, 128])
                B.qtT = sb6(f"qtT{ci}", [128, 128])
                B.esT = sb6(f"esT{ci}", [64, 128])
                B.cumsT = sb6(f"cumsT{ci}", [64, 128])
                B.enzT = sb6(f"enzT{ci}", [64, 128])
                B.enz_t = sb6(f"enz_t{ci}", [128, 64])
                B.car = sb6(f"car{ci}", [64, 2])
                B.aT = sb6(f"aT{ci}", [128, 128])
                B.ok_t = sb6(f"ok_t{ci}", [128, 64])
                B.et_t = sb6(f"et_t{ci}", [128, 64])
                B.w_t = sb6(f"w_t{ci}", [128, 64])
                B.wtT = sb6(f"wtT{ci}", [64, 128])
                B.w2T = sb6(f"w2T{ci}", [128, 128])
                B.sS = sb6(f"sS{ci}", [128, 2 * 64])
                B.sC = sb6(f"sC{ci}", [64, 2 * 128])
                B.oac = sb6(f"oac{ci}", [128, 2 * 1024])
                B.sft = sb6(f"sft{ci}", [128, 8])
                B.pst = sb6(f"pst{ci}", [128, 640])
                B.pbA = pb[2 * ci]
                B.pbB = pb[2 * ci + 1]
                bufs6.append(B)

            chains6 = [Chain(c, dsem={'sync': f'xs{i}', 'gpsimd': f'xg{i}'})
                       for i in range(CH6)]
            for c6 in chains6:
                c6.last = ch.last

            def gen_head(ci, h):
                B = bufs6[ci]
                chi = chains6[ci]
                trn = B.pbA[:, 0:128]
                pA = B.pbA[:, 128:256]
                pW2 = B.pbA[:, 256:384]
                pO = B.pbA[:, 384:512]
                pY = B.pbB[:, 0:64]
                pSu = B.pbB[:, 64:128]
                pCu = B.pbB[0:64, 128:256]

                def g_z(v):
                    return [v.memset(B.sS[:], 0.0), v.memset(B.sC[:], 0.0),
                            v.memset(B.oac[:], 0.0), v.memset(B.car[:], 0.0)]
                chi.grp('vector', g_z); yield

                for j in range(KCH):
                    par = j % 2
                    npar = 1 - par

                    def g_l(sy, j=j, h=h):
                        return [
                            sy.dma_start(out=B.kch[:],
                                         in_=ag2_out[j * 128:(j + 1) * 128,
                                                     h * 128:(h + 1) * 128]),
                            sy.dma_start(out=B.vch[:],
                                         in_=ag2_out[j * 128:(j + 1) * 128,
                                                     D + h * 128: D + (h + 1) * 128]),
                            sy.dma_start(out=B.qch[:],
                                         in_=ag2_out[j * 128:(j + 1) * 128,
                                                     2 * D + h * 128:
                                                     2 * D + (h + 1) * 128]),
                            sy.dma_start(out=B.ech[:],
                                         in_=ag2_out[j * 128:(j + 1) * 128,
                                                     3 * D + h * 64:
                                                     3 * D + (h + 1) * 64])]
                    chi.grp('sync', g_l, dma=True); yield

                    def g_t1(te):
                        return [te.transpose(trn, B.kch[:], ident[:])]
                    chi.grp('tensor', g_t1); yield

                    def g_e1(v):
                        return [v.tensor_copy(B.ktT[:], trn)]
                    chi.grp('vector', g_e1); yield

                    def g_t2(te):
                        return [te.transpose(trn, B.qch[:], ident[:])]
                    chi.grp('tensor', g_t2); yield

                    def g_e2(v):
                        return [v.tensor_copy(B.qtT[:], trn)]
                    chi.grp('vector', g_e2); yield

                    def g_t3(te):
                        return [te.transpose(B.pbA[0:64, 0:128], B.ech[:],
                                             ident[:])]
                    chi.grp('tensor', g_t3); yield

                    def g_e3(v):
                        return [v.tensor_copy(B.esT[:], B.pbA[0:64, 0:128])]
                    chi.grp('vector', g_e3); yield

                    def g_cs(v, par=par):
                        return [v.tensor_tensor_scan(B.cumsT[:], ones_t[0:64, :],
                                                     B.esT[:], B.car[:, par:par + 1],
                                                     ALU.mult, ALU.add)]
                    chi.grp('vector', g_cs); yield

                    def g_cc(v, npar=npar):
                        return [v.tensor_copy(B.car[:, npar:npar + 1],
                                              B.cumsT[:, 127:128])]
                    chi.grp('vector', g_cc); yield

                    def g_rc(v):
                        return [v.reciprocal(B.enzT[:], B.cumsT[:])]
                    chi.grp('vector', g_rc); yield

                    def g_t4(te):
                        return [te.transpose(B.pbA[:, 0:64], B.enzT[:],
                                             ident[0:64, 0:64])]
                    chi.grp('tensor', g_t4); yield

                    def g_e4(v):
                        return [v.tensor_copy(B.enz_t[:], B.pbA[:, 0:64])]
                    chi.grp('vector', g_e4); yield

                    def g_A(te):
                        return [te.matmul(pA, B.ktT[:], B.qtT[:],
                                          start=True, stop=True)]
                    chi.grp('tensor', g_A); yield

                    def g_Am(v):
                        return [v.tensor_tensor(B.aT[:], pA, maskA[:],
                                                ALU.mult)]
                    chi.grp('vector', g_Am); yield

                    def g_y(te, par=par):
                        return [te.matmul(pY, B.aT[:], B.ech[:],
                                          start=True, stop=False),
                                te.matmul(pY, B.qtT[:],
                                          B.sS[:, par * 64:(par + 1) * 64],
                                          start=False, stop=True)]
                    chi.grp('tensor', g_y); yield

                    def g_ok(v):
                        return [v.tensor_tensor(B.ok_t[:], pY,
                                                B.enz_t[:], ALU.mult)]
                    chi.grp('vector', g_ok); yield

                    def g_mx(v):
                        return [v.reduce_max(B.sft[:, 0:1], B.ok_t[:], AX)]
                    chi.grp('vector', g_mx); yield

                    def g_nm(v):
                        return [v.tensor_scalar_mul(B.sft[:, 1:2], B.sft[:, 0:1],
                                                    -1.0)]
                    chi.grp('vector', g_nm); yield

                    def g_ex(s):
                        return [s.activation(B.et_t[:], B.ok_t[:], AF.Exp,
                                             bias=B.sft[:, 1:2], scale=1.0,
                                             accum_out=B.sft[:, 2:3])]
                    chi.grp('scalar', g_ex); yield

                    def g_rs(v):
                        return [v.reciprocal(B.sft[:, 3:4], B.sft[:, 2:3])]
                    chi.grp('vector', g_rs); yield

                    def g_w(v):
                        return [v.scalar_tensor_tensor(B.w_t[:], B.et_t[:],
                                                       B.sft[:, 3:4], B.enz_t[:],
                                                       ALU.mult, ALU.mult)]
                    chi.grp('vector', g_w); yield

                    def g_t5(te):
                        return [te.transpose(B.pbA[0:64, 0:128], B.w_t[:],
                                             ident[:])]
                    chi.grp('tensor', g_t5); yield

                    def g_e5(v):
                        return [v.tensor_copy(B.wtT[:], B.pbA[0:64, 0:128])]
                    chi.grp('vector', g_e5); yield

                    def g_W2(te):
                        return [te.matmul(pW2, B.esT[:], B.wtT[:],
                                          start=True, stop=True)]
                    chi.grp('tensor', g_W2); yield

                    def g_W2m(v):
                        return [v.tensor_tensor(B.w2T[:], pW2, maskA[:],
                                                ALU.mult)]
                    chi.grp('vector', g_W2m); yield

                    def g_o(te, par=par):
                        return [te.matmul(pO, B.w2T[:], B.vch[:],
                                          start=True, stop=False),
                                te.matmul(pO, B.wtT[:],
                                          B.sC[:, par * 128:(par + 1) * 128],
                                          start=False, stop=True)]
                    chi.grp('tensor', g_o); yield

                    def g_oa(v, j=j):
                        sl = (j % 8) * 128
                        op = (j // 8) % 2
                        return [v.scalar_tensor_tensor(
                            B.oac[:, op * 1024 + sl: op * 1024 + sl + 128],
                            pO, gate_sb[:, j:j + 1],
                            B.oac[:, (1 - op) * 1024 + sl:
                                  (1 - op) * 1024 + sl + 128],
                            ALU.mult, ALU.add)]
                    chi.grp('vector', g_oa); yield

                    def g_su(te):
                        return [te.matmul(pSu, B.kch[:], B.ech[:],
                                          start=True, stop=True)]
                    chi.grp('tensor', g_su); yield

                    def g_sa(v, par=par, npar=npar):
                        return [v.scalar_tensor_tensor(
                            B.sS[:, npar * 64:(npar + 1) * 64], pSu,
                            1.0, B.sS[:, par * 64:(par + 1) * 64],
                            ALU.mult, ALU.add)]
                    chi.grp('vector', g_sa); yield

                    def g_cu(te):
                        return [te.matmul(pCu, B.ech[:], B.vch[:],
                                          start=True, stop=True)]
                    chi.grp('tensor', g_cu); yield

                    def g_ca(v, par=par, npar=npar):
                        return [v.scalar_tensor_tensor(
                            B.sC[:, npar * 128:(npar + 1) * 128],
                            pCu,
                            1.0, B.sC[:, par * 128:(par + 1) * 128],
                            ALU.mult, ALU.add)]
                    chi.grp('vector', g_ca); yield

                for sslot in range(8):
                    opar = 1
                    otile = B.oac[:, opar * 1024 + sslot * 128:
                                  opar * 1024 + sslot * 128 + 128]

                    def g_sq(s, otile=otile):
                        return [s.activation(B.pst[:, 512:640], otile, AF.Square,
                                             bias=consts[:, C_ZERO:C_ZERO + 1],
                                             scale=1.0, accum_out=B.sft[:, 4:5])]
                    chi.grp('scalar', g_sq); yield

                    def g_sqr(s):
                        return [s.activation(B.sft[:, 5:6], B.sft[:, 4:5], AF.Sqrt,
                                             bias=consts[:, C_EPS_RMS:C_EPS_RMS + 1],
                                             scale=consts[:, C_INV_DV:C_INV_DV + 1])]
                    chi.grp('scalar', g_sqr); yield

                    def g_rq(v):
                        return [v.reciprocal(B.sft[:, 6:7], B.sft[:, 5:6])]
                    chi.grp('vector', g_rq); yield

                    def g_on(v, otile=otile):
                        return [v.tensor_scalar(B.pst[:, 0:128], otile,
                                                B.sft[:, 6:7], None, ALU.mult)]
                    chi.grp('vector', g_on); yield

                    def g_gn(v):
                        return [v.tensor_tensor(B.pst[:, 384:512], B.pst[:, 0:128],
                                                gnw_bc[:], ALU.mult)]
                    chi.grp('vector', g_gn); yield

                    def g_gl(sy, sslot=sslot, h=h):
                        return [sy.dma_start(
                            out=B.pst[:, 128:256],
                            in_=g_dram[sslot * 128:(sslot + 1) * 128,
                                       h * 128:(h + 1) * 128])]
                    chi.grp('sync', g_gl, dma=True); yield

                    def g_sil(s):
                        return [s.activation(B.pst[:, 512:640], B.pst[:, 128:256],
                                             AF.Silu,
                                             bias=consts[:, C_ZERO:C_ZERO + 1],
                                             scale=1.0)]
                    chi.grp('scalar', g_sil); yield

                    def g_og(v):
                        return [v.tensor_tensor(B.pst[:, 256:384], B.pst[:, 384:512],
                                                B.pst[:, 512:640], ALU.mult)]
                    chi.grp('vector', g_og); yield

                    def g_d(gp, sslot=sslot, h=h):
                        return [gp.dma_start(
                            out=o_dram[sslot * 128:(sslot + 1) * 128,
                                       h * 128:(h + 1) * 128],
                            in_=B.pst[:, 256:384])]
                    chi.grp('gpsimd', g_d, dma=True); yield

            gens = []
            for ci in range(CH6):
                def mk(ci=ci):
                    for h in range(ci, HB, CH6):
                        yield from gen_head(ci, h)
                gens.append(mk())
            drive(gens)

            def g_bar6(v):
                return [v.memset(stg2[:, 0:1], 0.0)]
            ch.grp('vector', g_bar6, extra=[c6.last for c6 in chains6])

        _wb7 = wblock(); _wb7.__enter__()
        for th in range(2):
            for fc in range(DC):
                for q in range(4):
                    tt = th * 4 + q

                    def g_l(sy, fc=fc, tt=tt):
                        return [sy.dma_start(out=stg[:, 0:128],
                                             in_=o_dram[ts(tt),
                                                        fc * 128:(fc + 1) * 128])]
                    ch.grp('sync', g_l, dma=True)

                    def g_t(te):
                        return [te.transpose(pb[6][:, 0:128], stg[:, 0:128],
                                             ident[:])]
                    ch.grp('tensor', g_t)
                    split_evac(pb[6][:, 0:128],
                               hTh[:, fc * 512 + q * 128: fc * 512 + (q + 1) * 128],
                               hTl[:, fc * 512 + q * 128: fc * 512 + (q + 1) * 128])
            proj3(IN['WoT_b'], wo_cb, th)
        _wb7.__exit__(None, None, None)
        tap_xres("x3")
        if stop_here("x3"):
            print("cnt:", dict(c.cnt))
            return nc

        emit_ffn(None)

        def g_out(gp):
            return [gp.dma_start(out=xout[ts(tt), :],
                                 in_=xres[:, tt * D:(tt + 1) * D])
                    for tt in range(NT)]
        ch.grp('gpsimd', g_out, dma=True)
        emit_dumps()

        def g_fin(gp):
            return [gp.engine_nop()]
        ch.grp('gpsimd', g_fin)

    return nc


def _prep_inputs(inputs):
    f32 = np.float32
    x = np.ascontiguousarray(inputs['L0_attn_a_hidden_states'], dtype=f32)
    w = {k: np.ascontiguousarray(np.asarray(v).T, dtype=f32)
         for k, v in (("wqT_a", inputs['L0_attn_a_wq']),
                      ("wkT_a", inputs['L0_attn_a_wk']),
                      ("wvT_a", inputs['L0_attn_a_wv']),
                      ("woT_a", inputs['L0_attn_a_wo']),
                      ("WqT_b", inputs['L0_attn_b_Wq']),
                      ("WkT_b", inputs['L0_attn_b_Wk']),
                      ("WvT_b", inputs['L0_attn_b_Wv']),
                      ("WgT_b", inputs['L0_attn_b_Wg']),
                      ("WoT_b", inputs['L0_attn_b_Wo']),
                      ("Ws1T", inputs['L0_attn_b_Ws1']),
                      ("Ws2T", inputs['L0_attn_b_Ws2']))}
    wupT = np.ascontiguousarray(np.asarray(inputs['L0_ffn_W_up']).T, dtype=f32)
    wgateT = np.ascontiguousarray(np.asarray(inputs['L0_ffn_W_gate']).T, dtype=f32)
    wdownT = np.ascontiguousarray(np.asarray(inputs['L0_ffn_W_down']).T, dtype=f32)
    rel = np.ascontiguousarray(inputs['L0_attn_a_rel_pos'], dtype=f32)
    bs2 = np.ascontiguousarray(np.asarray(inputs['L0_attn_b_bs2']).reshape(1, 1024), dtype=f32)
    gnw = np.ascontiguousarray(np.asarray(inputs['L0_attn_b_g_norm_weight']).reshape(1, 128), dtype=f32)

    in_maps = []
    for core in range(NCORES):
        b, qi = core // 4, core % 4
        q0 = qi * TOK
        own0 = qi * NT
        gate = np.array([1.0 if own0 <= j < own0 + NT else 0.0
                         for j in range(32)], dtype=f32)
        widx = np.clip(np.arange(MS_LEN) - 512 - q0, 0, 1022)
        rel_ms = np.ascontiguousarray(rel[widx])
        m = {
            'xin': np.ascontiguousarray(x[b, q0:q0 + TOK, :]),
            'rel_ms': rel_ms,
            'Ws2T': w['Ws2T'], 'Ws1T': w['Ws1T'],
            'bs2': bs2, 'gnw': gnw,
            'wupT_s': np.ascontiguousarray(wupT[core * 256:(core + 1) * 256, :]),
            'wgateT_s': np.ascontiguousarray(wgateT[core * 256:(core + 1) * 256, :]),
            'wdownT_s': np.ascontiguousarray(wdownT[core * 1024:(core + 1) * 1024, :]),
            'meta_f': np.array([[q0, own0, 0, 0, 0, 0, 0, 0]], dtype=f32),
            'gate_own': gate.reshape(1, 32),
        }
        for k in ("wqT_a", "wkT_a", "wvT_a", "woT_a", "WqT_b", "WkT_b",
                  "WvT_b", "WgT_b", "WoT_b"):
            m[k] = w[k]
        in_maps.append(m)
    return in_maps


_NC_CACHE = {}


def kernel(**inputs):
    key = (STOP_AFTER, tuple(DEBUG_TAPS))
    if key not in _NC_CACHE:
        _NC_CACHE[key] = build_program()
    nc = _NC_CACHE[key]
    in_maps = _prep_inputs(inputs)
    res = run_bass_kernel_spmd(nc, in_maps, core_ids=list(range(NCORES)))
    out = np.empty((BATCH, SEQ, D), dtype=np.float32)
    for core in range(NCORES):
        b, qi = core // 4, core % 4
        out[b, qi * TOK:(qi + 1) * TOK, :] = res.results[core]["xout"]
    kernel.last_results = res.results
    return out


if __name__ == "__main__":
    build_program()
    print("build ok")
